# version 01
# baseline (speedup 1.0000x reference)
import contextlib
import math

import numpy as np
import concourse.bass as bass
import concourse.mybir as mybir
from concourse.bass_utils import run_bass_kernel_spmd

F32 = mybir.dt.float32
BF16 = mybir.dt.bfloat16
I32 = mybir.dt.int32
AF = mybir.ActivationFunctionType
OP = mybir.AluOpType
AX = mybir.AxisListType

NCORES = 8
T = 4096
D = 1024
NT = T // 128
ND = D // 128
EPS_LN = 1e-5

ENGS = ("tensor", "vector", "scalar", "gpsimd", "sync")

CONST_VALUES = [1e-5, 1e-10, float(np.float32(np.finfo(np.float32).eps)), 1e-9,
                1e-6, 0.0, 1.0, 0.5, -1.0, 2.0, 0.125,
                float(np.float32(1.0 / np.sqrt(np.float32(128.0)))),
                float(np.float32(1.0 / np.sqrt(np.float32(1024.0)))),
                float(np.float32(1.0 / np.sqrt(np.float32(64.0)))),
                float(np.float32(math.sqrt(2.0 / math.pi))), 0.0078125]
EPS32 = float(np.float32(np.finfo(np.float32).eps))


class Bld:
    def __init__(self, nc):
        self.nc = nc
        self.q = {e: [] for e in ENGS}
        self.sems = {}
        self.cnt = {}
        self.stack = contextlib.ExitStack()
        self.names = set()
        self.dram_cache = {}
        self.prep_toks = {}
        self.inputs = {}

    def sem(self, name):
        if name not in self.sems:
            self.sems[name] = self.stack.enter_context(self.nc.semaphore(name))
            self.cnt[name] = 0
        return self.sems[name]

    def now(self, name):
        self.sem(name)
        return self.cnt[name]

    def emit(self, eng, fn, incs=()):
        hs = [(self.sem(n), a) for (n, a) in incs]
        for (n, a) in incs:
            self.cnt[n] += a
        toks = {n: self.cnt[n] for (n, a) in incs}

        def run(e, fn=fn, hs=hs):
            inst = fn(e)
            for h, a in hs:
                inst = inst.then_inc(h, a)
            return inst

        self.q[eng].append(run)
        return toks

    def drain(self, eng):
        self.q[eng].append(lambda e: e.drain())

    def wait(self, eng, name, val):
        if val is None or val <= 0:
            return
        assert val <= self.cnt.get(name, 0), ("wait>emitted", eng, name, val,
                                              self.cnt.get(name))
        h = self.sem(name)
        self.q[eng].append(lambda e, h=h, v=val: e.wait_ge(h, v))

    def sb(self, name, shape, dtype):
        assert name not in self.names, name
        self.names.add(name)
        return self.stack.enter_context(self.nc.sbuf_tensor(name, shape, dtype))

    def shared(self, name, shape, dtype):
        if not hasattr(self, "_shared"):
            self._shared = {}
        if name not in self._shared:
            self._shared[name] = self.sb(name, shape, dtype)
        return self._shared[name]

    def ps(self, name, shape, dtype=F32):
        assert name not in self.names, name
        self.names.add(name)
        return self.stack.enter_context(self.nc.psum_tensor(name, shape, dtype))

    def dram(self, name, shape, dtype):
        if name not in self.dram_cache:
            kind = "ExternalOutput" if any(name.endswith(s) for s in DBG)                 else "Internal"
            self.dram_cache[name] = self.nc.dram_tensor(
                name, list(shape), dtype, kind=kind).ap()
        return self.dram_cache[name]

    def inp(self, name, shape, dtype=F32):
        if name not in self.dram_cache:
            self.dram_cache[name] = self.nc.dram_tensor(
                name, list(shape), dtype, kind="ExternalInput").ap()
            self.inputs[name] = (tuple(shape), dtype)
        return self.dram_cache[name]

    def outp(self, name, shape, dtype=F32):
        if name not in self.dram_cache:
            self.dram_cache[name] = self.nc.dram_tensor(
                name, list(shape), dtype, kind="ExternalOutput").ap()
        return self.dram_cache[name]

    def handoff(self, qeng, dsem, esem):
        h = self.sem(dsem)
        v = self.cnt[dsem]
        if v > 0:
            self.q[qeng].append(lambda e, h=h, v=v: e.wait_ge(h, v))
        he = self.sem(esem)
        self.cnt[esem] += 1
        tok = self.cnt[esem]
        self.q[qeng].append(lambda e, he=he: e.sem_inc(he, 1))
        return tok

    def dma(self, qeng, out, in_, sem=None, **kw):
        sem = sem or ("d_" + qeng)
        return self.emit(qeng, lambda e: e.dma_start(out=out, in_=in_, **kw),
                         incs=[(sem, 16)])[sem]

    def finalize(self, final_waits):
        nc = self.nc
        for eng in ENGS:
            for name, val in final_waits.items():
                if val and val > 0:
                    h = self.sem(name)
                    self.q[eng].append(lambda e, h=h, v=val: e.wait_ge(h, v))
        with nc.Block() as blk:
            for eng in ENGS:
                ops = self.q[eng]

                def body(e, ops=ops):
                    for f in ops:
                        f(e)
                getattr(blk, eng)(body)
        self.stack.close()


class Ctx:
    def __init__(self, b):
        self.b = b
        self.identity = b.sb("c_ident", [128, 128], BF16)
        self.ones_col = b.sb("c_onescol", [128, 1], BF16)
        self.ones_row = b.sb("c_onesrow", [1, 128], BF16)
        b.dma("sync", self.identity[:, :], b.inp("c_identity", [128, 128], BF16)[:, :],
              sem="d_cs")
        b.dma("sync", self.ones_col[:, :], b.inp("c_ones_col", [128, 1], BF16)[:, :],
              sem="d_cs")
        self.ones_t = b.sb("c_onest", [128, 128], F32)
        b.dma("sync", self.ones_t[:, :], b.inp("c_ones_t", [128, 128], F32)[:, :],
              sem="d_cs")
        b.dma("sync", self.ones_row[:, :],
              b.inp("c_ones_row", [1, 128], BF16)[:, :], sem="d_cs")
        self.sqscr = b.sb("c_sqscr", [128, D], F32)
        self.cf = b.sb("c_cf", [128, len(CONST_VALUES)], F32)
        b.dma("sync", self.cf[:, :],
              b.inp("c_f32", [128, len(CONST_VALUES)])[:, :], sem="d_cs")
        self.const_tok = b.handoff("sync", "d_cs", "HCs")
        self.psA = b.ps("c_psA", [128, 4 * 512], F32)
        self.psB = b.ps("c_psB", [128, 2 * 512], F32)
        self.psT = b.ps("c_psT", [128, 2 * 1024], BF16)
        self.ps_bc = self.psB
        self.bc_free = 0
        b.wait("scalar", "HCs", self.const_tok)
        for fn in (AF.Exp, AF.Gelu, AF.Silu, AF.Sigmoid, AF.Tanh, AF.Sqrt,
                   AF.Square, AF.Copy):
            b.emit("scalar", lambda e, fn=fn: e.activation(
                out=self.sqscr[0:1, 0:8], in_=self.cf[0:1, 0:8], func=fn))
            b.drain("scalar")
        b.emit("scalar", lambda e: e.activation(
            out=self.sqscr[0:1, 0:8], in_=self.cf[0:1, 0:8], func=AF.Copy),
            incs=[("S", 1)])

    def wait_consts(self, eng):
        self.b.wait(eng, "HCs", self.const_tok)

    def const(self, v):
        i = CONST_VALUES.index(v)
        return self.cf[:, i:i + 1]

    def lazy(self, name, shape, dtype=BF16):
        if not hasattr(self, "_lazy"):
            self._lazy = {}
        if name not in self._lazy:
            buf = self.b.sb("cl_" + name, list(shape), dtype)
            self.b.dma("sync", buf[:, :], self.b.inp(name, shape, dtype)[:, :],
                       sem="d_cs")
            tok = self.b.handoff("sync", "d_cs", "HCs")
            self._lazy[name] = (buf, ("HCs", tok))
        return self._lazy[name]


DBG = []
NO_ROPE = False


def dbg_dump(b, name, ap, shape, dtype=BF16):
    if name not in DBG:
        return
    dst = b.outp("dbg_" + name, list(shape), dtype)
    for s in ("V", "S", "T"):
        b.wait("sync", s, b.now(s))
    tok = b.dma("sync", dst[:, :], ap, sem="d_out")
    return tok


def bf16_mirror(b, name, host_shape):
    key = "bf_" + name
    if key in b.dram_cache and key in b.prep_toks:
        return b.dram_cache[key], b.prep_toks[key]
    src = b.inp(name, host_shape).flatten()
    dst = b.dram(key, list(host_shape), BF16)
    dstf = dst.flatten()
    n = int(np.prod(host_shape))
    CH = 8 * 1024 * 1024
    tok = None
    for o in range(0, n, CH):
        e = min(n, o + CH)
        tok = b.dma("gpsimd", dstf[o:e], src[o:e], sem="d_m")
    b.prep_toks[key] = tok
    return dst, tok


def bcast_row(b, ctx, out_sb, row_ap, n, row_wait=None):
    tokv = None
    for o in range(0, n, 512):
        e = min(n, o + 512)
        ctx.wait_consts("tensor")
        if row_wait is not None:
            b.wait("tensor", row_wait[0], row_wait[1])
            row_wait = None
        b.wait("tensor", "V", b.now("V"))
        b.wait("tensor", "S", b.now("S"))
        tk = b.emit("tensor", lambda en, o=o, e=e: en.matmul(
            ctx.ps_bc[:, 0:e - o], ctx.ones_row[:, :], row_ap[:, o:e],
            start=True, stop=True), incs=[("T", 1)])
        b.wait("vector", "T", tk["T"])
        tokv = b.emit("vector", lambda en, o=o, e=e: en.tensor_copy(
            out=out_sb[:, o:e], in_=ctx.ps_bc[:, 0:e - o]), incs=[("V", 1)])["V"]
        ctx.bc_free = tokv
    return tokv


def prologue(b, ctx, tag, x_dram, t0, t1, xnT, ncols, xnT_col0, ps_tr,
             xn_tok=None, gb_tiles=None, pre_tok=None, eps=EPS_LN,
             stats_sink=None, post_rms=None):
    ntile = (t1 - t0) // 128
    st = b.shared("pro_st", [128, 16], F32)
    s2s = b.shared("pro_s2", [128, 32], F32)
    xa = b.shared("scr8kA", [128, 2 * D], F32)
    xnl = None
    if xn_tok is None:
        xnl = b.shared("scr8kB", [128, 2 * D], F32)[:, :].bitcast(BF16)[:, 0:2 * D]
    b.wait("sync", "V", b.now("V"))
    b.wait("sync", "S", b.now("S"))
    b.wait("sync", "T", b.now("T"))
    b.wait("sync", "so0", b.now("so0"))
    b.wait("sync", "so1", b.now("so1"))
    ctx.wait_consts("scalar")
    z_tok = b.emit("vector", lambda e: e.memset(s2s[:, :], 0.0),
                   incs=[("V", 1)])["V"]
    b.wait("scalar", "V", z_tok)
    sq_tok = {}
    ap_tok = {}
    tr_ev = {}
    last_v = 0
    for i in range(ntile):
        buf = i % 2
        xt = xa[:, buf * D:(buf + 1) * D]
        if i >= 2:
            b.wait("sync", "V", ap_tok[i - 2])
            b.wait("sync", "S", sq_tok[i - 2])
        ld = b.dma("sync", xt, x_dram[t0 + i * 128:t0 + (i + 1) * 128, :],
                   sem=f"x{i % 2}")
        s2 = s2s[:, i:i + 1]
        s1 = st[:, buf * 8 + 1:buf * 8 + 2]
        mean = st[:, buf * 8 + 2:buf * 8 + 3]
        var = st[:, buf * 8 + 3:buf * 8 + 4]
        sd = st[:, buf * 8 + 4:buf * 8 + 5]
        rr = st[:, buf * 8 + 5:buf * 8 + 6]
        nmr = st[:, buf * 8 + 6:buf * 8 + 7]
        b.wait("scalar", f"x{i % 2}", ld)
        sq_tok[i] = b.emit("scalar", lambda e, xt=xt, s2=s2: e.activation(
            out=ctx.sqscr[:, :], in_=xt, func=AF.Square, accum_out=s2),
            incs=[("S", 1)])["S"]
        b.wait("vector", f"x{i % 2}", ld)
        b.emit("vector", lambda e, xt=xt, s1=s1: e.tensor_reduce(
            s1, xt, AX.X, OP.add))
        b.drain("vector")
        b.emit("vector", lambda e, s1=s1, mean=mean: e.tensor_scalar(
            out=mean, in0=s1, scalar1=1.0 / D, scalar2=None, op0=OP.mult))
        b.wait("vector", "S", sq_tok[i])
        b.drain("vector")
        b.emit("vector", lambda e, mean=mean, var=var: e.scalar_tensor_tensor(
            out=var, in0=mean, scalar=-1.0, in1=mean, op0=OP.mult, op1=OP.mult))
        b.drain("vector")
        tv = b.emit("vector", lambda e, s2=s2, var=var: e.scalar_tensor_tensor(
            out=var, in0=s2, scalar=1.0 / D, in1=var, op0=OP.mult, op1=OP.add),
            incs=[("V", 1)])["V"]
        b.wait("scalar", "V", tv)
        ts_ = b.emit("scalar", lambda e, var=var, sd=sd: e.activation(
            out=sd, in_=var, func=AF.Sqrt, bias=ctx.const(eps)),
            incs=[("S", 1)])["S"]
        b.wait("vector", "S", ts_)
        b.emit("vector", lambda e, sd=sd, rr=rr: e.reciprocal(rr, sd))
        b.drain("vector")
        if stats_sink is not None:
            rrd, nmrd = stats_sink
            b.emit("vector", lambda e, rr=rr, i=i, rrd=rrd: e.tensor_copy(
                out=rrd[:, i:i + 1], in_=rr))
            b.drain("vector")
            b.emit("vector", lambda e, mean=mean, i=i, rrd=rrd, nmrd=nmrd:
                   e.scalar_tensor_tensor(out=nmrd[:, i:i + 1], in0=mean,
                                          scalar=-1.0, in1=rrd[:, i:i + 1],
                                          op0=OP.mult, op1=OP.mult))
            b.drain("vector")
        if post_rms is not None:
            b.emit("vector", lambda e, sd=sd, var=var, rr=rr: e.tensor_tensor(
                out=sd, in0=var, in1=rr, op=OP.mult))
            b.drain("vector")
            tv2 = b.emit("vector", lambda e, sd=sd, rr=rr: e.tensor_tensor(
                out=sd, in0=sd, in1=rr, op=OP.mult), incs=[("V", 1)])["V"]
            b.wait("scalar", "V", tv2)
            ts2 = b.emit("scalar", lambda e, sd=sd, s1=s1: e.activation(
                out=s1, in_=sd, func=AF.Sqrt, bias=ctx.const(post_rms)),
                incs=[("S", 1)])["S"]
            b.wait("vector", "S", ts2)
            b.emit("vector", lambda e, sd=sd, s1=s1: e.reciprocal(sd, s1))
            b.drain("vector")
            b.emit("vector", lambda e, rr=rr, sd=sd: e.tensor_tensor(
                out=rr, in0=rr, in1=sd, op=OP.mult))
            b.drain("vector")
        b.emit("vector", lambda e, mean=mean, rr=rr, nmr=nmr: e.scalar_tensor_tensor(
            out=nmr, in0=mean, scalar=-1.0, in1=rr, op0=OP.mult, op1=OP.mult))
        b.drain("vector")
        if xn_tok is not None:
            xn_t = xn_tok[:, i * D:(i + 1) * D]
        else:
            xn_t = xnl[:, buf * D:(buf + 1) * D]
            if i >= 2:
                b.wait("vector", "T", tr_T[i - 2])
        if gb_tiles is None:
            ap_tok[i] = b.emit("vector", lambda e, xt=xt, xn_t=xn_t, rr=rr, nmr=nmr:
                               e.tensor_scalar(out=xn_t, in0=xt, scalar1=rr,
                                               scalar2=nmr, op0=OP.mult, op1=OP.add),
                               incs=[("V", 1)])["V"]
        else:
            gt, bt = gb_tiles
            b.emit("vector", lambda e, xt=xt, xn_t=xn_t, rr=rr, nmr=nmr:
                   e.tensor_scalar(out=xn_t, in0=xt, scalar1=rr,
                                   scalar2=nmr, op0=OP.mult, op1=OP.add))
            b.drain("vector")
            b.emit("vector", lambda e, xn_t=xn_t, gt=gt: e.tensor_tensor(
                out=xn_t, in0=xn_t, in1=gt[:, :], op=OP.mult))
            b.drain("vector")
            ap_tok[i] = b.emit("vector", lambda e, xn_t=xn_t, bt=bt: e.tensor_tensor(
                out=xn_t, in0=xn_t, in1=bt[:, :], op=OP.add), incs=[("V", 1)])["V"]
        pb = ps_tr[:, (i % 2) * 1024:(i % 2) * 1024 + 1024]
        if i >= 2:
            b.wait("tensor", "V", tr_ev[i - 2])
        if i == 0:
            ctx.wait_consts("tensor")
        b.wait("tensor", "V", ap_tok[i])
        ttk = None
        for c in range(ND):
            ttk = b.emit("tensor", lambda e, xn_t=xn_t, c=c, pb=pb: e.transpose(
                out=pb[:, c * 128:(c + 1) * 128], in_=xn_t[:, c * 128:(c + 1) * 128],
                identity=ctx.identity[:, :]),
                incs=[("T", 1)] if c == ND - 1 else ())
        if i == 0:
            tr_T = {}
        tr_T[i] = ttk["T"]
        b.wait("vector", "T", ttk["T"])
        if i == 0 and pre_tok is not None:
            b.wait("vector", pre_tok[0], pre_tok[1])
        dst = xnT[:, :].rearrange("p (c n) -> p c n", c=ND)[
            :, :, xnT_col0 + i * 128: xnT_col0 + (i + 1) * 128]
        src = pb.rearrange("p (c n) -> p c n", c=ND)
        tr_ev[i] = b.emit("vector", lambda e, dst=dst, src=src: e.tensor_copy(
            out=dst, in_=src), incs=[("V", 1)])["V"]
        last_v = tr_ev[i]
    return last_v


def mm_fm(b, ctx, Wbf, nft, ndc, X, xcols, evac, ps4, wb2, extra_wait=None):
    nts = (xcols + 511) // 512
    nbank = 2
    bank_free = [None] * nbank
    mm_last = {}
    guard = b.now("T")
    b.wait("gpsimd", "d_m", b.now("d_m"))
    for ft in range(nft):
        wb = wb2[:, (ft % 2) * ndc * 128:((ft % 2) + 1) * ndc * 128]
        b.wait("gpsimd", "T", mm_last[ft - 2] if ft >= 2 else guard)
        wtok = b.dma("gpsimd", wb, Wbf[ft], sem=f"wt{ft % 2}")
        for ts in range(nts):
            c0 = ts * 512
            c1 = min(xcols, c0 + 512)
            bk = (ft * nts + ts) % nbank
            pa = ps4[:, bk * 512:bk * 512 + (c1 - c0)]
            if ts == 0:
                b.wait("tensor", f"wt{ft % 2}", wtok)
            if extra_wait:
                for s_, v_ in extra_wait:
                    b.wait("tensor", s_, v_)
                extra_wait = None
            if bank_free[bk] is not None:
                b.wait("tensor", bank_free[bk][0], bank_free[bk][1])
            tk = None
            for dc in range(ndc):
                tk = b.emit("tensor", lambda e, pa=pa, wb=wb, dc=dc, c0=c0, c1=c1:
                            e.matmul(pa, wb[:, dc * 128:(dc + 1) * 128],
                                     X(dc)[:, c0:c1], start=(dc == 0),
                                     stop=(dc == ndc - 1)),
                            incs=[("T", 1)] if dc == ndc - 1 else ())
            mm_last[ft] = tk["T"]
            bank_free[bk] = evac(ft, ts, pa, c1 - c0, tk["T"])


def emit_l2_ffn(b, ctx, tag, x_in, x_out):
    TQ = 1024
    NQ = T // TQ
    w1bf, _ = bf16_mirror(b, "L2f_w1st", [32, 128, ND * 128])
    w2bf, _ = bf16_mirror(b, "L2f_w2st", [32, 128, ND * 128])
    h1bf, _ = bf16_mirror(b, "L2f_h1st", [32, 128, 32 * 128])
    h2bf, _ = bf16_mirror(b, "L2f_h2st", [32, 128, 32 * 128])
    w3bf, w3tok = bf16_mirror(b, "L2f_w3t", [32, 128, D])

    sc = b.sb(f"{tag}_sc", [128, 4 * 32], F32)
    b.dma("sync", sc[:, 0:32], b.inp("L2f_s1c", [128, 32])[:, :], sem="d_cs")
    b.dma("sync", sc[:, 32:64], b.inp("L2f_s2c", [128, 32])[:, :], sem="d_cs")
    b.dma("sync", sc[:, 64:96], b.inp("L2f_h1c", [128, 32])[:, :], sem="d_cs")
    b.dma("sync", sc[:, 96:128], b.inp("L2f_h2c", [128, 32])[:, :], sem="d_cs")
    sct = b.handoff("sync", "d_cs", "HCs")
    s3row = b.sb(f"{tag}_s3r", [1, D], BF16)
    b.dma("gpsimd", s3row[:, :], b.inp("L2f_s3", [1, D])[:, :], sem="d_cg")
    s3tok = b.handoff("gpsimd", "d_cg", "HCg")
    s3t = b.sb(f"{tag}_s3t", [128, D], BF16)
    bcast_row(b, ctx, s3t, s3row, D, row_wait=("HCg", s3tok))
    s3rdy = b.now("V")

    xnT = b.sb(f"{tag}_xnT", [128, ND * TQ], BF16)
    H1 = b.sb(f"{tag}_H1", [128, 32 * TQ], BF16)
    H2 = b.sb(f"{tag}_H2", [128, 32 * TQ], BF16)
    wb2a = b.sb(f"{tag}_wba", [128, 2 * ND * 128], BF16)
    wb2b = b.sb(f"{tag}_wbb", [128, 2 * 32 * 128], BF16)
    xold = b.sb(f"{tag}_xold", [128, 2 * D], F32)
    xnew = b.sb(f"{tag}_xnew", [128, 2 * D], F32)
    ps_tr = ctx.psT
    ps4 = ctx.psA[:, 0:1024]
    ps2 = ctx.psB
    out_tok = None

    for q in range(NQ):
        t0 = q * TQ
        ptok = prologue(b, ctx, f"{tag}q{q}", x_in, t0, t0 + TQ, xnT, TQ, 0,
                        ps_tr, pre_tok=("T", b.now("T")))
        Xfn = lambda dc: xnT[:, dc * TQ:(dc + 1) * TQ]

        def evac_a2(ft, ts, pa, nc_, t_tok):
            dst = H2[:, ft * TQ + ts * 512: ft * TQ + ts * 512 + nc_]
            b.wait("scalar", "T", t_tok)
            b.wait("scalar", "HCs", sct)
            tk = b.emit("scalar", lambda e, pa=pa, dst=dst, ft=ft: e.activation(
                out=dst, in_=pa, func=AF.Silu, scale=sc[:, ft:ft + 1]),
                incs=[("S", 1)])
            return ("S", tk["S"])

        def evac_b2(ft, ts, pa, nc_, t_tok):
            asrc = H2[:, ft * TQ + ts * 512: ft * TQ + ts * 512 + nc_]
            dst = H1[:, ft * TQ + ts * 512: ft * TQ + ts * 512 + nc_]
            b.wait("vector", "T", t_tok)
            b.wait("vector", "S", b.now("S"))
            tk = b.emit("vector", lambda e, pa=pa, asrc=asrc, dst=dst, ft=ft:
                        e.scalar_tensor_tensor(out=dst, in0=pa,
                                               scalar=sc[:, 32 + ft:33 + ft],
                                               in1=asrc, op0=OP.mult, op1=OP.mult),
                        incs=[("V", 1)])
            return ("V", tk["V"])

        if q == 0:
            dbg_dump(b, "xnT", xnT[:, :], [128, ND * TQ])
        mm_fm(b, ctx, w1bf, 32, ND, Xfn, TQ, evac_a2, ps4, wb2a,
              extra_wait=[("V", ptok)])
        mm_fm(b, ctx, w2bf, 32, ND, Xfn, TQ, evac_b2, ps4, wb2a,
              extra_wait=[("S", b.now("S"))])
        if q == 0:
            dbg_dump(b, "A", H2[:, :], [128, 32 * TQ])
            dbg_dump(b, "H1", H1[:, :], [128, 32 * TQ])

        H1fn = lambda dc: H1[:, dc * TQ:(dc + 1) * TQ]

        def evac_g1(ft, ts, pa, nc_, t_tok):
            dst = H2[:, ft * TQ + ts * 512: ft * TQ + ts * 512 + nc_]
            b.wait("scalar", "T", t_tok)
            tk = b.emit("scalar", lambda e, pa=pa, dst=dst, ft=ft: e.activation(
                out=dst, in_=pa, func=AF.Gelu, scale=sc[:, 64 + ft:65 + ft]),
                incs=[("S", 1)])
            return ("S", tk["S"])

        mm_fm(b, ctx, h1bf, 32, 32, H1fn, TQ, evac_g1, ps4, wb2b,
              extra_wait=[("V", b.now("V"))])
        if q == 0:
            dbg_dump(b, "H2", H2[:, :], [128, 32 * TQ])

        H2fn = lambda dc: H2[:, dc * TQ:(dc + 1) * TQ]

        def evac_g2(ft, ts, pa, nc_, t_tok):
            dst = H1[:, ft * TQ + ts * 512: ft * TQ + ts * 512 + nc_]
            b.wait("scalar", "T", t_tok)
            tk = b.emit("scalar", lambda e, pa=pa, dst=dst, ft=ft: e.activation(
                out=dst, in_=pa, func=AF.Gelu, scale=sc[:, 96 + ft:97 + ft]),
                incs=[("S", 1)])
            return ("S", tk["S"])

        mm_fm(b, ctx, h2bf, 32, 32, H2fn, TQ, evac_g2, ps4, wb2b,
              extra_wait=[("S", b.now("S"))])

        h3_done_S = b.now("S")
        b.wait("gpsimd", "T", b.now("T"))
        b.wait("gpsimd", "d_m", b.now("d_m"))
        w3ld = b.dma("gpsimd", H2[:, :].rearrange("p (g f) -> p g f", g=32),
                     w3bf.rearrange("g p f -> p g f"), sem="rw")
        out_v = {}
        psfree = [None, None]
        b.wait("sync", "V", b.now("V"))
        b.wait("sync", "S", b.now("S"))
        for tt in range(TQ // 128):
            tr0 = t0 + tt * 128
            buf = tt % 2
            b.so_tok = getattr(b, "so_tok", {})
            tgl = q * (TQ // 128) + tt
            if tt >= 2:
                b.wait("sync", "V", out_v[tt - 2])
            if tgl >= 2:
                b.wait("sync", f"so{tgl % 2}", b.so_tok[tgl - 2])
            xo = xold[:, buf * D:(buf + 1) * D]
            xl = b.dma("sync", xo, x_in[tr0:tr0 + 128, :], sem=f"xo{tt % 2}")
            tkv = None
            for half in range(2):
                bk = half
                pa = ps2[:, bk * 512:bk * 512 + 512]
                if tt == 0 and half == 0:
                    b.wait("tensor", "rw", w3ld)
                    b.wait("tensor", "S", h3_done_S)
                    b.wait("tensor", "V", b.now("V"))
                if psfree[bk] is not None:
                    b.wait("tensor", psfree[bk][0], psfree[bk][1])
                tkmm = None
                for gc in range(32):
                    tkmm = b.emit("tensor", lambda e, pa=pa, gc=gc, tt=tt, half=half:
                                  e.matmul(pa,
                                           H1[:, gc * TQ + tt * 128:
                                              gc * TQ + (tt + 1) * 128],
                                           H2[:, gc * 1024 + half * 512:
                                              gc * 1024 + half * 512 + 512],
                                           start=(gc == 0), stop=(gc == 31)),
                                  incs=[("T", 1)] if gc == 31 else ())
                b.wait("vector", "T", tkmm["T"])
                b.wait("vector", f"xo{tt % 2}", xl)
                if tt == 0 and half == 0:
                    b.wait("vector", "V", s3rdy)
                xn_sl = xnew[:, buf * D + half * 512: buf * D + half * 512 + 512]
                b.emit("vector", lambda e, pa=pa, half=half, xn_sl=xn_sl:
                       e.tensor_tensor(out=xn_sl, in0=pa,
                                       in1=s3t[:, half * 512:(half + 1) * 512],
                                       op=OP.mult))
                b.drain("vector")
                tkv = b.emit("vector", lambda e, xo=xo, half=half, xn_sl=xn_sl:
                             e.tensor_tensor(out=xn_sl, in0=xn_sl,
                                             in1=xo[:, half * 512:(half + 1) * 512],
                                             op=OP.add), incs=[("V", 1)])
                psfree[bk] = ("V", tkv["V"])
            out_v[tt] = tkv["V"]
            b.wait("sync", "V", tkv["V"])
            out_tok = b.dma("sync", x_out[tr0:tr0 + 128, :],
                            xnew[:, buf * D:(buf + 1) * D], sem=f"so{tt % 2}")
            b.so_tok[tgl] = out_tok
    return out_tok


def _st_tile(Wt, ndc, nft):
    d, f = Wt.shape
    assert d == ndc * 128 and f == nft * 128
    return np.ascontiguousarray(
        Wt.reshape(ndc, 128, nft, 128).transpose(2, 1, 0, 3).reshape(nft, 128,
                                                                     ndc * 128))


def _colmajor(s, nft):
    return np.ascontiguousarray(s.reshape(nft, 128).T)


def prep_consts(H):
    import ml_dtypes
    bf = ml_dtypes.bfloat16
    H["c_identity"] = np.eye(128, dtype=np.float32).astype(bf)
    H["c_ones_col"] = np.ones((128, 1), np.float32).astype(bf)
    H["c_ones_row"] = np.ones((1, 128), np.float32).astype(bf)
    H["c_ones_t"] = np.ones((128, 128), np.float32)
    H["c_f32"] = np.tile(np.asarray(CONST_VALUES, np.float32)[None, :], (128, 1))


def prep_l2_ffn(inp, H):
    if "L2f_w1st" in H:
        return
    w1t = inp["L2_ffn_w1_w"].T.astype(np.float32)
    w2t = inp["L2_ffn_w2_w"].T.astype(np.float32)
    h1t = inp["L2_ffn_h1_w"].T.astype(np.float32)
    h2t = inp["L2_ffn_h2_w"].T.astype(np.float32)
    w3t = inp["L2_ffn_w3_w"].T.astype(np.float32)
    H["L2f_w1st"] = _st_tile(w1t, ND, 32)
    H["L2f_w2st"] = _st_tile(w2t, ND, 32)
    H["L2f_h1st"] = _st_tile(h1t, 32, 32)
    H["L2f_h2st"] = _st_tile(h2t, 32, 32)
    H["L2f_w3t"] = np.ascontiguousarray(w3t.reshape(32, 128, D))
    H["L2f_s1c"] = _colmajor(inp["L2_ffn_w1_s"].astype(np.float32), 32)
    H["L2f_s2c"] = _colmajor(inp["L2_ffn_w2_s"].astype(np.float32), 32)
    H["L2f_h1c"] = _colmajor(inp["L2_ffn_h1_s"].astype(np.float32), 32)
    H["L2f_h2c"] = _colmajor(inp["L2_ffn_h2_s"].astype(np.float32), 32)
    H["L2f_s3"] = np.ascontiguousarray(
        inp["L2_ffn_w3_s"].astype(np.float32).reshape(1, D))


SUBS = {
    10: (emit_l2_ffn, prep_l2_ffn),
    12: (emit_l2_ffn, prep_l2_ffn),
}


def build_program(sub_ids):
    nc = bass.Bass("TRN2", target_bir_lowering=False)
    b = Bld(nc)
    ctx = Ctx(b)
    cur = b.inp("x_io", [T, D])
    for j, sid in enumerate(sub_ids):
        last = (j == len(sub_ids) - 1)
        nxt = b.outp("x_out", [T, D]) if last else b.dram(f"x_mid{j}", [T, D], F32)
        emit, _ = SUBS[sid]
        out_tok = emit(b, ctx, f"s{sid}_{j}", cur, nxt)
        cur = nxt
    b.finalize({"so0": b.now("so0"), "so1": b.now("so1"),
                "d_out": b.now("d_out")})
    return b


def run_group(sub_ids, x_full, H):
    b = build_program(sub_ids)
    in_maps = []
    for c in range(NCORES):
        m = {}
        for name, (shape, dtype) in b.inputs.items():
            if name == "x_io":
                m[name] = np.ascontiguousarray(x_full[c])
            else:
                a = H[name]
                m[name] = a
        in_maps.append(m)
    res = run_bass_kernel_spmd(b.nc, in_maps, core_ids=list(range(NCORES)))
    return np.stack([np.asarray(res.results[c]["x_out"], np.float32)
                     for c in range(NCORES)])


def host_prep(inputs, sub_ids):
    H = {}
    prep_consts(H)
    for sid in sub_ids:
        SUBS[sid][1](inputs, H)
    return H


ALL_SUBS = list(range(1, 17))


def kernel(**inputs):
    x = np.asarray(inputs["L0_attn_a_x"], np.float32)
    sub_ids = [s for s in ALL_SUBS if s in SUBS]
    H = host_prep(inputs, sub_ids)
    for sid in sub_ids:
        if sid in PHASED:
            x = PHASED[sid](x, H)
        else:
            x = run_group([sid], x, H)
    return x


def attn_stage_qk(b, ctx, tag, xnT, wst_name, nfq, qdram, kdram, qb_col=None,
                  kb_col=None, extra_wait=None, wb2=None):
    wbf, _ = bf16_mirror(b, wst_name, [2 * nfq, 128, ND * 128])
    stg = b.shared("aqk_stg", [128, 2 * 512], BF16)
    Xfn = lambda dc: xnT[:, dc * T:(dc + 1) * T]

    def evac(ft, ts, pa, nc_, t_tok):
        sl = stg[:, (ts % 2) * 512:(ts % 2) * 512 + nc_]
        bias = None
        if ft < nfq and qb_col is not None:
            bias = qb_col[:, ft:ft + 1]
        if ft >= nfq and kb_col is not None:
            bias = kb_col[:, ft - nfq:ft - nfq + 1]
        eng0 = "scalar" if bias is None else "vector"
        b.wait(eng0, "T", t_tok)
        g = b.dma_guard.get(("aqk", ts % 2))
        if g is not None:
            b.wait(eng0, g[0], g[1])
        if bias is None:
            tk = b.emit("scalar", lambda e, pa=pa, sl=sl: e.activation(
                out=sl, in_=pa, func=AF.Copy), incs=[("S", 1)])
            b.wait("sync", "S", tk["S"])
            tk = {"S": ("S", tk["S"])}
        else:
            b.wait("vector", "T", t_tok)
            tk = b.emit("vector", lambda e, pa=pa, sl=sl, bias=bias:
                        e.tensor_scalar(out=sl, in0=pa, scalar1=bias,
                                        scalar2=None, op0=OP.add),
                        incs=[("V", 1)])
            b.wait("sync", "V", tk["V"])
            tk = {"S": ("V", tk["V"])}
        dst = (qdram[ft] if ft < nfq else kdram[ft - nfq])
        dtok = b.dma("sync", dst[:, ts * 512:ts * 512 + nc_], sl,
                     sem=f"sq{ts % 2}")
        b.dma_guard[("aqk", ts % 2)] = (f"sq{ts % 2}", dtok)
        return tk["S"]

    b.dma_guard = getattr(b, "dma_guard", {})
    mm_fm(b, ctx, wbf, 2 * nfq, ND, Xfn, T, evac, ctx.psA[:, 0:1024],
          wb2, extra_wait=extra_wait)
    return None


def attn_stage_v(b, ctx, tag, xnT, wv_name, vdram, nheads, vb_tile=None,
                 xn_rdy=None, lam_neg=None):
    inner = nheads * 64
    wvbf, wvtok = bf16_mirror(b, wv_name, [ND, 128, inner])
    wv = b.shared("wshare", [128, ND * 1024], BF16)[:, 0:ND * inner]
    b.wait("gpsimd", "T", b.now("T"))
    b.wait("gpsimd", "d_m", b.now("d_m"))
    wvld = b.dma("gpsimd", wv[:, :].rearrange("p (c f) -> p c f", c=ND),
                 wvbf.rearrange("c p f -> p c f"), sem="rw")
    vstg = b.shared("av_stg", [128, 2 * 1040], BF16)
    b.wait("vector", "d_x", b.now("d_x"))
    ones_tok = b.emit("vector", lambda e: e.memset(
        vstg[:, :].rearrange("p (u h c) -> p u h c", u=2, c=65)[:, :, :, 64:65],
        1.0), incs=[("V", 1)])["V"]
    out_tok = None
    vdma_tok = {}
    for tt in range(NT):
        buf = tt % 2
        stg = vstg[:, buf * 1040:buf * 1040 + nheads * 65]
        for half in range(max(1, inner // 512)):
            c0 = half * 512
            c1 = min(inner, c0 + 512)
            bk = half % 2
            pa = ctx.psB[:, bk * 512:bk * 512 + (c1 - c0)]
            if tt == 0 and half == 0:
                b.wait("tensor", "rw", wvld)
                if xn_rdy is not None:
                    b.wait("tensor", "V", xn_rdy)
                b.wait("tensor", "V", b.now("V"))
                b.wait("tensor", "S", b.now("S"))
            if tt > 0:
                b.wait("tensor", "V", b.vstg_free.get(bk, 0))
            if tt >= 2 and half == 0:
                b.wait("vector", f"st{tt % 2}", vdma_tok[tt - 2])
            tk = None
            for dc in range(ND):
                tk = b.emit("tensor", lambda e, pa=pa, dc=dc, tt=tt, c0=c0, c1=c1:
                            e.matmul(pa, xnT[:, dc * T + tt * 128:
                                             dc * T + (tt + 1) * 128],
                                     wv[:, dc * inner + c0:dc * inner + c1],
                                     start=(dc == 0), stop=(dc == ND - 1)),
                            incs=[("T", 1)] if dc == ND - 1 else ())
            b.wait("vector", "T", tk["T"])
            if tt == 0 and half == 0:
                b.wait("vector", "V", ones_tok)
            h0 = c0 // 64
            nh = (c1 - c0) // 64
            dst = stg.rearrange("p (h c) -> p h c", c=65)[:, h0:h0 + nh, 0:64]
            if vb_tile is None:
                tv = b.emit("vector", lambda e, pa=pa, dst=dst, nh=nh, c1=c1, c0=c0:
                            e.tensor_copy(out=dst, in_=pa[:, 0:c1 - c0].rearrange(
                                "p (h c) -> p h c", c=64)), incs=[("V", 1)])
            else:
                tv = b.emit("vector", lambda e, pa=pa, dst=dst, c0=c0, c1=c1:
                            e.tensor_tensor(out=dst, in0=pa[:, 0:c1 - c0].rearrange(
                                "p (h c) -> p h c", c=64),
                                in1=vb_tile[:, c0:c1].rearrange(
                                "p (h c) -> p h c", c=64), op=OP.add),
                            incs=[("V", 1)])
            if not hasattr(b, "vstg_free"):
                b.vstg_free = {}
            b.vstg_free[bk] = tv["V"]
        b.wait("sync", "V", tv["V"])
        out_tok = b.dma("sync", vdram[tt * 128:(tt + 1) * 128, :], stg,
                        sem=f"st{tt % 2}")
        vdma_tok[tt] = out_tok
    return out_tok


def attn_core(b, ctx, tag, nheads, qdram, kdram, vdram, ybuf, scale_ap,
              mask=None, rope=None, stage_tok=None, norm_hook=None):
    nhp = nheads // 2
    qpr = b.shared("ac_q", [128, T], BF16)
    kpr = b.shared("ac_k", [128, T], BF16)
    qsw = b.shared("ac_qsw", [128, T], BF16) if rope is not None else None
    ksw = b.shared("ac_ksw", [128, T], BF16) if rope is not None else None
    vpr = b.shared("ac_v", [128, 2 * 32 * 130], BF16)
    pt = b.shared("ac_pt", [128, 2 * 1024], BF16)
    rsc = b.shared("ac_rsc", [128, 512], F32)
    rbt = b.shared("ac_rbt", [128, 512], BF16)
    ysc = b.shared("ac_ysc", [128, 512], BF16)
    pair_rdy = {}
    y_tok = 0
    ctx.wait_consts("scalar")
    ctx.wait_consts("tensor")
    b.wait("tensor", "V", b.now("V"))
    b.wait("tensor", "S", b.now("S"))
    tw = b.emit("tensor", lambda e: e.matmul(
        ctx.psA[:, 0:128], ctx.identity[:, :], ctx.identity[:, :],
        start=True, stop=True), incs=[("T", 1)])
    b.wait("scalar", "T", tw["T"])
    twe = b.emit("scalar", lambda e: e.activation(
        out=ysc[:, 0:128], in_=ctx.psA[:, 0:128], func=AF.Exp,
        scale=scale_ap), incs=[("S", 1)])
    b.wait("tensor", "S", twe["S"])
    tw2 = b.emit("tensor", lambda e: e.matmul(
        ctx.psB[0:65, 0:128], ctx.identity[0:128, 0:65], ysc[:, 0:128],
        start=True, stop=True), incs=[("T", 1)])
    b.wait("vector", "T", tw2["T"])
    twr = b.emit("vector", lambda e: e.reciprocal(rsc[64:65, 0:128],
                                                  ctx.psB[64:65, 0:128]),
                 incs=[("V", 1)])
    b.drain("vector")
    b.wait("tensor", "V", twr["V"])
    tw3 = b.emit("tensor", lambda e: e.matmul(
        ctx.psA[:, 0:128], ctx.ones_t[64:65, :], rsc[64:65, 0:128],
        start=True, stop=True), incs=[("T", 1)])
    b.wait("vector", "T", tw3["T"])
    b.emit("vector", lambda e: e.tensor_copy(out=rbt[:, 0:128],
                                             in_=ctx.psA[:, 0:128]),
           incs=[("V", 1)])
    b.drain("vector")
    b.emit("vector", lambda e: e.tensor_tensor(
        out=ysc[0:64, 0:128], in0=ctx.psB[0:64, 0:128], in1=rbt[0:64, 0:128],
        op=OP.mult), incs=[("V", 1)])
    warm_v = b.now("V")
    NQ = T // 512
    for hp in range(nhp):
        buf = hp % 2
        qp = qpr[:, :]
        kp = kpr[:, :]
        vp = vpr[:, buf * 32 * 130:(buf + 1) * 32 * 130]
        psem = f"pl{hp % 2}"
        if hp >= 1:
            b.wait("sync", "T", pair_rdy[hp - 1])
            b.wait("sync", "V", b.now("V"))
        if hp == 0:
            for s_ in ("sq0", "sq1", "st0", "st1"):
                b.wait("sync", s_, b.now(s_))
        if rope is None:
            lq = b.dma("sync", qp, qdram[hp][:, :], sem=psem)
            lk = b.dma("sync", kp, kdram[hp][:, :], sem=psem)
        else:
            lq = b.dma("sync", qp, qdram[hp][:, :], sem=psem)
            lk = b.dma("sync", kp, kdram[hp][:, :], sem=psem)
            for blk in range(4):
                sbase = (blk ^ 1) * 32
                lq = b.dma("sync", qsw[:, :][sbase:sbase + 32, :],
                           qdram[hp][blk * 32:(blk + 1) * 32, :], sem=psem)
                lk = b.dma("sync", ksw[:, :][sbase:sbase + 32, :],
                           kdram[hp][blk * 32:(blk + 1) * 32, :], sem=psem)
        lv = b.dma("sync", vp.rearrange("p (k c) -> p k c", c=130),
                   vdram[:, hp * 130:(hp + 1) * 130].rearrange(
                       "(k p) c -> p k c", p=128), sem=psem)
        if rope is not None:
            C, Sg = rope
            b.wait("vector", psem, lv)
            for src, sw in ((qp, qsw[:, :]), (kp, ksw[:, :])):
                b.emit("vector", lambda e, src=src, sw=sw, Sg=Sg: e.tensor_tensor(
                    out=sw, in0=sw, in1=Sg[:, :], op=OP.mult))
                b.emit("vector", lambda e, src=src, C=C: e.tensor_tensor(
                    out=src, in0=src, in1=C[:, :], op=OP.mult))
                b.drain("vector")
                b.emit("vector", lambda e, src=src, sw=sw: e.tensor_tensor(
                    out=src, in0=src, in1=sw, op=OP.add), incs=[("V", 1)])
            rope_tok = b.now("V")
            rope_prev = rope_tok
            if hp == 0 and "ropedq" in DBG:
                b.wait("sync", "V", rope_tok)
                b.dma("sync", b.outp("dbg_ropedq", [128, T], BF16)[:, :], qp,
                      sem="d_out")
                b.dma("sync", b.outp("dbg_ropedk", [128, T], BF16)[:, :], kp,
                      sem="d_out")
        for qb in range(NQ):
            if mask == "causal":
                kts = list(range(0, 4 * qb + 4))
            elif mask == "anticausal":
                kts = list(range(4 * qb, 32))
            else:
                kts = list(range(32))
            groups = [kts[i:i + 2] for i in range(0, len(kts), 2)]
            avb = {0: ctx.psB[:, 0:512], 1: ctx.psB[:, 512:1024]}
            first_av = {0: True, 1: True}
            ptok = {}
            for gi, g in enumerate(groups):
                for hh in range(2):
                    base = hh * 64
                    for j, kt in enumerate(g):
                        bk = hh * 2 + j
                        pa = ctx.psA[:, bk * 512:(bk + 1) * 512]
                        if hp == 0 and qb == 0 and gi == 0 and hh == 0 and j == 0:
                            b.wait("tensor", psem, lv)
                            b.wait("tensor", "V", b.now("V"))
                            b.wait("tensor", "S", b.now("S"))
                            if rope is not None:
                                b.wait("tensor", "V", rope_tok)
                        elif gi == 0 and hh == 0 and j == 0:
                            b.wait("tensor", psem, lv)
                            b.wait("tensor", "V", b.now("V"))
                            if rope is not None:
                                b.wait("tensor", "V", rope_tok)
                        b.wait("tensor", "S", ptok.get(bk, 0))
                        tkS = b.emit("tensor", lambda e, pa=pa, kp=kp, qp=qp,
                                     base=base, kt=kt, qb=qb: e.matmul(
                            pa, kp[base:base + 64, kt * 128:(kt + 1) * 128],
                            qp[base:base + 64, qb * 512:(qb + 1) * 512],
                            start=True, stop=True),
                            incs=[("T", 1)])
                    nck = len(g) * 512
                    pae = ctx.psA[:, hh * 1024:hh * 1024 + nck]
                    pslot = pt[:, hh * 1024:hh * 1024 + nck]
                    b.wait("scalar", "T", tkS["T"])
                    b.wait("scalar", "T", b.now("T"))
                    tke = b.emit("scalar", lambda e, pae=pae, pslot=pslot:
                                 e.activation(out=pslot, in_=pae, func=AF.Exp,
                                              scale=scale_ap), incs=[("S", 1)])
                    ptok[hh * 2] = tke["S"]
                    ptok[hh * 2 + 1] = tke["S"]
                    mtok = None
                    for j, kt in enumerate(g):
                        a = kt - 4 * qb
                        if mask is not None and 0 <= a <= 3:
                            mt = ctx.mask_c if mask == "causal" else ctx.mask_a
                            b.wait("vector", "S", tke["S"])
                            mtok = b.emit("vector", lambda e, pslot=pslot, j=j,
                                          mt=mt, a=a: e.tensor_tensor(
                                out=pslot[:, j * 512:(j + 1) * 512],
                                in0=pslot[:, j * 512:(j + 1) * 512],
                                in1=mt[:, :][:, a * 512:(a + 1) * 512], op=OP.mult),
                                incs=[("V", 1)])["V"]
                    h = hp * 2 + hh
                    b.wait("tensor", "S", tke["S"])
                    if mtok:
                        b.wait("tensor", "V", mtok)
                    for j, kt in enumerate(g):
                        last = (gi == len(groups) - 1 and j == len(g) - 1)
                        tkA = b.emit("tensor", lambda e, vp=vp, kt=kt, hh=hh,
                                     pslot=pslot, j=j, avb=avb, fa=first_av[hh],
                                     last=last: e.matmul(
                            avb[hh][0:65, :], vp[:, kt * 130 + hh * 65:
                                                 kt * 130 + hh * 65 + 65],
                            pslot[:, j * 512:(j + 1) * 512],
                            start=fa, stop=last),
                            incs=[("T", 1)] if last else ())
                        first_av[hh] = False
            av_tok = b.now("T")
            for hh in range(2):
                h = hp * 2 + hh
                pb = avb[hh]
                b.wait("vector", "T", b.now("T"))
                trc = b.emit("vector", lambda e, pb=pb: e.reciprocal(
                    rsc[64:65, 0:512], pb[64:65, 0:512]), incs=[("V", 1)])
                b.drain("vector")
                b.wait("tensor", "V", trc["V"])
                b.wait("tensor", "S", b.now("S"))
                tkb = b.emit("tensor", lambda e: e.matmul(
                    ctx.psA[:, 0:512], ctx.ones_t[64:65, :], rsc[64:65, 0:512],
                    start=True, stop=True), incs=[("T", 1)])
                b.wait("vector", "T", tkb["T"])
                b.emit("vector", lambda e: e.tensor_copy(out=rbt[:, :],
                                                         in_=ctx.psA[:, 0:512]))
                b.drain("vector")
                ic = h // 2
                if hh == 0:
                    dst = ybuf[0:64, ic * T + qb * 512: ic * T + (qb + 1) * 512]
                    tv = b.emit("vector", lambda e, pb=pb, dst=dst: e.tensor_tensor(
                        out=dst, in0=pb[0:64, :], in1=rbt[0:64, :], op=OP.mult),
                        incs=[("V", 1)])
                    y_tok = tv["V"]
                else:
                    b.wait("vector", "d_s", b.now("d_s"))
                    tv = b.emit("vector", lambda e, pb=pb: e.tensor_tensor(
                        out=ysc[0:64, :], in0=pb[0:64, :], in1=rbt[0:64, :],
                        op=OP.mult), incs=[("V", 1)])
                    b.wait("scalar", "V", tv["V"])
                    dst = ybuf[64:128, ic * T + qb * 512: ic * T + (qb + 1) * 512]
                    ydma = b.dma("scalar", dst, ysc[0:64, :], sem="d_s")
                    y_tok = tv["V"]
                if norm_hook is not None:
                    norm_hook(h, hh, pb, rbt, qb)
        pair_rdy[hp] = b.now("T")
    b.wait("vector", "d_s", b.now("d_s"))
    b.emit("vector", lambda e: e.nop(), incs=[("V", 1)])
    if "acy" in DBG:
        b.wait("sync", "V", b.now("V"))
        b.dma("sync", b.outp("dbg_acy", [128, nhp * T], BF16)[:, :],
              ybuf[:, 0:nhp * T], sem="d_out")
    return b.now("V")


def resid_proj(b, ctx, tag, ybuf, nic, wo_name, x_in, x_out, bias_tile=None,
               add_xn=None, y_rdy=None, post_hook=None, inner_scale=None):
    wobf, wotok = bf16_mirror(b, wo_name, [nic, 128, D])
    wo = b.shared("wshare", [128, ND * 1024], BF16)[:, 0:nic * 1024]
    b.wait("gpsimd", "T", b.now("T"))
    b.wait("gpsimd", "d_m", b.now("d_m"))
    wold = b.dma("gpsimd", wo[:, :].rearrange("p (c f) -> p c f", c=nic),
                 wobf.rearrange("c p f -> p c f"), sem="rw")
    xold = b.shared("scr8kA", [128, 2 * D], F32)
    xnew = b.shared("scr8kB", [128, 2 * D], F32)
    out_v = {}
    psfree = [None, None]
    out_tok = None
    so_tok = {}
    b.wait("sync", "V", b.now("V"))
    b.wait("sync", "S", b.now("S"))
    b.wait("sync", "T", b.now("T"))
    for tt in range(NT):
        buf = tt % 2
        if tt >= 2:
            b.wait("sync", "V", out_v[tt - 2])
            b.wait("sync", f"so{tt % 2}", so_tok[tt - 2])
        xo = xold[:, buf * D:(buf + 1) * D]
        xl = b.dma("sync", xo, x_in[tt * 128:(tt + 1) * 128, :],
                   sem=f"xo{tt % 2}")
        tkv = None
        for half in range(2):
            bk = half
            pa = ctx.psB[:, bk * 512:bk * 512 + 512]
            if tt == 0 and half == 0:
                b.wait("tensor", "rw", wold)
                if y_rdy is not None:
                    for s_, v_ in y_rdy:
                        b.wait("tensor", s_, v_)
                    b.wait("tensor", "d_s", b.now("d_s"))
                b.wait("tensor", "V", b.now("V"))
                b.wait("tensor", "S", b.now("S"))
            if psfree[bk] is not None:
                b.wait("tensor", psfree[bk][0], psfree[bk][1])
            tkmm = None
            for ic in range(nic):
                tkmm = b.emit("tensor", lambda e, pa=pa, ic=ic, tt=tt, half=half:
                              e.matmul(pa,
                                       ybuf[:, ic * T + tt * 128:
                                            ic * T + (tt + 1) * 128],
                                       wo[:, ic * 1024 + half * 512:
                                          ic * 1024 + half * 512 + 512],
                                       start=(ic == 0), stop=(ic == nic - 1)),
                              incs=[("T", 1)] if ic == nic - 1 else ())
            b.wait("vector", "T", tkmm["T"])
            b.wait("vector", f"xo{tt % 2}", xl)
            xn_sl = xnew[:, buf * D + half * 512: buf * D + half * 512 + 512]
            xo_sl = xo[:, half * 512:(half + 1) * 512]
            if add_xn is None:
                b.emit("vector", lambda e, pa=pa, xn_sl=xn_sl, xo_sl=xo_sl:
                       e.tensor_tensor(out=xn_sl, in0=pa, in1=xo_sl, op=OP.add))
            else:
                rr_all, nmr_all = add_xn
                b.emit("vector", lambda e, xo_sl=xo_sl, xn_sl=xn_sl, tt=tt,
                       rr_all=rr_all: e.tensor_scalar(
                    out=xn_sl, in0=xo_sl, scalar1=rr_all[:, tt:tt + 1],
                    scalar2=nmr_all[:, tt:tt + 1], op0=OP.mult, op1=OP.add))
                b.drain("vector")
                b.emit("vector", lambda e, xn_sl=xn_sl, xo_sl=xo_sl:
                       e.tensor_tensor(out=xn_sl, in0=xn_sl, in1=xo_sl, op=OP.add))
                b.drain("vector")
                b.emit("vector", lambda e, pa=pa, xn_sl=xn_sl:
                       e.tensor_tensor(out=xn_sl, in0=xn_sl, in1=pa, op=OP.add))
            if bias_tile is not None:
                b.drain("vector")
                b.emit("vector", lambda e, xn_sl=xn_sl, bias_tile=bias_tile,
                       half=half: e.tensor_tensor(
                    out=xn_sl, in0=xn_sl,
                    in1=bias_tile[:, half * 512:(half + 1) * 512], op=OP.add))
            lastf = b.q["vector"].pop()
            hV = b.sem("V")
            b.cnt["V"] += 1
            tkv = {"V": b.cnt["V"]}
            b.q["vector"].append(lambda e, lastf=lastf, hV=hV: lastf(e).then_inc(hV, 1))
            psfree[bk] = ("V", tkv["V"])
        out_v[tt] = tkv["V"]
        b.wait("sync", "V", tkv["V"])
        out_tok = b.dma("sync", x_out[tt * 128:(tt + 1) * 128, :],
                        xnew[:, buf * D:(buf + 1) * D], sem=f"so{tt % 2}")
        so_tok[tt] = out_tok
    return out_tok


def emit_l1_attn_a(b, ctx, tag, x_in, x_out):
    g_row = b.sb(f"{tag}_gr", [1, D], BF16)
    b_row = b.sb(f"{tag}_br", [1, D], BF16)
    bo_row = b.sb(f"{tag}_bor", [1, D], BF16)
    b.dma("gpsimd", g_row[:, :], b.inp("L1aa_gamma", [1, D])[:, :], sem="d_cg")
    b.dma("gpsimd", b_row[:, :], b.inp("L1aa_beta", [1, D])[:, :], sem="d_cg")
    b.dma("gpsimd", bo_row[:, :], b.inp("L1aa_bout", [1, D])[:, :], sem="d_cg")
    t1 = b.handoff("gpsimd", "d_cg", "HCg")
    gt = b.shared("gb_g", [128, D], BF16)
    bt = b.shared("gb_b", [128, D], BF16)
    ot = b.shared("gb_o", [128, D], BF16)
    bcast_row(b, ctx, gt, g_row, D, row_wait=("HCg", t1))
    bcast_row(b, ctx, bt, b_row, D)
    bcast_row(b, ctx, ot, bo_row, D)
    ropeC, rtok = ctx.lazy("c_rope5_C", [128, T])
    ropeSg, rtok = ctx.lazy("c_rope5_Sg", [128, T])

    xnT = b.shared("xnT", [128, ND * T], BF16)
    ybuf = xnT
    wb2 = b.shared("wb2qk", [128, 2 * ND * 128], BF16)
    qd = b.dram(f"{tag}_qd", [8, 128, T], BF16)
    kd = b.dram(f"{tag}_kd", [8, 128, T], BF16)
    vd = b.dram(f"{tag}_vd", [T, 16 * 65], BF16)

    pro = prologue(b, ctx, tag, x_in, 0, T, xnT, T, 0, ctx.psT,
                   gb_tiles=(gt, bt), pre_tok=("T", b.now("T")))
    attn_stage_qk(b, ctx, tag, xnT, "L1aa_wqk_st", 8, qd, kd,
                  extra_wait=[("V", pro)], wb2=wb2)
    attn_stage_v(b, ctx, tag, xnT, "L1aa_wv", vd, 16, xn_rdy=pro)
    b.wait("sync", rtok[0], rtok[1])
    ytok = attn_core(b, ctx, tag, 16, qd, kd, vd, ybuf, ctx.const(0.125),
                     rope=None if NO_ROPE else (ropeC, ropeSg),
                     stage_tok=b.now("d_x"))
    return resid_proj(b, ctx, tag, ybuf, 8, "L1aa_wo_t", x_in, x_out,
                      bias_tile=ot, y_rdy=[("V", ytok)])


def prep_l1_attn_a(inp, H):
    if "L1aa_wqk_st" in H:
        return
    import ml_dtypes
    bf = ml_dtypes.bfloat16
    w = inp["L1_attn_a_w_qkv"].astype(np.float32)
    H["L1aa_wqk_st"] = _st_tile(np.ascontiguousarray(w[:, 0:2048]), ND, 16)
    H["L1aa_wv"] = np.ascontiguousarray(w[:, 2048:].reshape(ND, 128, 1024))
    H["L1aa_wo_t"] = np.ascontiguousarray(
        inp["L1_attn_a_w_out"].astype(np.float32).reshape(8, 128, D))
    H["L1aa_gamma"] = inp["L1_attn_a_ln_gamma"].astype(np.float32).reshape(1, D)
    H["L1aa_beta"] = inp["L1_attn_a_ln_beta"].astype(np.float32).reshape(1, D)
    H["L1aa_bout"] = inp["L1_attn_a_b_out"].astype(np.float32).reshape(1, D)
    hd = 64
    invf = 1.0 / 10000.0 ** (np.arange(0, hd, 2, dtype=np.float32) / hd)
    t = np.arange(T, dtype=np.float32)
    ang = t[None, :] * invf[:, None]
    cos = np.cos(ang).astype(np.float32)
    sin = np.sin(ang).astype(np.float32)
    C = np.tile(cos, (4, 1))
    Sg = np.empty((128, T), np.float32)
    for blk in range(4):
        s = sin if (blk % 2 == 1) else -sin
        Sg[blk * 32:(blk + 1) * 32] = s
    H["c_rope5_C"] = C.astype(bf)
    H["c_rope5_Sg"] = Sg.astype(bf)


def emit_l2_attn_a(b, ctx, tag, x_in, x_out):
    xnT = b.shared("xnT", [128, ND * T], BF16)
    ybuf = xnT
    wb2 = b.shared("wb2qk", [128, 2 * ND * 128], BF16)
    rr_all = b.shared("st_rr", [128, NT], F32)
    nmr_all = b.shared("st_nmr", [128, NT], F32)
    qd = b.dram(f"{tag}_qd", [8, 128, T], BF16)
    kd = b.dram(f"{tag}_kd", [8, 128, T], BF16)
    vd = b.dram(f"{tag}_vd", [T, 16 * 65], BF16)

    pro = prologue(b, ctx, tag, x_in, 0, T, xnT, T, 0, ctx.psT,
                   pre_tok=("T", b.now("T")), stats_sink=(rr_all, nmr_all))
    attn_stage_qk(b, ctx, tag, xnT, "L2aa_wqk_st", 8, qd, kd,
                  extra_wait=[("V", pro)], wb2=wb2)
    attn_stage_v(b, ctx, tag, xnT, "L2aa_wv", vd, 16, xn_rdy=pro)
    ytok = attn_core(b, ctx, tag, 16, qd, kd, vd, ybuf, ctx.const(0.125),
                     stage_tok=b.now("d_x"))
    if "ybuf9" in DBG:
        b.wait("sync", "V", ytok)
        b.wait("sync", "d_s", b.now("d_s"))
        b.dma("sync", b.outp("dbg_ybuf9", [128, 8 * T], BF16)[:, :], ybuf[:, :],
              sem="d_out")
    b.wait("scalar", "V", ytok)
    b.wait("scalar", "d_s", b.now("d_s"))
    gtok = None
    for c in range(8):
        gtok = b.emit("scalar", lambda e, c=c: e.activation(
            out=ybuf[:, c * T:(c + 1) * T], in_=ybuf[:, c * T:(c + 1) * T],
            func=AF.Gelu), incs=[("S", 1)])["S"]
    return resid_proj(b, ctx, tag, ybuf, 8, "L2aa_wo_t", x_in, x_out,
                      add_xn=(rr_all, nmr_all), y_rdy=[("S", gtok)])


def prep_l2_attn_a(inp, H):
    if "L2aa_wqk_st" in H:
        return
    wq = inp["L2_attn_a_Wq"].astype(np.float32)
    wk = inp["L2_attn_a_Wk"].astype(np.float32)
    H["L2aa_wqk_st"] = _st_tile(np.concatenate([wq, wk], axis=1), ND, 16)
    H["L2aa_wv"] = np.ascontiguousarray(
        inp["L2_attn_a_Wv"].astype(np.float32).reshape(ND, 128, 1024))
    H["L2aa_wo_t"] = np.ascontiguousarray(
        inp["L2_attn_a_Wpost"].astype(np.float32).reshape(8, 128, D))


def emit_l2_attn_b(b, ctx, tag, x_in, x_out):
    xnT = b.shared("xnT", [128, ND * T], BF16)
    wb2 = b.shared("wb2qk", [128, 2 * ND * 128], BF16)
    kds = b.dram(f"{tag}_kd", [32, 128, ND * 128], BF16)
    qds = b.dram(f"{tag}_qd", [8, 128, ND * 512], BF16)
    vd = b.dram(f"{tag}_vd", [T, D], BF16)
    maskA, mtok = ctx.lazy("c_mask_anti", [128, 4 * 512])
    ctx.mask_a = maskA

    pro = prologue(b, ctx, tag, x_in, 0, T, xnT, T, 0, ctx.psT,
                   pre_tok=("T", b.now("T")))
    wbf, _ = bf16_mirror(b, "L2b_wqk_st", [16, 128, ND * 128])
    stg = b.shared("aqk_stg", [128, 2 * 512], BF16)
    Xfn = lambda dc: xnT[:, dc * T:(dc + 1) * T]
    b.dma_guard = getattr(b, "dma_guard", {})

    def evac(ft, ts, pa, nc_, t_tok):
        if ft < 8:
            dc = ft
            sl = stg[:, (ts % 2) * 512:(ts % 2) * 512 + nc_]
            b.wait("scalar", "T", t_tok)
            g = b.dma_guard.get(("aqk", ts % 2))
            if g is not None:
                b.wait("scalar", g[0], g[1])
            tk = b.emit("scalar", lambda e, pa=pa, sl=sl: e.activation(
                out=sl, in_=pa, func=AF.Copy), incs=[("S", 1)])
            b.wait("sync", "S", tk["S"])
            dtok = b.dma("sync", qds[ts][:, dc * 512:dc * 512 + nc_], sl,
                         sem=f"sq{ts % 2}")
            b.dma_guard[("aqk", ts % 2)] = (f"sq{ts % 2}", dtok)
            return ("S", tk["S"])
        dc = ft - 8
        sl = stg[:, (ts % 2) * 512:(ts % 2) * 512 + nc_]
        b.wait("scalar", "T", t_tok)
        g = b.dma_guard.get(("aqk", ts % 2))
        if g is not None:
            b.wait("scalar", g[0], g[1])
        tk = b.emit("scalar", lambda e, pa=pa, sl=sl: e.activation(
            out=sl, in_=pa, func=AF.Copy), incs=[("S", 1)])
        b.wait("sync", "S", tk["S"])
        dst = kds[4 * ts:4 * ts + 4, :, dc * 128:(dc + 1) * 128].rearrange(
            "k p c -> p k c")
        src = sl.rearrange("p (k c) -> p k c", c=128)
        dtok = b.dma("sync", dst, src, sem=f"sq{ts % 2}")
        b.dma_guard[("aqk", ts % 2)] = (f"sq{ts % 2}", dtok)
        return ("S", tk["S"])

    mm_fm(b, ctx, wbf, 16, ND, Xfn, T, evac, ctx.psA[:, 0:1024], wb2,
          extra_wait=[("V", pro)])
    qf_rdy = b.now("S")
    wvbf, _ = bf16_mirror(b, "L2b_wv", [ND, 128, D])
    wv = b.shared("wshare", [128, ND * 1024], BF16)
    b.wait("gpsimd", "T", b.now("T"))
    b.wait("gpsimd", "d_m", b.now("d_m"))
    wvld = b.dma("gpsimd", wv[:, :].rearrange("p (c f) -> p c f", c=ND),
                 wvbf.rearrange("c p f -> p c f"), sem="rw")
    vstg = b.shared("av_stg", [128, 2 * 1040], BF16)
    vfree = {}
    vdma = {}
    for tt in range(NT):
        buf = tt % 2
        stgv = vstg[:, buf * 1040:buf * 1040 + D]
        for half in range(2):
            bk = half
            pa = ctx.psB[:, bk * 512:bk * 512 + 512]
            if tt == 0 and half == 0:
                b.wait("tensor", "rw", wvld)
                b.wait("tensor", "V", b.now("V"))
                b.wait("tensor", "S", b.now("S"))
            if tt > 0:
                b.wait("tensor", "V", vfree.get(bk, 0))
            tk = None
            for dc in range(ND):
                tk = b.emit("tensor", lambda e, pa=pa, dc=dc, tt=tt, half=half:
                            e.matmul(pa, xnT[:, dc * T + tt * 128:
                                             dc * T + (tt + 1) * 128],
                                     wv[:, dc * 1024 + half * 512:
                                        dc * 1024 + half * 512 + 512],
                                     start=(dc == 0), stop=(dc == ND - 1)),
                            incs=[("T", 1)] if dc == ND - 1 else ())
            b.wait("vector", "T", tk["T"])
            if tt >= 2 and half == 0:
                b.wait("vector", f"st{tt % 2}", vdma[tt - 2])
            tv = b.emit("vector", lambda e, pa=pa, stgv=stgv, half=half:
                        e.tensor_copy(out=stgv[:, half * 512:(half + 1) * 512],
                                      in_=pa), incs=[("V", 1)])
            vfree[bk] = tv["V"]
        b.wait("sync", "V", tv["V"])
        vdma[tt] = b.dma("sync", vd[tt * 128:(tt + 1) * 128, :], stgv,
                         sem=f"st{tt % 2}")

    kb = b.shared("l2b_kb", [128, 2 * 1024], BF16)
    vb = b.shared("l2b_vb", [128, 2 * 1024], BF16)
    qbuf = b.shared("l2b_qb", [128, ND * 512], BF16)
    pt = b.shared("l2b_pt", [128, 32 * 512], BF16)
    rqf = b.shared("l2b_rq", [128, 8], F32)
    xold = b.shared("scr8kA", [128, 2 * D], F32)
    xnew = b.shared("scr8kB", [128, 2 * D], F32)
    out_tok = None
    out_v = {}
    for s_ in ("sq0", "sq1", "st0", "st1"):
        b.wait("sync", s_, b.now(s_))
    b.wait("sync", "S", qf_rdy)
    ctx.wait_consts("scalar")
    for qb in range(8):
        kts = list(range(4 * qb, 32))
        b.wait("sync", "T", b.now("T")) if qb > 0 else None
        if qb == 0:
            for s_ in ("sq0", "sq1", "st0", "st1"):
                b.wait("sync", s_, b.now(s_))
        qld = b.dma("sync", qbuf[:, :], qds[qb][:, :], sem="ql")
        kld = {}
        for ki, kt in enumerate(kts):
            kbuf = kb[:, (ki % 2) * 1024:(ki % 2) * 1024 + 1024]
            if ki >= 2:
                b.wait("sync", "T", b.ktok[kts[ki - 2]])
            b.ktok = getattr(b, "ktok", {})
            kld[kt] = b.dma("sync", kbuf, kds[kt][:, :], sem=f"kl{ki % 2}")
            bk = ki % 2
            pa = ctx.psA[:, bk * 512:bk * 512 + 512]
            if qb == 0 and ki == 0:
                b.wait("tensor", "V", b.now("V"))
                b.wait("tensor", "S", b.now("S"))
            if qb > 0 and ki == 0:
                b.wait("tensor", "V", b.now("V"))
            b.wait("tensor", f"kl{ki % 2}", kld[kt])
            if ki == 0:
                b.wait("tensor", "ql", qld)
            b.wait("tensor", "S", b.exp_tok.get(bk, 0) if hasattr(b, "exp_tok") else 0)
            tkS = None
            for dc in range(ND):
                tkS = b.emit("tensor", lambda e, pa=pa, kbuf=kbuf, dc=dc:
                             e.matmul(pa, kbuf[:, dc * 128:(dc + 1) * 128],
                                      qbuf[:, dc * 512:(dc + 1) * 512],
                                      start=(dc == 0), stop=(dc == ND - 1)),
                             incs=[("T", 1)] if dc == ND - 1 else ())
            b.ktok[kt] = tkS["T"]
            pslot = pt[:, ki * 512:(ki + 1) * 512]
            b.wait("scalar", "T", tkS["T"])
            tke = b.emit("scalar", lambda e, pa=pa, pslot=pslot: e.activation(
                out=pslot, in_=pa, func=AF.Exp, scale=ctx.const(0.03125)),
                incs=[("S", 1)])
            if not hasattr(b, "exp_tok"):
                b.exp_tok = {}
            b.exp_tok[bk] = tke["S"]
            a = kt - 4 * qb
            if 0 <= a <= 3:
                b.wait("vector", "S", tke["S"])
                b.wait("vector", mtok[0], mtok[1])
                b.emit("vector", lambda e, pslot=pslot, a=a: e.tensor_tensor(
                    out=pslot, in0=pslot, in1=maskA[:, :][:, a * 512:(a + 1) * 512],
                    op=OP.mult), incs=[("V", 1)])
        exp_all0 = b.now("S")
        mask_all0 = b.now("V")
        ctx.wait_consts("tensor")
        b.wait("tensor", "S", exp_all0)
        b.wait("tensor", "V", mask_all0)
        tvq = None
        for qt in range(4):
            prs = ctx.psA[:, 3 * 512 + 8 * qt: 3 * 512 + 8 * qt + 1]
            if tvq is not None:
                b.wait("tensor", "V", tvq["V"])
            tks = None
            for ki in range(len(kts)):
                tks = b.emit("tensor", lambda e, prs=prs, ki=ki, qt=qt,
                             last=(ki == len(kts) - 1):
                             e.matmul(prs, pt[:, ki * 512 + qt * 128:
                                              ki * 512 + (qt + 1) * 128],
                                      ctx.ones_col[:, :],
                                      start=(ki == 0), stop=last),
                             incs=[("T", 1)] if ki == len(kts) - 1 else ())
            b.wait("vector", "T", tks["T"])
            tvq = b.emit("vector", lambda e, prs=prs, qt=qt: e.tensor_copy(
                out=rqf[:, qt:qt + 1], in_=prs), incs=[("V", 1)])
        b.drain("vector")
        tvq = b.emit("vector", lambda e: e.reciprocal(rqf[:, 0:4], rqf[:, 0:4]),
                     incs=[("V", 1)])
        b.drain("vector")
        exp_all = b.now("S")
        for qt in range(4):
            tglob = qb * 4 + qt
            buf = tglob % 2
            b.so2_tok = getattr(b, "so2_tok", {})
            if tglob >= 2:
                b.wait("sync", "V", out_v[tglob - 2])
                b.wait("sync", f"so{tglob % 2}", b.so2_tok[tglob - 2])
            xo = xold[:, buf * D:(buf + 1) * D]
            xl = b.dma("sync", xo, x_in[tglob * 128:(tglob + 1) * 128, :],
                       sem=f"xo{tglob % 2}")
            avt = {}
            for ki, kt in enumerate(kts):
                vbuf = vb[:, (ki % 2) * 1024:(ki % 2) * 1024 + 1024]
                if ki >= 2:
                    b.wait("sync", "T", avt[ki - 2])
                vld = b.dma("sync", vbuf, vd[kt * 128:(kt + 1) * 128, :],
                            sem=f"vl{ki % 2}")
                b.wait("tensor", f"vl{ki % 2}", vld)
                if ki == 0:
                    b.wait("tensor", "S", exp_all)
                    b.wait("tensor", "V", b.now("V"))
                tkA = None
                for half in range(2):
                    pa = ctx.psB[:, half * 512:half * 512 + 512]
                    tkA = b.emit("tensor", lambda e, pa=pa, vbuf=vbuf, half=half,
                                 ki=ki, qt=qt, last=(ki == len(kts) - 1):
                                 e.matmul(pa, pt[:, ki * 512 + qt * 128:
                                                 ki * 512 + (qt + 1) * 128],
                                          vbuf[:, half * 512:(half + 1) * 512],
                                          start=(ki == 0), stop=last),
                                 incs=[("T", 1)] if (half == 1) else ())
                avt[ki] = tkA["T"]
            b.wait("vector", "T", avt[len(kts) - 1])
            b.wait("vector", f"xo{tglob % 2}", xl)
            b.wait("vector", "V", tvq["V"])
            tkv = None
            for half in range(2):
                pa = ctx.psB[:, half * 512:half * 512 + 512]
                xn_sl = xnew[:, buf * D + half * 512:buf * D + half * 512 + 512]
                xo_sl = xo[:, half * 512:(half + 1) * 512]
                tkv = b.emit("vector", lambda e, pa=pa, xn_sl=xn_sl, xo_sl=xo_sl,
                             qt=qt: e.scalar_tensor_tensor(
                    out=xn_sl, in0=pa, scalar=rqf[:, qt:qt + 1], in1=xo_sl,
                    op0=OP.mult, op1=OP.add), incs=[("V", 1)])
            out_v[tglob] = tkv["V"]
            b.wait("sync", "V", tkv["V"])
            out_tok = b.dma("sync", x_out[tglob * 128:(tglob + 1) * 128, :],
                            xnew[:, buf * D:(buf + 1) * D], sem=f"so{tglob % 2}")
            b.so2_tok[tglob] = out_tok
    return out_tok


def prep_l2_attn_b(inp, H):
    if "L2b_wqk_st" in H:
        return
    import ml_dtypes
    bf = ml_dtypes.bfloat16
    w = inp["L2_attn_b_W"].astype(np.float32)
    H["L2b_wqk_st"] = _st_tile(np.ascontiguousarray(w[:, 0:2048]), ND, 16)
    H["L2b_wv"] = np.ascontiguousarray(w[:, 2048:].reshape(ND, 128, D))
    m = np.zeros((128, 4 * 512), np.float32)
    p = np.arange(128)[:, None]
    j = np.arange(512)[None, :]
    for a in range(4):
        m[:, a * 512:(a + 1) * 512] = (128 * a + p >= j).astype(np.float32)
    H["c_mask_anti"] = m.astype(bf)
    H["c_ident_f32"] = np.eye(128, dtype=np.float32)


SUBS[5] = (emit_l1_attn_a, prep_l1_attn_a)
SUBS[9] = (emit_l2_attn_a, prep_l2_attn_a)
SUBS[11] = (emit_l2_attn_b, prep_l2_attn_b)


def ln_stream(b, ctx, tag, x_in, x_out, eps):
    st = b.shared("pro_st", [128, 16], F32)
    s2s = b.shared("pro_s2", [128, 32], F32)
    xa = b.shared("scr8kA", [128, 2 * D], F32)
    xo = b.shared("scr8kB", [128, 2 * D], F32)
    b.wait("sync", "V", b.now("V"))
    b.wait("sync", "S", b.now("S"))
    b.wait("sync", "so0", b.now("so0"))
    b.wait("sync", "so1", b.now("so1"))
    ctx.wait_consts("scalar")
    out_tok = None
    sq_tok, ap_tok = {}, {}
    for i in range(NT):
        buf = i % 2
        xt = xa[:, buf * D:(buf + 1) * D]
        xn_t = xo[:, buf * D:(buf + 1) * D]
        if i >= 2:
            b.wait("sync", "V", ap_tok[i - 2])
            b.wait("sync", "S", sq_tok[i - 2])
            b.wait("sync", f"so{i % 2}", out_tok_d[i - 2])
        ld = b.dma("sync", xt, x_in[i * 128:(i + 1) * 128, :], sem=f"x{i % 2}")
        s2 = s2s[:, i:i + 1]
        s1 = st[:, buf * 8 + 1:buf * 8 + 2]
        mean = st[:, buf * 8 + 2:buf * 8 + 3]
        var = st[:, buf * 8 + 3:buf * 8 + 4]
        sd = st[:, buf * 8 + 4:buf * 8 + 5]
        rr = st[:, buf * 8 + 5:buf * 8 + 6]
        nmr = st[:, buf * 8 + 6:buf * 8 + 7]
        b.wait("scalar", f"x{i % 2}", ld)
        sq_tok[i] = b.emit("scalar", lambda e, xt=xt, s2=s2: e.activation(
            out=ctx.sqscr[:, :], in_=xt, func=AF.Square, accum_out=s2),
            incs=[("S", 1)])["S"]
        b.wait("vector", f"x{i % 2}", ld)
        b.emit("vector", lambda e, xt=xt, s1=s1: e.tensor_reduce(
            s1, xt, AX.X, OP.add))
        b.drain("vector")
        b.emit("vector", lambda e, s1=s1, mean=mean: e.tensor_scalar(
            out=mean, in0=s1, scalar1=1.0 / D, scalar2=None, op0=OP.mult))
        b.wait("vector", "S", sq_tok[i])
        b.drain("vector")
        b.emit("vector", lambda e, mean=mean, var=var: e.scalar_tensor_tensor(
            out=var, in0=mean, scalar=-1.0, in1=mean, op0=OP.mult, op1=OP.mult))
        b.drain("vector")
        tv = b.emit("vector", lambda e, s2=s2, var=var: e.scalar_tensor_tensor(
            out=var, in0=s2, scalar=1.0 / D, in1=var, op0=OP.mult, op1=OP.add),
            incs=[("V", 1)])["V"]
        b.wait("scalar", "V", tv)
        ts_ = b.emit("scalar", lambda e, var=var, sd=sd: e.activation(
            out=sd, in_=var, func=AF.Sqrt, bias=ctx.const(eps)),
            incs=[("S", 1)])["S"]
        b.wait("vector", "S", ts_)
        b.emit("vector", lambda e, sd=sd, rr=rr: e.reciprocal(rr, sd))
        b.drain("vector")
        b.emit("vector", lambda e, mean=mean, rr=rr, nmr=nmr: e.scalar_tensor_tensor(
            out=nmr, in0=mean, scalar=-1.0, in1=rr, op0=OP.mult, op1=OP.mult))
        b.drain("vector")
        ap_tok[i] = b.emit("vector", lambda e, xt=xt, xn_t=xn_t, rr=rr, nmr=nmr:
                           e.tensor_scalar(out=xn_t, in0=xt, scalar1=rr,
                                           scalar2=nmr, op0=OP.mult, op1=OP.add),
                           incs=[("V", 1)])["V"]
        b.wait("sync", "V", ap_tok[i])
        out_tok = b.dma("sync", x_out[i * 128:(i + 1) * 128, :], xn_t,
                        sem=f"so{i % 2}")
        if i == 0:
            out_tok_d = {}
        out_tok_d[i] = out_tok
    return out_tok


def emit_l0_attn_b(b, ctx, tag, x_in, x_out):
    EPSI = 1e-10
    xp = b.dram(f"{tag}_xp", [T, D], F32)
    xa_d = b.dram(f"{tag}_xa", [T, D], F32)
    ln_stream(b, ctx, tag, x_in, xp, EPS_LN)

    bq_c = b.sb(f"{tag}_bqc", [128, 8], F32)
    bk_c = b.sb(f"{tag}_bkc", [128, 8], F32)
    b.dma("sync", bq_c[:, :], b.inp("L0ab_bq_c", [128, 8])[:, :], sem="d_cs")
    b.dma("sync", bk_c[:, :], b.inp("L0ab_bk_c", [128, 8])[:, :], sem="d_cs")
    bct = b.handoff("sync", "d_cs", "HCs")
    bv_r = b.sb(f"{tag}_bvr", [1, D], BF16)
    bp_r = b.sb(f"{tag}_bpr", [1, D], BF16)
    b.dma("gpsimd", bv_r[:, :], b.inp("L0ab_bv", [1, D])[:, :], sem="d_cg")
    b.dma("gpsimd", bp_r[:, :], b.inp("L0ab_bproj", [1, D])[:, :], sem="d_cg")
    t1 = b.handoff("gpsimd", "d_cg", "HCg")
    bvt = b.shared("gb_g", [128, D], BF16)
    bpt = b.shared("gb_b", [128, D], BF16)
    bcast_row(b, ctx, bvt, bv_r, D, row_wait=("HCg", t1))
    bcast_row(b, ctx, bpt, bp_r, D)

    xnT = b.shared("xnT", [128, ND * T], BF16)
    ybuf = xnT
    wb2 = b.shared("wb2qk", [128, 2 * ND * 128], BF16)
    qd = b.dram(f"{tag}_qd", [8, 128, T], BF16)
    kd = b.dram(f"{tag}_kd", [8, 128, T], BF16)
    vd = b.dram(f"{tag}_vd", [T, 16 * 65], BF16)
    maskC, mctok = ctx.lazy("c_mask_caus", [128, 4 * 512])
    ctx.mask_c = maskC

    pro = prologue(b, ctx, tag + "a", xp, 0, T, xnT, T, 0, ctx.psT,
                   pre_tok=("T", b.now("T")), eps=EPSI)
    b.wait("scalar", "HCs", bct)
    b.wait("vector", "HCs", bct)
    attn_stage_qk(b, ctx, tag, xnT, "L0ab_wqk_st", 8, qd, kd,
                  extra_wait=[("V", pro)], wb2=wb2, qb_col=bq_c, kb_col=bk_c)
    attn_stage_v(b, ctx, tag, xnT, "L0ab_wv", vd, 16, vb_tile=bvt, xn_rdy=pro)
    b.wait("sync", mctok[0], mctok[1])
    ytok = attn_core(b, ctx, tag, 16, qd, kd, vd, ybuf, ctx.const(0.125),
                     mask="causal", stage_tok=None)
    resid_proj(b, ctx, tag, ybuf, 8, "L0ab_wproj_t", xp, xa_d,
               bias_tile=bpt, y_rdy=[("V", ytok)])

    w1bf, _ = bf16_mirror(b, "L0ab_wfc_st", [32, 128, ND * 128])
    w2bf, w2tok = bf16_mirror(b, "L0ab_wmlp_t", [32, 128, D])
    bfc_c = b.sb(f"{tag}_bfc", [128, 32], F32)
    b.dma("sync", bfc_c[:, :], b.inp("L0ab_bfc_c", [128, 32])[:, :],
          sem="d_cs")
    bfct = b.handoff("sync", "d_cs", "HCs")
    bm_r = b.sb(f"{tag}_bmr", [1, D], BF16)
    b.dma("gpsimd", bm_r[:, :], b.inp("L0ab_bmlp", [1, D])[:, :], sem="d_cg")
    t2 = b.handoff("gpsimd", "d_cg", "HCg")
    bmt = b.shared("gb_o", [128, D], BF16)
    bcast_row(b, ctx, bmt, bm_r, D, row_wait=("HCg", t2))
    TQ = 256
    xnTq = xnT[:, 0:ND * TQ]
    w2sb = xnT[:, ND * TQ:ND * TQ + 32 * 512]

    H1 = b.shared("l0ab_H1", [128, 32 * 256], BF16)
    hs = b.shared("l0ab_hs", [128, 4 * 512], F32)
    wb2b = b.shared(f"{tag}_wbb", [128, 2 * ND * 128], BF16)
    xold = b.shared("scr8kA", [128, 2 * D], F32)
    xnew = b.shared("scr8kB", [128, 2 * D], F32)
    xpb = b.shared("l0ab_xpb", [128, 2 * D], F32)
    out_tok = None
    C_GELU = float(np.float32(math.sqrt(2.0 / math.pi)))
    for q in range(T // TQ):
        t0 = q * TQ
        pro2 = prologue(b, ctx, f"{tag}m{q}", xa_d, t0, t0 + TQ, xnTq, TQ, 0,
                        ctx.psT, pre_tok=("T", b.now("T")), eps=EPSI)
        Xfn = lambda dc: xnTq[:, dc * TQ:(dc + 1) * TQ]

        def evac_g(ft, ts, pa, nc_, t_tok):
            hsl = hs[:, 0:nc_]
            usl = hs[:, 512:512 + nc_]
            wsl = hs[:, 1024:1024 + nc_]
            tsl = hs[:, 1536:1536 + nc_]
            dst = H1[:, ft * TQ + ts * 512: ft * TQ + ts * 512 + nc_]
            b.wait("vector", "T", t_tok)
            b.wait("vector", "HCs", bfct)
            if hasattr(b, "hs_free"):
                b.wait("vector", "V", b.hs_free)
            tkh = b.emit("vector", lambda e, pa=pa, hsl=hsl, ft=ft:
                         e.tensor_scalar(out=hsl, in0=pa,
                                         scalar1=bfc_c[:, ft:ft + 1],
                                         scalar2=None, op0=OP.add),
                         incs=[("V", 1)])
            b.wait("scalar", "V", tkh["V"])
            tk1 = b.emit("scalar", lambda e, hsl=hsl, usl=usl: e.activation(
                out=usl, in_=hsl, func=AF.Square), incs=[("S", 1)])
            b.wait("vector", "S", tk1["S"])
            b.emit("vector", lambda e, usl=usl, hsl=hsl, wsl=wsl:
                   e.scalar_tensor_tensor(out=wsl, in0=usl, scalar=0.044715,
                                          in1=hsl, op0=OP.mult, op1=OP.mult))
            b.drain("vector")
            tk2 = b.emit("vector", lambda e, wsl=wsl, hsl=hsl: e.tensor_tensor(
                out=wsl, in0=wsl, in1=hsl, op=OP.add), incs=[("V", 1)])
            b.wait("scalar", "V", tk2["V"])
            tk3 = b.emit("scalar", lambda e, wsl=wsl, tsl=tsl: e.activation(
                out=tsl, in_=wsl, func=AF.Tanh, scale=ctx.const(C_GELU)),
                incs=[("S", 1)])
            b.wait("vector", "S", tk3["S"])
            tk4 = b.emit("vector", lambda e, tsl=tsl, hsl=hsl, dst=dst:
                         e.scalar_tensor_tensor(out=dst, in0=tsl, scalar=1.0,
                                                in1=hsl, op0=OP.add, op1=OP.mult),
                         incs=[("V", 1)])
            b.hs_free = tk4["V"]
            return ("V", tk4["V"])

        mm_fm(b, ctx, w1bf, 32, ND, Xfn, TQ, evac_g, ctx.psA[:, 0:1024], wb2b,
              extra_wait=[("V", pro2)])
        h_done = b.now("V")
        psfree = [None, None]
        xnew4 = b.shared("scr8kB", [128, 2 * D], F32)
        b.wait("sync", "V", b.now("V"))
        b.wait("sync", "S", b.now("S"))
        st_tok = {}
        for half in range(2):
            b.wait("gpsimd", "T", b.now("T"))
            b.wait("gpsimd", "d_m", b.now("d_m"))
            w2ld = b.dma("gpsimd",
                         w2sb.rearrange("p (c f) -> p c f", c=32),
                         w2bf[:, :, half * 512:(half + 1) * 512].rearrange(
                             "c p f -> p c f"), sem="rw")
            for tt in range(TQ // 128):
                tg = q * (TQ // 128) + tt
                buf = tt % 2
                bk = tt % 2
                pa = ctx.psB[:, bk * 512:bk * 512 + 512]
                xo = xold[:, buf * D:(buf + 1) * D]
                xpo = xpb[:, buf * D:(buf + 1) * D]
                if (half, tt) >= (0, 2):
                    b.wait("sync", "V", xl_guard)
                xl = b.dma("sync", xo, x_in[tg * 128:(tg + 1) * 128, :],
                           sem=f"xo{tt % 2}")
                xl2 = b.dma("sync", xpo, xa_d[tg * 128:(tg + 1) * 128, :],
                            sem=f"xo{tt % 2}")
                if tt == 0:
                    b.wait("tensor", "rw", w2ld)
                    if half == 0:
                        b.wait("tensor", "V", h_done)
                    b.wait("tensor", "V", b.now("V"))
                    b.wait("tensor", "S", b.now("S"))
                if psfree[bk] is not None:
                    b.wait("tensor", psfree[bk][0], psfree[bk][1])
                tkmm = None
                for gc in range(32):
                    tkmm = b.emit("tensor", lambda e, pa=pa, gc=gc, tt=tt:
                                  e.matmul(
                        pa, H1[:, gc * TQ + tt * 128: gc * TQ + (tt + 1) * 128],
                        w2sb[:, gc * 512:(gc + 1) * 512],
                        start=(gc == 0), stop=(gc == 31)),
                        incs=[("T", 1)] if gc == 31 else ())
                b.wait("vector", "T", tkmm["T"])
                b.wait("vector", f"xo{tt % 2}", xl2)
                if tt in st_tok:
                    b.wait("vector", f"so{tt % 2}", st_tok[tt])
                xn_sl = xnew4[:, tt * D + half * 512: tt * D + half * 512 + 512]
                b.emit("vector", lambda e, pa=pa, xn_sl=xn_sl, xpo=xpo, half=half:
                       e.scalar_tensor_tensor(
                    out=xn_sl, in0=pa, scalar=ctx.const(0.5),
                    in1=xpo[:, half * 512:(half + 1) * 512],
                    op0=OP.mult, op1=OP.add))
                b.drain("vector")
                b.emit("vector", lambda e, xn_sl=xn_sl, xo=xo, half=half:
                       e.tensor_tensor(out=xn_sl, in0=xn_sl,
                                       in1=xo[:, half * 512:(half + 1) * 512],
                                       op=OP.add))
                b.drain("vector")
                tkv = b.emit("vector", lambda e, xn_sl=xn_sl, half=half:
                             e.tensor_tensor(out=xn_sl, in0=xn_sl,
                                             in1=bmt[:, half * 512:(half + 1) * 512],
                                             op=OP.add), incs=[("V", 1)])
                psfree[bk] = ("V", tkv["V"])
                xl_guard = tkv["V"]
                if half == 1:
                    b.wait("sync", "V", tkv["V"])
                    out_tok = b.dma("sync", x_out[tg * 128:(tg + 1) * 128, :],
                                    xnew4[:, tt * D:(tt + 1) * D],
                                    sem=f"so{tt % 2}")
                    st_tok[tt] = out_tok
    return out_tok


def prep_l0_attn_b(inp, H):
    if "L0ab_wqk_st" in H:
        return
    import ml_dtypes
    bf = ml_dtypes.bfloat16
    w = inp["L0_attn_b_w_attn"].astype(np.float32)
    wt = w.T
    H["L0ab_wqk_st"] = _st_tile(np.ascontiguousarray(wt[:, 0:2048]), ND, 16)
    H["L0ab_wv"] = np.ascontiguousarray(wt[:, 2048:].reshape(ND, 128, 1024))
    ba = inp["L0_attn_b_b_attn"].astype(np.float32)
    H["L0ab_bq_c"] = _colmajor(ba[0:1024], 8)
    H["L0ab_bk_c"] = _colmajor(ba[1024:2048], 8)
    H["L0ab_bv"] = ba[2048:].reshape(1, D)
    H["L0ab_wproj_t"] = np.ascontiguousarray(
        inp["L0_attn_b_w_proj"].astype(np.float32).T.reshape(8, 128, D))
    H["L0ab_bproj"] = inp["L0_attn_b_b_proj"].astype(np.float32).reshape(1, D)
    H["L0ab_wfc_st"] = _st_tile(
        np.ascontiguousarray(inp["L0_attn_b_w_fc"].astype(np.float32).T), ND, 32)
    H["L0ab_bfc_c"] = _colmajor(inp["L0_attn_b_b_fc"].astype(np.float32), 32)
    H["L0ab_wmlp_t"] = np.ascontiguousarray(
        inp["L0_attn_b_w_mlp_proj"].astype(np.float32).T.reshape(32, 128, D))
    H["L0ab_bmlp"] = inp["L0_attn_b_b_mlp_proj"].astype(np.float32).reshape(1, D)
    m = np.zeros((128, 4 * 512), np.float32)
    p = np.arange(128)[:, None]
    j = np.arange(512)[None, :]
    for a in range(4):
        m[:, a * 512:(a + 1) * 512] = (p <= j - 128 * a).astype(np.float32)
    H["c_mask_caus"] = m.astype(bf)


SUBS[3] = (emit_l0_attn_b, prep_l0_attn_b)


def emit_l1_attn_b(b, ctx, tag, x_in, x_out):
    xnT = b.shared("xnT", [128, ND * T], BF16)
    ybuf = xnT
    wb2 = b.shared("wb2qk", [128, 2 * ND * 128], BF16)
    qd = b.dram(f"{tag}_qd", [8, 128, T], BF16)
    kd = b.dram(f"{tag}_kd", [8, 128, T], BF16)
    vd = b.dram(f"{tag}_vd", [T, D], BF16)
    bq_c = b.sb(f"{tag}_bqc", [128, 8], F32)
    bk_c = b.sb(f"{tag}_bkc", [128, 8], F32)
    b.dma("sync", bq_c[:, :], b.inp("L1ab_bq_c", [128, 8])[:, :], sem="d_cs")
    lnwb = b.sb(f"{tag}_lnwb", [128, 16], F32)
    b.dma("sync", lnwb[:, 0:8], b.inp("L1ab_lnw_c", [128, 8])[:, :],
          sem="d_cs")
    b.dma("sync", lnwb[:, 8:16], b.inp("L1ab_lnb_c", [128, 8])[:, :],
          sem="d_cs")
    b.dma("sync", bk_c[:, :], b.inp("L1ab_bk_c", [128, 8])[:, :], sem="d_cs")
    bct = b.handoff("sync", "d_cs", "HCs")
    lnt = bct
    bv_r = b.sb(f"{tag}_bvr", [1, D], BF16)
    b.dma("gpsimd", bv_r[:, :], b.inp("L1ab_bv", [1, D])[:, :], sem="d_cg")
    bo_r = b.sb(f"{tag}_bor", [1, D], BF16)
    b.dma("gpsimd", bo_r[:, :], b.inp("L1ab_bo", [1, D])[:, :], sem="d_cg")
    lam4 = b.sb(f"{tag}_lam4", [128, 4], BF16)
    b.dma("gpsimd", lam4[:, :], b.inp("L1ab_lam4", [128, 4])[:, :],
          sem="d_cg")
    t1 = b.handoff("gpsimd", "d_cg", "HCg")
    lamt = t1
    bvt = b.shared("gb_g", [128, D], BF16)
    bcast_row(b, ctx, bvt, bv_r, D, row_wait=("HCg", t1))
    bot = b.shared("gb_b", [128, D], BF16)
    bcast_row(b, ctx, bot, bo_r, D)

    pro = prologue(b, ctx, tag, x_in, 0, T, xnT, T, 0, ctx.psT,
                   pre_tok=("T", b.now("T")))
    b.wait("scalar", "HCs", bct)
    b.wait("vector", "HCs", bct)
    attn_stage_qk(b, ctx, tag, xnT, "L1ab_wqk_st", 8, qd, kd,
                  extra_wait=[("V", pro)], wb2=wb2, qb_col=bq_c, kb_col=bk_c)
    wvbf, _ = bf16_mirror(b, "L1ab_wv", [ND, 128, D])
    wv = b.shared("wshare", [128, ND * 1024], BF16)
    b.wait("gpsimd", "T", b.now("T"))
    b.wait("gpsimd", "d_m", b.now("d_m"))
    wvld = b.dma("gpsimd", wv[:, :].rearrange("p (c f) -> p c f", c=ND),
                 wvbf.rearrange("c p f -> p c f"), sem="rw")
    vstg = b.shared("av_stg", [128, 2 * 1040], BF16)
    vfree = {}
    vdma = {}
    tv = None
    for tt in range(NT):
        buf = tt % 2
        stgv = vstg[:, buf * 1040:buf * 1040 + D]
        for half in range(2):
            bk = half
            pa = ctx.psB[:, bk * 512:bk * 512 + 512]
            if tt == 0 and half == 0:
                b.wait("tensor", "rw", wvld)
                b.wait("tensor", "V", pro)
                b.wait("tensor", "V", b.now("V"))
                b.wait("tensor", "S", b.now("S"))
            if tt > 0:
                b.wait("tensor", "V", vfree.get(bk, 0))
            tk = None
            for dc in range(ND):
                tk = b.emit("tensor", lambda e, pa=pa, dc=dc, tt=tt, half=half:
                            e.matmul(pa, xnT[:, dc * T + tt * 128:
                                             dc * T + (tt + 1) * 128],
                                     wv[:, dc * 1024 + half * 512:
                                        dc * 1024 + half * 512 + 512],
                                     start=(dc == 0), stop=(dc == ND - 1)),
                            incs=[("T", 1)] if dc == ND - 1 else ())
            b.wait("vector", "T", tk["T"])
            if tt >= 2 and half == 0:
                b.wait("vector", f"st{tt % 2}", vdma[tt - 2])
            tv = b.emit("vector", lambda e, pa=pa, stgv=stgv, half=half:
                        e.tensor_tensor(out=stgv[:, half * 512:(half + 1) * 512],
                                        in0=pa,
                                        in1=bvt[:, half * 512:(half + 1) * 512],
                                        op=OP.add), incs=[("V", 1)])
            vfree[bk] = tv["V"]
        b.wait("sync", "V", tv["V"])
        vdma[tt] = b.dma("sync", vd[tt * 128:(tt + 1) * 128, :], stgv,
                         sem=f"st{tt % 2}")

    b.wait("tensor", "HCg", lamt)
    b.wait("tensor", "V", b.now("V"))
    b.wait("tensor", "S", b.now("S"))
    tkl = b.emit("tensor", lambda e: e.matmul(
        ctx.psA[0:1, 0:1], lam4[:, 0:1], lam4[:, 1:2], start=True, stop=True))
    tkl = b.emit("tensor", lambda e: e.matmul(
        ctx.psA[0:1, 4:5], lam4[:, 2:3], lam4[:, 3:4], start=True, stop=True),
        incs=[("T", 1)])
    b.wait("scalar", "T", tkl["T"])
    lsc = b.shared("ac_rsc", [128, 512], F32)
    tke = b.emit("scalar", lambda e: e.activation(
        out=lsc[0:1, 0:1], in_=ctx.psA[0:1, 0:1], func=AF.Exp), incs=[("S", 1)])
    tke = b.emit("scalar", lambda e: e.activation(
        out=lsc[0:1, 1:2], in_=ctx.psA[0:1, 4:5], func=AF.Exp), incs=[("S", 1)])
    b.wait("vector", "S", tke["S"])
    b.emit("vector", lambda e: e.tensor_tensor(
        out=lsc[0:1, 2:3], in0=lsc[0:1, 1:2], in1=lsc[0:1, 0:1], op=OP.subtract))
    b.drain("vector")
    tkv2 = b.emit("vector", lambda e: e.tensor_scalar(
        out=lsc[0:1, 2:3], in0=lsc[0:1, 2:3], scalar1=-0.8, scalar2=None,
        op0=OP.add), incs=[("V", 1)])
    b.wait("tensor", "V", tkv2["V"])
    tkb = b.emit("tensor", lambda e: e.matmul(
        ctx.psA[:, 8:9], ctx.ones_t[0:1, :], lsc[0:1, 2:3],
        start=True, stop=True), incs=[("T", 1)])
    b.wait("vector", "T", tkb["T"])
    lamcol = b.shared("l2b_rq", [128, 8], F32)
    tlam = b.emit("vector", lambda e: e.tensor_copy(out=lamcol[:, 0:1],
                                                    in_=ctx.psA[:, 8:9]),
                  incs=[("V", 1)])["V"]

    qb_ = b.shared("ac_q", [128, T], BF16)
    kb_ = b.shared("ac_k", [128, T], BF16)
    vvb = b.shared("ac_v", [128, 2 * 32 * 130], BF16)
    pt = b.shared("ac_pt", [128, 2 * 1024], BF16)
    rbt = b.shared("ac_rbt", [128, 512], BF16)
    v1b = vvb[:, 0:32 * 128]
    v2b = vvb[:, 32 * 128:2 * 32 * 128]
    stage_tok = b.now("d_x")
    pair_rdy = None
    y_tok = 0
    for h in range(8):
        if pair_rdy is not None:
            b.wait("sync", "T", pair_rdy)
        else:
            for s_ in ("sq0", "sq1", "st0", "st1"):
                b.wait("sync", s_, b.now(s_))
        lq = b.dma("sync", qb_[:, :], qd[h][:, :], sem="pl0")
        lk = b.dma("sync", kb_[:, :], kd[h][:, :], sem="pl0")
        lv = b.dma("sync", v1b.rearrange("p (k c) -> p k c", c=128),
                   vd[:, (h % 4) * 128:(h % 4 + 1) * 128].rearrange(
                       "(k p) c -> p k c", p=128), sem="pl0")
        lv = b.dma("sync", v2b.rearrange("p (k c) -> p k c", c=128),
                   vd[:, (4 + h % 4) * 128:(5 + h % 4) * 128].rearrange(
                       "(k p) c -> p k c", p=128), sem="pl0")
        b.wait("vector", "pl0", lv)
        b.wait("vector", "V", tlam)
        vv_tok = b.emit("vector", lambda e: e.scalar_tensor_tensor(
            out=v1b, in0=v2b, scalar=lamcol[:, 0:1], in1=v1b,
            op0=OP.mult, op1=OP.add), incs=[("V", 1)])["V"]
        for qb in range(T // 512):
            ptok = {}
            first_av = True
            rs_first = True
            for gi in range(16):
                for j in range(2):
                    kt = gi * 2 + j
                    bk = j
                    pa = ctx.psA[:, bk * 512:(bk + 1) * 512]
                    if gi == 0 and j == 0:
                        b.wait("tensor", "pl0", lv)
                        b.wait("tensor", "V", b.now("V"))
                    b.wait("tensor", "S", ptok.get(bk, 0))
                    tkS = b.emit("tensor", lambda e, pa=pa, kt=kt, qb=qb: e.matmul(
                        pa, kb_[:, kt * 128:(kt + 1) * 128],
                        qb_[:, qb * 512:(qb + 1) * 512],
                        start=True, stop=True), incs=[("T", 1)])
                pslot = pt[:, 0:1024]
                b.wait("scalar", "T", tkS["T"])
                tke = b.emit("scalar", lambda e, pslot=pslot: e.activation(
                    out=pslot, in_=ctx.psA[:, 0:1024], func=AF.Exp,
                    scale=ctx.const(float(np.float32(1.0 / np.sqrt(np.float32(128.0)))))),
                    incs=[("S", 1)])
                ptok[0] = tke["S"]
                ptok[1] = tke["S"]
                b.wait("tensor", "S", tke["S"])
                if gi == 0:
                    b.wait("tensor", "V", vv_tok)
                for j in range(2):
                    kt = gi * 2 + j
                    last = (gi == 15 and j == 1)
                    b.emit("tensor", lambda e, kt=kt, j=j, fa=first_av, last=last:
                           e.matmul(ctx.psB[:, 0:512],
                                    v1b[:, kt * 128:(kt + 1) * 128],
                                    pt[:, j * 512:(j + 1) * 512],
                                    start=fa, stop=last))
                    b.emit("tensor", lambda e, kt=kt, j=j, fa=rs_first, last=last:
                           e.matmul(ctx.psB[0:1, 512:1024],
                                    ctx.ones_col[:, :],
                                    pt[:, j * 512:(j + 1) * 512],
                                    start=fa, stop=last),
                           incs=[("T", 1)] if last else ())
                    first_av = False
                    rs_first = False
            av_tok = b.now("T")
            b.wait("vector", "T", av_tok)
            lsc2 = lsc
            b.emit("vector", lambda e: e.tensor_copy(out=lsc2[0:1, 0:512],
                                                     in_=ctx.psB[0:1, 512:1024]))
            b.drain("vector")
            trc = b.emit("vector", lambda e: e.reciprocal(lsc2[0:1, 0:512],
                                                          lsc2[0:1, 0:512]),
                         incs=[("V", 1)])
            b.wait("tensor", "V", trc["V"])
            tkb2 = b.emit("tensor", lambda e: e.matmul(
                ctx.psA[:, 0:512], ctx.ones_t[0:1, :], lsc2[0:1, 0:512],
                start=True, stop=True), incs=[("T", 1)])
            b.wait("vector", "T", tkb2["T"])
            b.emit("vector", lambda e: e.tensor_copy(out=rbt[:, :],
                                                     in_=ctx.psA[:, 0:512]))
            b.drain("vector")
            ty = b.emit("vector", lambda e, h=h, qb=qb: e.tensor_tensor(
                out=ybuf[:, h * T + qb * 512:h * T + (qb + 1) * 512],
                in0=ctx.psB[:, 0:512], in1=rbt[:, :], op=OP.mult),
                incs=[("V", 1)])
            y_tok = ty["V"]
        pair_rdy = b.now("T")

    sqsl = b.shared("l1ab_sq", [128, 2 * T], BF16)
    arr = b.shared("st_rr", [128, NT], F32)
    anmr = b.shared("st_nmr", [128, NT], F32)
    st = b.shared("pro_st", [128, 16], F32)
    b.wait("vector", "HCs", lnt)
    b.emit("vector", lambda e: e.tensor_scalar(
        out=lnwb[:, 0:8], in0=lnwb[:, 0:8], scalar1=0.2, scalar2=None,
        op0=OP.mult))
    b.drain("vector")
    ctx.wait_consts("tensor")
    b.wait("tensor", "V", max(b.now("V"), y_tok))
    b.wait("tensor", "S", b.now("S"))
    for tt in range(NT):
        for c in range(8):
            b.emit("tensor", lambda e, tt=tt, c=c: e.matmul(
                ctx.psA[:, tt * 8 + c:tt * 8 + c + 1],
                ybuf[:, c * T + tt * 128:c * T + (tt + 1) * 128],
                ctx.ones_col[:, :], start=True, stop=True),
                incs=[("T", 1)] if (tt == NT - 1 and c == 7) else ())
    sq_done = {}
    for c in range(8):
        sl = sqsl[:, (c % 2) * T:((c % 2) + 1) * T]
        b.wait("scalar", "V", y_tok)
        if c >= 2:
            b.wait("scalar", "T", sq_done[c - 2])
        tsq = b.emit("scalar", lambda e, c=c, sl=sl: e.activation(
            out=sl, in_=ybuf[:, c * T:(c + 1) * T], func=AF.Square),
            incs=[("S", 1)])["S"]
        b.wait("tensor", "S", tsq)
        tk = None
        for tt in range(NT):
            tk = b.emit("tensor", lambda e, tt=tt, c=c, sl=sl: e.matmul(
                ctx.psA[:, 512 + tt * 8 + c:512 + tt * 8 + c + 1],
                sl[:, tt * 128:(tt + 1) * 128],
                ctx.ones_col[:, :], start=True, stop=True),
                incs=[("T", 1)] if tt == NT - 1 else ())
        sq_done[c] = tk["T"]
    sums_done = b.now("T")
    b.wait("vector", "T", sums_done)
    pv = None
    for tt in range(NT):
        s1 = st[:, 0:1]
        s2 = st[:, 1:2]
        mean = st[:, 2:3]
        var = st[:, 3:4]
        sd = st[:, 4:5]
        rr = st[:, 5:6]
        b.emit("vector", lambda e, tt=tt, s1=s1: e.tensor_reduce(
            s1, ctx.psA[:, tt * 8:(tt + 1) * 8], AX.X, OP.add))
        b.emit("vector", lambda e, tt=tt, s2=s2: e.tensor_reduce(
            s2, ctx.psA[:, 512 + tt * 8:512 + (tt + 1) * 8], AX.X, OP.add))
        b.drain("vector")
        b.emit("vector", lambda e, s1=s1, mean=mean: e.tensor_scalar(
            out=mean, in0=s1, scalar1=1.0 / D, scalar2=None, op0=OP.mult))
        b.drain("vector")
        b.emit("vector", lambda e, mean=mean, var=var: e.scalar_tensor_tensor(
            out=var, in0=mean, scalar=-1.0, in1=mean, op0=OP.mult, op1=OP.mult))
        b.drain("vector")
        tv_ = b.emit("vector", lambda e, s2=s2, var=var: e.scalar_tensor_tensor(
            out=var, in0=s2, scalar=1.0 / D, in1=var, op0=OP.mult, op1=OP.add),
            incs=[("V", 1)])["V"]
        b.wait("scalar", "V", tv_)
        ts_ = b.emit("scalar", lambda e, var=var, sd=sd: e.activation(
            out=sd, in_=var, func=AF.Sqrt, bias=ctx.const(EPS_LN)),
            incs=[("S", 1)])["S"]
        b.wait("vector", "S", ts_)
        b.emit("vector", lambda e, sd=sd, rr=rr: e.reciprocal(rr, sd))
        b.drain("vector")
        b.emit("vector", lambda e, rr=rr, tt=tt: e.tensor_copy(
            out=arr[:, tt:tt + 1], in_=rr))
        b.drain("vector")
        pv = b.emit("vector", lambda e, mean=mean, rr=rr, tt=tt:
                    e.scalar_tensor_tensor(out=anmr[:, tt:tt + 1], in0=mean,
                                           scalar=-1.0, in1=rr, op0=OP.mult,
                                           op1=OP.mult), incs=[("V", 1)])["V"]
    stats_done = pv
    b.wait("scalar", "T", b.now("T"))
    for c in range(8):
        b.emit("scalar", lambda e, c=c: e.activation(
            out=ybuf[:, c * T:(c + 1) * T], in_=ybuf[:, c * T:(c + 1) * T],
            func=AF.Copy, scale=lnwb[:, c:c + 1]), incs=[("S", 1)])
    yscale_done = b.now("S")
    return resid_ln_proj(b, ctx, tag, ybuf, "L1ab_wo_t", x_in, x_out, lnwb,
                         arr, anmr, bot, yscale_done, stats_done)


def resid_ln_proj(b, ctx, tag, ybuf, wo_name, x_in, x_out, lnwb, arr, anmr,
                  bot, y_rdy, st_rdy):
    wobf, _ = bf16_mirror(b, wo_name, [ND, 128, D])
    wo = b.shared("wshare", [128, ND * 1024], BF16)
    b.wait("gpsimd", "T", b.now("T"))
    b.wait("gpsimd", "d_m", b.now("d_m"))
    wold = b.dma("gpsimd", wo[:, :].rearrange("p (c f) -> p c f", c=ND),
                 wobf.rearrange("c p f -> p c f"), sem="rw")
    b.emit("vector", lambda e: e.tensor_scalar(
        out=lnwb[:, 8:16], in0=lnwb[:, 8:16], scalar1=0.2, scalar2=None,
        op0=OP.mult))
    b.drain("vector")
    lnbf = b.shared("l1ab_lnbf", [128, 16], BF16)
    tlb = b.emit("vector", lambda e: e.tensor_copy(out=lnbf[:, :],
                                                   in_=lnwb[:, :]),
                 incs=[("V", 1)])["V"]
    b.wait("tensor", "rw", wold)
    b.wait("tensor", "V", tlb)
    b.wait("tensor", "S", b.now("S"))
    rowm = {}
    for which, cols in (("w1", slice(0, 8)), ("cv", slice(8, 16))):
        for half in range(2):
            o = (0 if which == "w1" else 2) + half
            tkr = None
            for ci in range(8):
                tkr = b.emit("tensor", lambda e, o=o, ci=ci, half=half,
                             cols=cols: e.matmul(
                    ctx.psA[0:1, o * 512:o * 512 + 512],
                    lnbf[:, cols][:, ci:ci + 1],
                    wo[:, ci * 1024 + half * 512:ci * 1024 + half * 512 + 512],
                    start=(ci == 0), stop=(ci == 7)),
                    incs=[("T", 1)] if ci == 7 else ())
            rowm[(which, half)] = tkr["T"]
    rowsb = b.shared("ac_rsc", [128, 512], F32)[0:1, :]
    w1r = b.shared("l1ab_w1r", [1, 2 * D], BF16)
    b.wait("vector", "T", rowm[("cv", 1)])
    tv1 = b.emit("vector", lambda e: e.tensor_copy(out=w1r[0:1, 0:D],
                                                   in_=ctx.psA[0:1, 0:1024]))
    b.drain("vector")
    tv1 = b.emit("vector", lambda e: e.tensor_copy(out=w1r[0:1, D:2 * D],
                                                   in_=ctx.psA[0:1, 1024:2048]),
                 incs=[("V", 1)])
    w1t = b.shared("rlp_w1t", [128, D], BF16)
    cvt = b.shared("rlp_cvt", [128, D], BF16)
    bcast_row(b, ctx, w1t, w1r[0:1, 0:D], D, row_wait=("V", tv1["V"]))
    bcast_row(b, ctx, cvt, w1r[0:1, D:2 * D], D)
    b.drain("vector")
    tcv = b.emit("vector", lambda e: e.tensor_tensor(
        out=cvt[:, :], in0=cvt[:, :], in1=bot[:, :], op=OP.add),
        incs=[("V", 1)])["V"]
    xold = b.shared("scr8kA", [128, 2 * D], F32)
    xnew = b.shared("scr8kB", [128, 2 * D], F32)
    out_v = {}
    psfree = [None, None]
    out_tok = None
    so_tok = {}
    b.wait("sync", "V", b.now("V"))
    b.wait("sync", "S", b.now("S"))
    for tt in range(NT):
        buf = tt % 2
        if tt >= 2:
            b.wait("sync", "V", out_v[tt - 2])
            b.wait("sync", f"so{tt % 2}", so_tok[tt - 2])
        xo = xold[:, buf * D:(buf + 1) * D]
        xl = b.dma("sync", xo, x_in[tt * 128:(tt + 1) * 128, :],
                   sem=f"xo{tt % 2}")
        tkv = None
        for half in range(2):
            bk = half
            pa = ctx.psB[:, bk * 512:bk * 512 + 512]
            if tt == 0 and half == 0:
                b.wait("tensor", "V", max(b.now("V"), tcv))
                b.wait("tensor", "S", y_rdy)
            if psfree[bk] is not None:
                b.wait("tensor", psfree[bk][0], psfree[bk][1])
            tkmm = None
            for ic in range(ND):
                tkmm = b.emit("tensor", lambda e, pa=pa, ic=ic, tt=tt, half=half:
                              e.matmul(pa,
                                       ybuf[:, ic * T + tt * 128:
                                            ic * T + (tt + 1) * 128],
                                       wo[:, ic * 1024 + half * 512:
                                          ic * 1024 + half * 512 + 512],
                                       start=(ic == 0), stop=(ic == ND - 1)),
                              incs=[("T", 1)] if ic == ND - 1 else ())
            b.wait("vector", "T", tkmm["T"])
            b.wait("vector", f"xo{tt % 2}", xl)
            if tt == 0 and half == 0:
                b.wait("vector", "V", st_rdy)
            xn_sl = xnew[:, buf * D + half * 512: buf * D + half * 512 + 512]
            xo_sl = xo[:, half * 512:(half + 1) * 512]
            b.emit("vector", lambda e, pa=pa, xn_sl=xn_sl, xo_sl=xo_sl, tt=tt:
                   e.scalar_tensor_tensor(out=xn_sl, in0=pa,
                                          scalar=arr[:, tt:tt + 1], in1=xo_sl,
                                          op0=OP.mult, op1=OP.add))
            b.drain("vector")
            b.emit("vector", lambda e, xn_sl=xn_sl, tt=tt, half=half:
                   e.scalar_tensor_tensor(
                out=xn_sl, in0=w1t[:, half * 512:(half + 1) * 512],
                scalar=anmr[:, tt:tt + 1], in1=xn_sl, op0=OP.mult, op1=OP.add))
            b.drain("vector")
            tkv = b.emit("vector", lambda e, xn_sl=xn_sl, half=half:
                         e.tensor_tensor(out=xn_sl, in0=xn_sl,
                                         in1=cvt[:, half * 512:(half + 1) * 512],
                                         op=OP.add), incs=[("V", 1)])
            psfree[bk] = ("V", tkv["V"])
        out_v[tt] = tkv["V"]
        b.wait("sync", "V", tkv["V"])
        out_tok = b.dma("sync", x_out[tt * 128:(tt + 1) * 128, :],
                        xnew[:, buf * D:(buf + 1) * D], sem=f"so{tt % 2}")
        so_tok[tt] = out_tok
    return out_tok


def prep_l1_attn_b(inp, H):
    if "L1ab_wqk_st" in H:
        return
    wq = inp["L1_attn_b_Wq"].astype(np.float32)
    wk = inp["L1_attn_b_Wk"].astype(np.float32)
    H["L1ab_wqk_st"] = _st_tile(np.concatenate([wq, wk], axis=1), ND, 16)
    H["L1ab_wv"] = np.ascontiguousarray(
        inp["L1_attn_b_Wv"].astype(np.float32).reshape(ND, 128, D))
    H["L1ab_wo_t"] = np.ascontiguousarray(
        inp["L1_attn_b_Wo"].astype(np.float32).reshape(ND, 128, D))
    H["L1ab_bq_c"] = _colmajor(inp["L1_attn_b_bq"].astype(np.float32), 8)
    H["L1ab_bk_c"] = _colmajor(inp["L1_attn_b_bk"].astype(np.float32), 8)
    H["L1ab_bv"] = inp["L1_attn_b_bv"].astype(np.float32).reshape(1, D)
    H["L1ab_bo"] = inp["L1_attn_b_bo"].astype(np.float32).reshape(1, D)
    H["L1ab_lnw_c"] = _colmajor(inp["L1_attn_b_ln_w"].astype(np.float32), 8)
    H["L1ab_lnb_c"] = _colmajor(inp["L1_attn_b_ln_b"].astype(np.float32), 8)
    lam = np.stack([inp["L1_attn_b_lambda_q1"], inp["L1_attn_b_lambda_k1"],
                    inp["L1_attn_b_lambda_q2"], inp["L1_attn_b_lambda_k2"]],
                   axis=1).astype(np.float32)
    H["L1ab_lam4"] = lam


SUBS[7] = (emit_l1_attn_b, prep_l1_attn_b)



def route_top2(b, ctx, Lsel, Lw, rt, eqs, oh1, oh2, sel1, sel2, iota8):
    m1 = rt[:, 0:1]
    i1 = rt[:, 1:2]
    b.emit("vector", lambda e: e.tensor_reduce(m1, Lsel, AX.X, OP.max))
    b.drain("vector")
    b.emit("vector", lambda e: e.tensor_scalar(
        out=eqs[:, 0:8], in0=Lsel, scalar1=m1, scalar2=None, op0=OP.is_equal))
    b.drain("vector")
    b.emit("vector", lambda e: e.scalar_tensor_tensor(
        out=eqs[:, 8:16], in0=eqs[:, 0:8], scalar=-999.0,
        in1=iota8[:, :][:, 8:16], op0=OP.mult, op1=OP.add))
    b.drain("vector")
    b.emit("vector", lambda e: e.tensor_reduce(i1, eqs[:, 8:16], AX.X, OP.min))
    b.drain("vector")
    b.emit("vector", lambda e: e.tensor_scalar(
        out=oh1, in0=iota8[:, :][:, 0:8], scalar1=i1, scalar2=None,
        op0=OP.is_equal))
    b.drain("vector")
    b.emit("vector", lambda e: e.tensor_tensor(
        out=eqs[:, 0:8], in0=oh1, in1=Lw, op=OP.mult))
    b.drain("vector")
    b.emit("vector", lambda e: e.tensor_reduce(sel1, eqs[:, 0:8], AX.X, OP.add))
    b.emit("vector", lambda e: e.scalar_tensor_tensor(
        out=eqs[:, 8:16], in0=oh1, scalar=-1e9, in1=Lsel, op0=OP.mult,
        op1=OP.add))
    b.drain("vector")
    b.emit("vector", lambda e: e.tensor_reduce(m1, eqs[:, 8:16], AX.X, OP.max))
    b.drain("vector")
    b.emit("vector", lambda e: e.tensor_scalar(
        out=eqs[:, 0:8], in0=eqs[:, 8:16], scalar1=m1, scalar2=None,
        op0=OP.is_equal))
    b.drain("vector")
    b.emit("vector", lambda e: e.scalar_tensor_tensor(
        out=eqs[:, 8:16], in0=eqs[:, 0:8], scalar=-999.0,
        in1=iota8[:, :][:, 8:16], op0=OP.mult, op1=OP.add))
    b.drain("vector")
    b.emit("vector", lambda e: e.tensor_reduce(i1, eqs[:, 8:16], AX.X, OP.min))
    b.drain("vector")
    b.emit("vector", lambda e: e.tensor_scalar(
        out=oh2, in0=iota8[:, :][:, 0:8], scalar1=i1, scalar2=None,
        op0=OP.is_equal))
    b.drain("vector")
    b.emit("vector", lambda e: e.tensor_tensor(
        out=eqs[:, 0:8], in0=oh2, in1=Lw, op=OP.mult))
    b.drain("vector")
    b.emit("vector", lambda e: e.tensor_reduce(sel2, eqs[:, 0:8], AX.X, OP.add))
    b.drain("vector")


L3B_NC = None
L3B_NH = None


def emit_l3_attn_b(b, ctx, tag, x_in, x_out):
    C = 128
    NC = L3B_NC or (T // C)
    xnT = b.shared("xnT", [128, ND * T], BF16)
    wqkvbf, _ = bf16_mirror(b, "L3b_wqkv_st", [3, 8, 128, ND * 128])
    wgbf, _ = bf16_mirror(b, "L3b_wg_rows", [ND, 128, D])
    wobf, _ = bf16_mirror(b, "L3b_wo_rows", [ND, 128, D])
    wqkv = b.sb(f"{tag}_wqkv", [128, 24 * 1024], BF16)
    wgr = b.sb(f"{tag}_wgr", [128, ND * 1024], BF16)
    wot = b.shared("wshare", [128, ND * 1024], BF16)
    b.wait("gpsimd", "T", b.now("T"))
    b.wait("gpsimd", "d_m", b.now("d_m"))
    wqld = b.dma("gpsimd", wqkv[:, :].rearrange("p (a f) -> p a f", a=24),
                 wqkvbf.rearrange("q c p f -> p (q c) f"), sem="rw")
    wgld = b.dma("gpsimd", wgr[:, :].rearrange("p (c f) -> p c f", c=ND),
                 wgbf.rearrange("c p f -> p c f"), sem="rw")
    wold = b.dma("gpsimd", wot[:, :].rearrange("p (c f) -> p c f", c=ND),
                 wobf.rearrange("c p f -> p c f"), sem="rw")
    wab = b.sb(f"{tag}_wab", [128, ND * 16], BF16)
    b.dma("sync", wab[:, :].rearrange("p (c f) -> p c f", c=ND),
          b.inp("L3b_wab", [ND, 128, 16], BF16)[:, :, :].rearrange("c p f -> p c f"),
          sem="d_cs")
    cvw = b.sb(f"{tag}_cvw", [128, 96], F32)
    b.dma("sync", cvw[:, :], b.inp("L3b_convw", [128, 96])[:, :], sem="d_cs")
    cvb = b.sb(f"{tag}_cvb", [128, 24], F32)
    b.dma("sync", cvb[:, :], b.inp("L3b_convb", [128, 24])[:, :], sem="d_cs")
    abb = b.sb(f"{tag}_abb", [1, 16], F32)
    b.dma("sync", abb[:, :], b.inp("L3b_abb", [1, 16])[:, :], sem="d_cs")
    lngr = b.sb(f"{tag}_lngr", [1, 256], F32)
    b.dma("sync", lngr[:, :], b.inp("L3b_lngb", [1, 256])[:, :], sem="d_cs")
    ctk = b.handoff("sync", "d_cs", "HCs")
    idf, idtok = ctx.lazy("c_idf", [128, 128], F32)
    triu, trtok = ctx.lazy("c_triu", [128, 128], F32)
    trln, tlntok = ctx.lazy("c_trilsneg", [128, 128], F32)

    lam = b.sb(f"{tag}_lam", [128, NT * 8], F32)
    bet = b.sb(f"{tag}_bet", [128, NT * 8], F32)
    St = b.sb(f"{tag}_St", [128, 8 * 128], F32)
    scr = b.sb(f"{tag}_scr", [128, 8 * 128], F32)
    Lc8 = b.sb(f"{tag}_L", [128, 16], F32)
    Lrow = b.sb(f"{tag}_Lr", [1, 128], F32)
    pre = b.sb(f"{tag}_pre", [128, 131], F32)
    qkv3 = b.sb(f"{tag}_q3", [128, 3 * 1024], F32)
    on = b.sb(f"{tag}_on", [128, 1024], F32)
    og = b.sb(f"{tag}_og", [128, 1024], BF16)
    gtok = b.sb(f"{tag}_g", [128, 1024], BF16)
    ogT = b.sb(f"{tag}_ogT", [128, 1024], BF16)
    abt = b.sb(f"{tag}_abt", [128, 16], F32)
    lngT = b.sb(f"{tag}_lngT", [128, 256], F32)
    sqs = b.sb(f"{tag}_sqs", [128, 136], F32)
    xob = b.shared("scr8kA", [128, 2 * D], F32)
    xnb = b.shared("scr8kB", [128, 2 * D], F32)
    T1 = ctx.psB[:, 0:512]
    T2 = ctx.psB[:, 512:1024]
    T3 = ctx.psA[:, 1024:1536]
    T4 = ctx.psA[:, 1536:2048]

    pro = prologue(b, ctx, tag, x_in, 0, T, xnT, T, 0, ctx.psT,
                   pre_tok=("T", b.now("T")))
    b.wait("tensor", "HCs", ctk)
    b.wait("tensor", "V", b.now("V"))
    b.wait("tensor", "S", b.now("S"))
    for i in range(2):
        tk = b.emit("tensor", lambda e, i=i: e.matmul(
            T2[:, 0:128], ctx.ones_t[0:1, 0:128], lngr[0:1, i * 128:(i + 1) * 128],
            start=True, stop=True), incs=[("T", 1)])
        b.wait("vector", "T", tk["T"])
        b.emit("vector", lambda e, i=i: e.tensor_copy(
            out=lngT[:, i * 128:(i + 1) * 128], in_=T2[:, 0:128]),
            incs=[("V", 1)])
        b.wait("tensor", "V", b.now("V"))
    tk = b.emit("tensor", lambda e: e.matmul(
        T2[:, 0:16], ctx.ones_t[0:1, 0:128], abb[0:1, :],
        start=True, stop=True), incs=[("T", 1)])
    b.wait("vector", "T", tk["T"])
    bct = b.emit("vector", lambda e: e.tensor_copy(
        out=abt[:, :], in_=T2[:, 0:16]), incs=[("V", 1)])["V"]
    b.wait("tensor", "V", pro)
    prev_ab = None
    for tt in range(NT):
        if prev_ab is not None:
            b.wait("tensor", "S", prev_ab)
        tk = None
        for dc in range(ND):
            tk = b.emit("tensor", lambda e, dc=dc, tt=tt: e.matmul(
                T1[:, 0:16], xnT[:, dc * T + tt * 128:dc * T + (tt + 1) * 128],
                wab[:, dc * 16:(dc + 1) * 16], start=(dc == 0),
                stop=(dc == ND - 1)), incs=[("T", 1)] if dc == ND - 1 else ())
        b.wait("vector", "T", tk["T"])
        if tt == 0:
            b.wait("vector", "V", bct)
        sv = b.emit("vector", lambda e: e.tensor_tensor(
            out=abt[:, :], in0=T1[:, 0:16], in1=abt[:, :], op=OP.add),
            incs=[("V", 1)])["V"]
        b.drain("vector")
        b.wait("scalar", "V", sv)
        b.emit("scalar", lambda e, tt=tt: e.activation(
            out=lam[:, tt * 8:(tt + 1) * 8], in_=abt[:, 0:8],
            func=AF.Sigmoid), incs=[("S", 1)])
        b.drain("scalar")
        b.emit("scalar", lambda e, tt=tt: e.activation(
            out=lam[:, tt * 8:(tt + 1) * 8], in_=lam[:, tt * 8:(tt + 1) * 8],
            func=AF.Ln), incs=[("S", 1)])
        b.drain("scalar")
        prev_ab = b.emit("scalar", lambda e, tt=tt: e.activation(
            out=bet[:, tt * 8:(tt + 1) * 8], in_=abt[:, 8:16], func=AF.Sigmoid),
            incs=[("S", 1)])["S"]
        b.wait("vector", "S", prev_ab)
        bct = b.emit("vector", lambda e: e.tensor_copy(
            out=abt[:, :], in_=T2[:, 0:16]), incs=[("V", 1)])["V"]
        b.wait("tensor", "V", bct)
    ab_done_s = b.now("S")
    ab_done_v = b.now("V")

    b.wait("vector", "V", b.now("V"))
    st0 = b.emit("vector", lambda e: e.memset(St[:, :], 0.0), incs=[("V", 1)])["V"]

    P_, PT_, U_, G_, E_, W_, KD_, O_ = [scr[:, i * 128:(i + 1) * 128]
                                        for i in range(8)]
    out_v = {}
    so_tok = {}
    out_tok = None
    for c in range(NC):
        t0 = c * C
        b.wait("tensor", "V", b.now("V"))
        b.wait("tensor", "S", b.now("S"))
        if c == 0:
            b.wait("tensor", "rw", wold)
            b.wait("tensor", idtok[0], idtok[1])
            b.wait("tensor", trtok[0], trtok[1])
            b.wait("vector", trtok[0], trtok[1])
            b.wait("vector", tlntok[0], tlntok[1])
        for proj in range(3):
            for ft in range(8):
                lo = max(0, t0 - 3)
                wcols = t0 + 128 - lo
                if getattr(b, "l3b_ppv", None) is not None:
                    b.wait("tensor", "V", b.l3b_ppv)
                tk = None
                for dc in range(ND):
                    tk = b.emit("tensor", lambda e, proj=proj, ft=ft, dc=dc,
                                lo=lo, wcols=wcols: e.matmul(
                        T1[:, 0:wcols],
                        wqkv[:, (proj * 8 + ft) * 1024 + dc * 128:
                             (proj * 8 + ft) * 1024 + (dc + 1) * 128],
                        xnT[:, dc * T + lo:dc * T + lo + wcols],
                        start=(dc == 0), stop=(dc == ND - 1)),
                        incs=[("T", 1)] if dc == ND - 1 else ())
                b.wait("vector", "T", tk["T"])
                if hasattr(b, "l3b_prf"):
                    b.wait("vector", "S", b.l3b_prf)
                if c == 0:
                    b.emit("vector", lambda e: e.memset(pre[:, 0:3], 0.0))
                    b.drain("vector")
                pv = b.emit("vector", lambda e, wcols=wcols: e.tensor_copy(
                    out=pre[:, 131 - wcols:131], in_=T1[:, 0:wcols]),
                    incs=[("V", 1)])["V"]
                b.l3b_ppv = pv
                b.drain("vector")
                zb = on[:, 0:128]
                b.emit("vector", lambda e, proj=proj, ft=ft: e.tensor_scalar(
                    out=zb, in0=pre[:, 0:128],
                    scalar1=cvw[:, proj * 32 + ft * 4:proj * 32 + ft * 4 + 1],
                    scalar2=None, op0=OP.mult))
                b.drain("vector")
                for tau in range(1, 4):
                    b.emit("vector", lambda e, proj=proj, ft=ft, tau=tau:
                           e.scalar_tensor_tensor(
                        out=zb, in0=pre[:, tau:tau + 128],
                        scalar=cvw[:, proj * 32 + ft * 4 + tau:
                                   proj * 32 + ft * 4 + tau + 1],
                        in1=zb, op0=OP.mult, op1=OP.add))
                    b.drain("vector")
                zv = b.emit("vector", lambda e: e.nop(), incs=[("V", 1)])["V"]
                b.wait("scalar", "V", zv)
                dstq = qkv3[:, proj * 1024 + ft * 128:proj * 1024 + (ft + 1) * 128]
                sk = b.emit("scalar", lambda e, proj=proj, ft=ft, dstq=dstq:
                            e.activation(out=dstq, in_=zb, func=AF.Silu,
                                         bias=cvb[:, proj * 8 + ft:
                                                  proj * 8 + ft + 1]),
                            incs=[("S", 1)])["S"]
                b.l3b_prf = sk
                if proj == 1:
                    b.wait("vector", "S", sk)
                    b.emit("vector", lambda e, dstq=dstq: e.tensor_scalar(
                        out=dstq, in0=dstq, scalar1=ctx.const(
                            float(np.float32(1.0 / np.sqrt(np.float32(128.0))))),
                        scalar2=None, op0=OP.mult), incs=[("V", 1)])
                    b.drain("vector")
        qkv_s = b.now("S")
        qkv_v = b.now("V")
        b.wait("tensor", "S", qkv_s)
        b.wait("tensor", "V", qkv_v)
        tk = b.emit("tensor", lambda e, c=c: e.matmul(
            T2[:, 0:8], triu[:, :], lam[:, c * 8:(c + 1) * 8],
            start=True, stop=True), incs=[("T", 1)])
        b.wait("vector", "T", tk["T"])
        b.emit("vector", lambda e: e.tensor_copy(
            out=Lc8[:, 0:8], in_=T2[:, 0:8]))
        b.drain("vector")
        lv = b.emit("vector", lambda e, c=c: e.tensor_tensor(
            out=Lc8[:, 8:16], in0=Lc8[:, 0:8], in1=lam[:, c * 8:(c + 1) * 8],
            op=OP.subtract), incs=[("V", 1)])["V"]
        b.wait("tensor", "V", lv)
        for h in range(L3B_NH or 8):
            qT = qkv3[:, h * 128:(h + 1) * 128]
            kT = qkv3[:, 1024 + h * 128:1024 + (h + 1) * 128]
            vT = qkv3[:, 2048 + h * 128:2048 + (h + 1) * 128]
            Sh = St[:, h * 128:(h + 1) * 128]
            Lcol = Lc8[:, h:h + 1]
            Lm1c = Lc8[:, 8 + h:8 + h + 1]
            bcol = bet[:, c * 8 + h:c * 8 + h + 1]
            tk = b.emit("tensor", lambda e, Lcol=Lcol: e.transpose(
                out=T2[0:1, 0:128], in_=Lcol, identity=idf[:, :]),
                incs=[("T", 1)])
            b.wait("vector", "T", tk["T"])
            lrv = b.emit("vector", lambda e: e.tensor_copy(
                out=Lrow[:, :], in_=T2[0:1, 0:128]), incs=[("V", 1)])["V"]
            b.wait("tensor", "V", lrv)
            if getattr(b, "l3b_ppv", None) is not None:
                b.wait("tensor", "V", b.l3b_ppv)
                b.l3b_ppv = None
            tk = b.emit("tensor", lambda e: e.matmul(
                T2[:, 0:128], ctx.ones_t[0:1, 0:128], Lrow[0:1, :],
                start=True, stop=True), incs=[("T", 1)])
            tkg = b.emit("tensor", lambda e, kT=kT: e.matmul(
                T1[:, 0:128], kT, kT, start=True, stop=True), incs=[("T", 1)])
            b.wait("vector", "T", tk["T"])
            b.emit("vector", lambda e, Lm1c=Lm1c: e.tensor_scalar(
                out=E_, in0=T2[:, 0:128], scalar1=-1.0, scalar2=Lm1c,
                op0=OP.mult, op1=OP.add))
            b.drain("vector")
            ev = b.emit("vector", lambda e: e.tensor_scalar(
                out=E_, in0=E_, scalar1=0.0, scalar2=None, op0=OP.min),
                incs=[("V", 1)])["V"]
            b.wait("scalar", "V", ev)
            b.wait("scalar", "S", b.now("S"))
            es = b.emit("scalar", lambda e: e.activation(
                out=E_, in_=E_, func=AF.Exp), incs=[("S", 1)])["S"]
            b.wait("vector", "S", es)
            b.wait("vector", "T", tkg["T"])
            b.emit("vector", lambda e: e.tensor_tensor(
                out=P_, in0=E_, in1=T1[:, 0:128], op=OP.mult))
            b.drain("vector")
            b.emit("vector", lambda e, bcol=bcol: e.tensor_scalar(
                out=P_, in0=P_, scalar1=bcol, scalar2=None, op0=OP.mult))
            b.drain("vector")
            pv = b.emit("vector", lambda e: e.tensor_tensor(
                out=P_, in0=P_, in1=trln[:, :], op=OP.mult), incs=[("V", 1)])["V"]
            b.wait("tensor", "V", pv)
            tkw = b.emit("tensor", lambda e, kT=kT, Sh=Sh: e.matmul(
                T1[:, 0:128], kT, Sh, start=True, stop=True), incs=[("T", 1)])
            tkv_ = b.emit("tensor", lambda e, vT=vT: e.transpose(
                out=T2[:, 0:128], in_=vT, identity=idf[:, :]), incs=[("T", 1)])
            b.wait("scalar", "V", b.now("V"))
            e2 = b.emit("scalar", lambda e, Lm1c=Lm1c, h=h: e.activation(
                out=Lc8[:, 8 + h:8 + h + 1], in_=Lm1c, func=AF.Exp),
                incs=[("S", 1)])["S"]
            b.wait("vector", "S", e2)
            b.emit("vector", lambda e, bcol=bcol, h=h: e.tensor_tensor(
                out=Lc8[:, 8 + h:8 + h + 1], in0=Lc8[:, 8 + h:8 + h + 1],
                in1=bcol, op=OP.mult))
            b.drain("vector")
            b.emit("vector", lambda e, h=h: e.tensor_scalar(
                out=Lc8[:, 8 + h:8 + h + 1], in0=Lc8[:, 8 + h:8 + h + 1],
                scalar1=-1.0, scalar2=None, op0=OP.mult))
            b.drain("vector")
            b.wait("vector", "T", tkv_["T"])
            b.emit("vector", lambda e, bcol=bcol: e.tensor_scalar(
                out=U_, in0=T2[:, 0:128], scalar1=bcol, scalar2=None,
                op0=OP.mult))
            b.drain("vector")
            b.wait("vector", "T", tkw["T"])
            uv = b.emit("vector", lambda e, h=h: e.scalar_tensor_tensor(
                out=U_, in0=T1[:, 0:128], scalar=Lc8[:, 8 + h:8 + h + 1],
                in1=U_, op0=OP.mult, op1=OP.add), incs=[("V", 1)])["V"]
            b.wait("tensor", "V", uv)
            for mit in range(7):
                tk = b.emit("tensor", lambda e: e.transpose(
                    out=T2[:, 0:128], in_=P_, identity=idf[:, :]),
                    incs=[("T", 1)])
                b.wait("vector", "T", tk["T"])
                ptv = b.emit("vector", lambda e: e.tensor_copy(
                    out=PT_, in_=T2[:, 0:128]), incs=[("V", 1)])["V"]
                b.wait("tensor", "V", ptv)
                b.emit("tensor", lambda e: e.matmul(
                    T1[:, 0:128], PT_, U_, start=True, stop=False))
                tku = b.emit("tensor", lambda e: e.matmul(
                    T1[:, 0:128], idf[:, :], U_, start=False, stop=True),
                    incs=[("T", 1)])
                b.wait("vector", "T", tku["T"])
                un = b.emit("vector", lambda e: e.tensor_copy(
                    out=U_, in_=T1[:, 0:128]), incs=[("V", 1)])["V"]
                if mit < 6:
                    b.wait("tensor", "V", un)
                    tkp = b.emit("tensor", lambda e: e.matmul(
                        T2[:, 0:128], PT_, P_, start=True, stop=True),
                        incs=[("T", 1)])
                    b.wait("vector", "T", tkp["T"])
                    pn = b.emit("vector", lambda e: e.tensor_copy(
                        out=P_, in_=T2[:, 0:128]), incs=[("V", 1)])["V"]
                    b.wait("tensor", "V", pn)
                else:
                    b.wait("tensor", "V", un)
            tk = b.emit("tensor", lambda e: e.matmul(
                T2[:, 0:128], ctx.ones_t[0:1, 0:128], Lrow[0:1, :],
                start=True, stop=True), incs=[("T", 1)])
            tkqk = b.emit("tensor", lambda e, kT=kT, qT=qT: e.matmul(
                T1[:, 0:128], kT, qT, start=True, stop=True), incs=[("T", 1)])
            b.wait("vector", "T", tk["T"])
            b.emit("vector", lambda e, Lcol=Lcol: e.tensor_scalar(
                out=O_[:, 0:1], in0=Lcol, scalar1=-1.0, scalar2=None,
                op0=OP.mult))
            b.drain("vector")
            b.emit("vector", lambda e: e.tensor_scalar(
                out=E_, in0=T2[:, 0:128], scalar1=1.0, scalar2=O_[:, 0:1],
                op0=OP.mult, op1=OP.add))
            b.drain("vector")
            ev = b.emit("vector", lambda e: e.tensor_scalar(
                out=E_, in0=E_, scalar1=0.0, scalar2=None, op0=OP.min),
                incs=[("V", 1)])["V"]
            b.wait("scalar", "V", ev)
            es = b.emit("scalar", lambda e: e.activation(
                out=E_, in_=E_, func=AF.Exp), incs=[("S", 1)])["S"]
            b.wait("vector", "S", es)
            b.emit("vector", lambda e: e.tensor_tensor(
                out=E_, in0=E_, in1=triu[:, :], op=OP.mult))
            b.drain("vector")
            b.wait("vector", "T", tkqk["T"])
            qkv_m = b.emit("vector", lambda e: e.tensor_tensor(
                out=W_, in0=E_, in1=T1[:, 0:128], op=OP.mult),
                incs=[("V", 1)])["V"]
            b.wait("tensor", "V", qkv_m)
            tko1 = b.emit("tensor", lambda e, qT=qT, Sh=Sh: e.matmul(
                T3[:, 0:128], qT, Sh, start=True, stop=True), incs=[("T", 1)])
            tko2 = b.emit("tensor", lambda e: e.matmul(
                T4[:, 0:128], W_, U_, start=True, stop=True), incs=[("T", 1)])
            b.wait("scalar", "V", b.now("V"))
            el = b.emit("scalar", lambda e, Lcol=Lcol, h=h: e.activation(
                out=Lc8[:, 8 + h:8 + h + 1], in_=Lcol, func=AF.Exp),
                incs=[("S", 1)])["S"]
            b.wait("vector", "S", el)
            b.wait("vector", "T", tko2["T"])
            onh = on[:, h * 128:(h + 1) * 128]
            b.emit("vector", lambda e, onh=onh, h=h: e.tensor_scalar(
                out=onh, in0=T3[:, 0:128], scalar1=Lc8[:, 8 + h:8 + h + 1],
                scalar2=None, op0=OP.mult))
            b.drain("vector")
            ov = b.emit("vector", lambda e, onh=onh: e.tensor_tensor(
                out=onh, in0=onh, in1=T4[:, 0:128], op=OP.add),
                incs=[("V", 1)])["V"]
            b.wait("tensor", "V", ov)
            tk = b.emit("tensor", lambda e: e.matmul(
                T2[:, 0:1], ctx.ones_t[0:1, 0:128], Lrow[0:1, 127:128],
                start=True, stop=True), incs=[("T", 1)])
            tkk = b.emit("tensor", lambda e, kT=kT: e.transpose(
                out=T3[:, 0:128], in_=kT, identity=idf[:, :]), incs=[("T", 1)])
            b.wait("vector", "T", tk["T"])
            b.emit("vector", lambda e, h=h: e.tensor_copy(
                out=Lc8[:, 8 + h:8 + h + 1], in_=T2[:, 0:1]))
            b.drain("vector")
            b.emit("vector", lambda e, Lcol=Lcol, h=h: e.tensor_tensor(
                out=E_[:, 0:1], in0=Lc8[:, 8 + h:8 + h + 1], in1=Lcol,
                op=OP.subtract))
            b.drain("vector")
            wv = b.emit("vector", lambda e: e.nop(), incs=[("V", 1)])["V"]
            b.wait("scalar", "V", wv)
            ew1 = b.emit("scalar", lambda e: e.activation(
                out=E_[:, 0:1], in_=E_[:, 0:1], func=AF.Exp), incs=[("S", 1)])
            ew2 = b.emit("scalar", lambda e, h=h: e.activation(
                out=Lc8[:, 8 + h:8 + h + 1], in_=Lc8[:, 8 + h:8 + h + 1],
                func=AF.Exp), incs=[("S", 1)])
            b.wait("vector", "S", ew2["S"])
            b.wait("vector", "T", tkk["T"])
            kdv = b.emit("vector", lambda e: e.tensor_scalar(
                out=KD_, in0=T3[:, 0:128], scalar1=E_[:, 0:1], scalar2=None,
                op0=OP.mult), incs=[("V", 1)])["V"]
            b.wait("tensor", "V", kdv)
            tks = b.emit("tensor", lambda e: e.matmul(
                T4[:, 0:128], KD_, U_, start=True, stop=True), incs=[("T", 1)])
            b.wait("vector", "T", tks["T"])
            stv = b.emit("vector", lambda e, Sh=Sh, h=h: e.scalar_tensor_tensor(
                out=Sh, in0=Sh, scalar=Lc8[:, 8 + h:8 + h + 1],
                in1=T4[:, 0:128], op0=OP.mult, op1=OP.add),
                incs=[("V", 1)])["V"]
            b.wait("tensor", "V", stv)
        heads_v = b.now("V")
        heads_s = b.now("S")
        b.wait("tensor", "V", heads_v)
        b.wait("tensor", "S", heads_s)
        tkh = {}
        for half in range(2):
            pa = ctx.psA[:, half * 512:half * 512 + 512]
            tk = None
            for dc in range(ND):
                tk = b.emit("tensor", lambda e, pa=pa, dc=dc, half=half, t0=t0:
                            e.matmul(
                    pa, xnT[:, dc * T + t0:dc * T + t0 + 128],
                    wgr[:, dc * 1024 + half * 512:dc * 1024 + half * 512 + 512],
                    start=(dc == 0), stop=(dc == ND - 1)),
                    incs=[("T", 1)] if dc == ND - 1 else ())
            tkh[half] = tk["T"]
        b.wait("scalar", "T", tkh[1])
        b.wait("scalar", "V", heads_v)
        gs = b.emit("scalar", lambda e: e.activation(
            out=gtok[:, :], in_=ctx.psA[:, 0:1024], func=AF.Sigmoid),
            incs=[("S", 1)])["S"]
        b.wait("vector", "S", gs)
        zm = b.emit("vector", lambda e: e.memset(sqs[:, 128:136], 0.0),
                    incs=[("V", 1)])["V"]
        b.drain("vector")
        b.wait("scalar", "V", zm)
        ln_done = None
        for h in range(L3B_NH or 8):
            onh = on[:, h * 128:(h + 1) * 128]
            s1 = Lc8[:, 8 + h:8 + h + 1]
            b.emit("vector", lambda e, onh=onh, s1=s1: e.tensor_reduce(
                s1, onh, AX.X, OP.add))
            b.drain("vector")
            b.emit("vector", lambda e, s1=s1: e.tensor_scalar(
                out=s1, in0=s1, scalar1=1.0 / 128.0, scalar2=None,
                op0=OP.mult), incs=[("V", 1)])
            b.drain("vector")
            sq = b.emit("scalar", lambda e, onh=onh, h=h: e.activation(
                out=sqs[:, 0:128], in_=onh, func=AF.Square,
                accum_out=sqs[:, 128 + h:129 + h]),
                incs=[("S", 1)])["S"]
            b.wait("vector", "S", sq)
            var = Lc8[:, h:h + 1]
            b.emit("vector", lambda e, var=var, s1=s1: e.scalar_tensor_tensor(
                out=var, in0=s1, scalar=-1.0, in1=s1, op0=OP.mult,
                op1=OP.mult))
            b.drain("vector")
            vv2 = b.emit("vector", lambda e, var=var, h=h:
                         e.scalar_tensor_tensor(
                out=var, in0=sqs[:, 128 + h:129 + h],
                scalar=1.0 / 128.0, in1=var, op0=OP.mult, op1=OP.add),
                incs=[("V", 1)])["V"]
            b.wait("scalar", "V", vv2)
            sd = b.emit("scalar", lambda e, var=var: e.activation(
                out=var, in_=var, func=AF.Sqrt, bias=ctx.const(1e-5)),
                incs=[("S", 1)])["S"]
            b.wait("vector", "S", sd)
            b.emit("vector", lambda e, var=var: e.reciprocal(var, var))
            b.drain("vector")
            b.emit("vector", lambda e, s1=s1, var=var: e.scalar_tensor_tensor(
                out=s1, in0=s1, scalar=-1.0, in1=var, op0=OP.mult,
                op1=OP.mult))
            b.drain("vector")
            b.emit("vector", lambda e, onh=onh, var=var, s1=s1: e.tensor_scalar(
                out=onh, in0=onh, scalar1=var, scalar2=s1, op0=OP.mult,
                op1=OP.add))
            b.drain("vector")
            b.emit("vector", lambda e, onh=onh: e.tensor_tensor(
                out=onh, in0=onh, in1=lngT[:, 0:128], op=OP.mult))
            b.drain("vector")
            b.emit("vector", lambda e, onh=onh: e.tensor_tensor(
                out=onh, in0=onh, in1=lngT[:, 128:256], op=OP.add))
            b.drain("vector")
            ln_done = b.emit("vector", lambda e, onh=onh, h=h: e.tensor_tensor(
                out=og[:, h * 128:(h + 1) * 128], in0=onh,
                in1=gtok[:, h * 128:(h + 1) * 128], op=OP.mult),
                incs=[("V", 1)])["V"]
        b.wait("tensor", "V", ln_done)
        ttk = None
        for h in range(8):
            ttk = b.emit("tensor", lambda e, h=h, c=c: e.transpose(
                out=ctx.psT[:, (c % 2) * 1024 + h * 128:
                            (c % 2) * 1024 + (h + 1) * 128],
                in_=og[:, h * 128:(h + 1) * 128],
                identity=ctx.identity[:, :]),
                incs=[("T", 1)] if h == 7 else ())
        b.wait("vector", "T", ttk["T"])
        otv = b.emit("vector", lambda e, c=c: e.tensor_copy(
            out=ogT[:, :], in_=ctx.psT[:, (c % 2) * 1024:(c % 2) * 1024 + 1024]),
            incs=[("V", 1)])["V"]
        tgl = c
        buf = c % 2
        xo = xob[:, buf * D:(buf + 1) * D]
        xn2 = xnb[:, buf * D:(buf + 1) * D]
        if tgl >= 2:
            b.wait("sync", "V", out_v[tgl - 2])
            b.wait("sync", f"so{tgl % 2}", so_tok[tgl - 2])
        b.wait("sync", "V", b.now("V"))
        xl = b.dma("sync", xo, x_in[t0:t0 + 128, :], sem=f"xo{c % 2}")
        b.wait("tensor", "V", otv)
        tkf = {}
        for half in range(2):
            pa = ctx.psA[:, half * 512:half * 512 + 512]
            tk = None
            for dc in range(ND):
                tk = b.emit("tensor", lambda e, pa=pa, dc=dc, half=half:
                            e.matmul(
                    pa, ogT[:, dc * 128:(dc + 1) * 128],
                    wot[:, dc * 1024 + half * 512:dc * 1024 + half * 512 + 512],
                    start=(dc == 0), stop=(dc == ND - 1)),
                    incs=[("T", 1)] if dc == ND - 1 else ())
            tkf[half] = tk["T"]
        tv = None
        for half in range(2):
            b.wait("vector", "T", tkf[1])
            b.wait("vector", f"xo{c % 2}", xl)
            xnh = xn2[:, half * 512:(half + 1) * 512]
            tv = b.emit("vector", lambda e, xnh=xnh, xo=xo, half=half:
                        e.tensor_tensor(
                out=xnh, in0=ctx.psA[:, half * 512:half * 512 + 512],
                in1=xo[:, half * 512:(half + 1) * 512], op=OP.add),
                incs=[("V", 1)])
        out_v[tgl] = tv["V"]
        b.wait("sync", "V", tv["V"])
        out_tok = b.dma("sync", x_out[t0:t0 + 128, :],
                        xnb[:, buf * D:(buf + 1) * D], sem=f"so{tgl % 2}")
        so_tok[tgl] = out_tok
    return out_tok


def prep_l3_attn_b(inp, H):
    if "L3b_wqkv_st" in H:
        return
    wq = inp["L3_attn_b_Wq"].astype(np.float32)
    wk = inp["L3_attn_b_Wk"].astype(np.float32)
    wv = inp["L3_attn_b_Wv"].astype(np.float32)
    H["L3b_wqkv_st"] = np.ascontiguousarray(np.stack(
        [_st_tile(np.ascontiguousarray(w), ND, 8) for w in (wq, wk, wv)]))
    H["L3b_wg_rows"] = np.ascontiguousarray(
        inp["L3_attn_b_Wg"].astype(np.float32).reshape(ND, 128, D))
    H["L3b_wo_rows"] = np.ascontiguousarray(
        inp["L3_attn_b_Wo"].astype(np.float32).reshape(ND, 128, D))
    import ml_dtypes
    H["L3b_wab"] = np.ascontiguousarray(np.concatenate(
        [inp["L3_attn_b_Wa"].astype(np.float32),
         inp["L3_attn_b_Wb"].astype(np.float32)],
        axis=1).reshape(ND, 128, 16).astype(ml_dtypes.bfloat16))
    H["L3b_abb"] = np.ascontiguousarray(np.concatenate(
        [inp["L3_attn_b_ba"].astype(np.float32),
         inp["L3_attn_b_bb"].astype(np.float32)]).reshape(1, 16))
    cw = []
    cb = []
    for p in ("q", "k", "v"):
        w = inp[f"L3_attn_b_conv_{p}_w"].astype(np.float32)
        cw.append(w[:, 0, :].reshape(8, 128, 4).transpose(1, 0, 2).reshape(128, 32))
        cb.append(inp[f"L3_attn_b_conv_{p}_b"].astype(np.float32)
                  .reshape(8, 128).T)
    H["L3b_convw"] = np.ascontiguousarray(np.concatenate(cw, axis=1))
    H["L3b_convb"] = np.ascontiguousarray(np.concatenate(cb, axis=1))
    H["L3b_lngb"] = np.ascontiguousarray(np.concatenate(
        [inp["L3_attn_b_ln_g"].astype(np.float32),
         inp["L3_attn_b_ln_b"].astype(np.float32)]).reshape(1, 256))
    H["c_idf"] = np.eye(128, dtype=np.float32)
    tri = (np.arange(128)[:, None] <= np.arange(128)[None, :])
    H["c_triu"] = tri.astype(np.float32)
    H["c_trilsneg"] = -(np.arange(128)[:, None]
                        > np.arange(128)[None, :]).astype(np.float32)


SUBS[15] = (emit_l3_attn_b, prep_l3_attn_b)


def l0f_route_tile(b, ctx, tt, xnT, wg, lgs, rt, eqs, oh, iota8, prev_v):
    bk = tt % 2
    pa = ctx.psB[:, bk * 512:bk * 512 + 8]
    if prev_v is not None:
        b.wait("tensor", "V", prev_v)
    tk = None
    for dc in range(ND):
        tk = b.emit("tensor", lambda e, pa=pa, dc=dc, tt=tt: e.matmul(
            pa, xnT[:, dc * T + tt * 128:dc * T + (tt + 1) * 128],
            wg[:, dc * 8:(dc + 1) * 8], start=(dc == 0), stop=(dc == ND - 1)),
            incs=[("T", 1)] if dc == ND - 1 else ())
    b.wait("vector", "T", tk["T"])
    lg = lgs[:, 0:8]
    cp = b.emit("vector", lambda e, pa=pa, lg=lg: e.tensor_copy(
        out=lg, in_=pa), incs=[("V", 1)])["V"]
    b.drain("vector")
    route_top2(b, ctx, lg, lg, rt, eqs, oh[:, 0:8], oh[:, 8:16],
               rt[:, 2:3], rt[:, 3:4], iota8)
    b.emit("vector", lambda e: e.tensor_tensor(
        out=rt[:, 6:7], in0=rt[:, 2:3], in1=rt[:, 3:4], op=OP.subtract))
    b.drain("vector")
    tvv = b.emit("vector", lambda e: e.nop(), incs=[("V", 1)])
    b.wait("scalar", "V", tvv["V"])
    ts1 = b.emit("scalar", lambda e: e.activation(
        out=rt[:, 4:5], in_=rt[:, 6:7], func=AF.Sigmoid), incs=[("S", 1)])
    b.wait("vector", "S", ts1["S"])
    vv = b.emit("vector", lambda e: e.tensor_scalar(
        out=rt[:, 5:6], in0=rt[:, 4:5], scalar1=-1.0, scalar2=1.0,
        op0=OP.mult, op1=OP.add), incs=[("V", 1)])["V"]
    return vv, cp


def emit_l0f_A(b, ctx, tag, x_in):
    xnT = b.shared("xnT", [128, ND * T], BF16)
    wgbf, _ = bf16_mirror(b, "L0f_wgr", [ND, 128, 8])
    iota8, iotok = ctx.lazy("c_iota8", [128, 16], F32)
    pro = prologue(b, ctx, tag, x_in, 0, T, xnT, T, 0, ctx.psT,
                   pre_tok=("T", b.now("T")))
    wg = b.sb(f"{tag}_wg", [128, ND * 8], BF16)
    b.wait("gpsimd", "d_m", b.now("d_m"))
    wgld = b.dma("gpsimd", wg[:, :].rearrange("p (c f) -> p c f", c=ND),
                 wgbf.rearrange("c p f -> p c f"), sem="rw")
    rt = b.shared("l3f_rt", [128, 16], F32)
    eqs = b.shared("l3f_eqs", [128, 16], F32)
    oh = b.shared("l3f_oh", [128, 16], F32)
    lgs = b.shared("l3f_lg", [128, 16], F32)
    prev_v = None
    cnt_tok = None
    for tt in range(NT):
        if tt == 0:
            b.wait("tensor", "rw", wgld)
            b.wait("tensor", "V", pro)
            b.wait("tensor", "V", b.now("V"))
            b.wait("tensor", "S", b.now("S"))
            b.wait("vector", iotok[0], iotok[1])
        if cnt_tok is not None:
            b.wait("vector", "T", cnt_tok)
        vv, prev_v = l0f_route_tile(b, ctx, tt, xnT, wg, lgs, rt, eqs, oh,
                                    iota8, prev_v)
        b.wait("tensor", "V", vv)
        for k in range(2):
            pak = ctx.psA[0:1, k * 512:k * 512 + 8]
            r = b.emit("tensor", lambda e, pak=pak, k=k, tt=tt: e.matmul(
                pak, ctx.ones_t[:, 0:1], oh[:, k * 8:(k + 1) * 8],
                start=(tt == 0), stop=(tt == NT - 1)),
                incs=[("T", 1)] if k == 1 else ())
            if k == 1:
                cnt_tok = r["T"]
    crow = b.sb(f"{tag}_crow", [1, 16], F32)
    b.wait("vector", "T", cnt_tok)
    cv = None
    for k in range(2):
        cv = b.emit("vector", lambda e, k=k: e.tensor_copy(
            out=crow[:, k * 8:(k + 1) * 8],
            in_=ctx.psA[0:1, k * 512:k * 512 + 8]), incs=[("V", 1)])["V"]
    cnt_out = b.outp("cnt_part", [1, 16], F32)
    b.wait("sync", "V", cv)
    b.dma("sync", cnt_out[:, :], crow[:, :], sem="d_out")


def emit_l0f_B(b, ctx, tag, x_in, x_out):
    TQ = 256
    CAP = 8192
    xnT = b.shared("xnT", [128, ND * T], BF16)
    wgbf, _ = bf16_mirror(b, "L0f_wgr", [ND, 128, 8])
    w1bf, _ = bf16_mirror(b, "L0f_w1_st", [8, 32, 128, ND * 128])
    w2bf, _ = bf16_mirror(b, "L0f_w2_ch", [8, 32, 128, ND * 128])
    iota8, iotok = ctx.lazy("c_iota8", [128, 16], F32)
    triu, trtok = ctx.lazy("c_triu", [128, 128], F32)
    cnt = b.sb(f"{tag}_cnt", [8, 16], F32)
    b.dma("sync", cnt[:, :], b.inp("cnt_all", [8, 16])[:, :], sem="d_cs")
    cmask = b.sb(f"{tag}_cm", [8, 1], F32)
    b.dma("sync", cmask[:, :], b.inp("core_mask", [8, 1])[:, :], sem="d_cs")
    b1c = b.sb(f"{tag}_b1c", [128, 8 * 32], F32)
    b.dma("sync", b1c[:, :], b.inp("L0f_b1t", [128, 8 * 32])[:, :], sem="d_cs")
    b2f = b.sb(f"{tag}_b2f", [8, 1024], F32)
    b.dma("sync", b2f[:, :], b.inp("L0f_b2", [8, 1024])[:, :], sem="d_cs")
    ctk = b.handoff("sync", "d_cs", "HCs")
    b2bf = b.sb(f"{tag}_b2bf", [8, 1024], BF16)
    runsum = b.sb(f"{tag}_rsum", [1, 16], F32)
    weE = b.sb(f"{tag}_weE", [8, T], BF16)
    weR = b.sb(f"{tag}_weR", [1, 8 * T], BF16)

    pro = prologue(b, ctx, tag, x_in, 0, T, xnT, T, 0, ctx.psT,
                   pre_tok=("T", b.now("T")))
    wg = b.sb(f"{tag}_wg", [128, ND * 8], BF16)
    b.wait("gpsimd", "d_m", b.now("d_m"))
    wgld = b.dma("gpsimd", wg[:, :].rearrange("p (c f) -> p c f", c=ND),
                 wgbf.rearrange("c p f -> p c f"), sem="rw")
    rt = b.shared("l3f_rt", [128, 16], F32)
    eqs = b.shared("l3f_eqs", [128, 16], F32)
    oh = b.shared("l3f_oh", [128, 16], F32)
    lgs = b.shared("l3f_lg", [128, 16], F32)
    b.wait("tensor", "HCs", ctk)
    b.wait("tensor", "V", b.now("V"))
    b.wait("tensor", "S", b.now("S"))
    b.emit("tensor", lambda e: e.matmul(ctx.psB[0:1, 0:16], cmask[:, :],
                                        cnt[:, :], start=True, stop=True))
    pfx = b.emit("tensor", lambda e: e.matmul(
        ctx.psB[0:1, 512:520], ctx.ones_t[0:8, 0:1], cnt[:, 0:8],
        start=True, stop=True), incs=[("T", 1)])["T"]
    b.wait("vector", "T", pfx)
    b.emit("vector", lambda e: e.tensor_copy(out=runsum[:, :],
                                             in_=ctx.psB[0:1, 0:16]))
    b.drain("vector")
    b.emit("vector", lambda e: e.tensor_tensor(
        out=runsum[:, 8:16], in0=runsum[:, 8:16], in1=ctx.psB[0:1, 512:520],
        op=OP.add))
    b.drain("vector")
    rsv = b.emit("vector", lambda e: e.tensor_copy(out=b2bf[:, :], in_=b2f[:, :]),
                 incs=[("V", 1)])["V"]
    prev_v = None
    tr_fr = None
    loc_tok = None
    for tt in range(NT):
        bk = tt % 2
        if tt == 0:
            b.wait("tensor", "rw", wgld)
            b.wait("tensor", "V", pro)
            b.wait("tensor", "V", rsv)
            b.wait("tensor", "S", b.now("S"))
            b.wait("vector", iotok[0], iotok[1])
            b.wait("vector", trtok[0], trtok[1])
        if loc_tok is not None:
            b.wait("vector", "T", loc_tok)
        vv, prev_v = l0f_route_tile(b, ctx, tt, xnT, wg, lgs, rt, eqs, oh,
                                    iota8, prev_v)
        b.wait("tensor", "V", vv)
        lmm = {}
        for k in range(2):
            pak = ctx.psA[:, (2 + k) * 512:(2 + k) * 512 + 8]
            b.emit("tensor", lambda e, pak=pak, k=k: e.matmul(
                pak, triu[:, :], oh[:, k * 8:(k + 1) * 8],
                start=True, stop=False))
            lmm[k] = b.emit("tensor", lambda e, pak=pak, k=k: e.matmul(
                pak, ctx.ones_t[0:1, 0:128], runsum[0:1, k * 8:(k + 1) * 8],
                start=False, stop=True), incs=[("T", 1)])["T"]
        csm = {}
        for k in range(2):
            csm[k] = b.emit("tensor", lambda e, k=k: e.matmul(
                ctx.psA[0:1, (2 + k) * 512 + 16:(2 + k) * 512 + 24],
                ctx.ones_t[:, 0:1], oh[:, k * 8:(k + 1) * 8],
                start=True, stop=True), incs=[("T", 1)])["T"]
        loc_tok = csm[1]
        b.wait("vector", "T", loc_tok)
        for k in range(2):
            pak = ctx.psA[:, (2 + k) * 512:(2 + k) * 512 + 8]
            b.emit("vector", lambda e, pak=pak, k=k: e.tensor_scalar(
                out=eqs[:, k * 8:(k + 1) * 8], in0=pak, scalar1=CAP + 0.5,
                scalar2=None, op0=OP.is_lt))
            b.drain("vector")
            b.emit("vector", lambda e, k=k: e.tensor_tensor(
                out=runsum[:, k * 8:(k + 1) * 8],
                in0=runsum[:, k * 8:(k + 1) * 8],
                in1=ctx.psA[0:1, (2 + k) * 512 + 16:(2 + k) * 512 + 24],
                op=OP.add))
            b.drain("vector")
        km = rt[:, 8:16]
        weff = lgs[:, 0:8]
        b.emit("vector", lambda e: e.tensor_tensor(
            out=km, in0=oh[:, 0:8], in1=eqs[:, 0:8], op=OP.mult))
        b.drain("vector")
        b.emit("vector", lambda e: e.tensor_scalar(
            out=weff, in0=km, scalar1=rt[:, 4:5], scalar2=None, op0=OP.mult))
        b.drain("vector")
        b.emit("vector", lambda e: e.tensor_tensor(
            out=km, in0=oh[:, 8:16], in1=eqs[:, 8:16], op=OP.mult))
        b.drain("vector")
        b.emit("vector", lambda e: e.scalar_tensor_tensor(
            out=weff, in0=km, scalar=rt[:, 5:6], in1=weff, op0=OP.mult,
            op1=OP.add))
        b.drain("vector")
        webb = b.shared("l3f_webb", [128, 8], BF16)
        wbv = b.emit("vector", lambda e: e.tensor_copy(
            out=webb[:, :], in_=weff), incs=[("V", 1)])["V"]
        b.wait("tensor", "V", wbv)
        if tr_fr is not None:
            b.wait("tensor", "V", tr_fr)
        twe = b.emit("tensor", lambda e, bk=bk: e.transpose(
            out=ctx.psT[0:8, bk * 1024:bk * 1024 + 128], in_=webb[:, :],
            identity=ctx.identity[:, :]), incs=[("T", 1)])["T"]
        b.wait("vector", "T", twe)
        wev = b.emit("vector", lambda e, tt=tt, bk=bk: e.tensor_copy(
            out=weE[:, tt * 128:(tt + 1) * 128],
            in_=ctx.psT[0:8, bk * 1024:bk * 1024 + 128]), incs=[("V", 1)])["V"]
        b.wait("tensor", "V", wev)
        ttk = None
        for e_ in range(8):
            ttk = b.emit("tensor", lambda e, e_=e_, bk=bk: e.transpose(
                out=ctx.psT[0:1, bk * 1024 + e_ * 128:bk * 1024 + (e_ + 1) * 128],
                in_=webb[:, e_:e_ + 1], identity=ctx.identity[:, :]),
                incs=[("T", 1)] if e_ == 7 else ())
        b.wait("vector", "T", ttk["T"])
        dst = weR[0:1, :].rearrange("p (e t) -> p e t", e=8)[
            :, :, tt * 128:(tt + 1) * 128]
        tr_fr = b.emit("vector", lambda e, dst=dst, bk=bk: e.tensor_copy(
            out=dst, in_=ctx.psT[0:1, bk * 1024:bk * 1024 + 1024].rearrange(
                "p (e t) -> p e t", e=8)), incs=[("V", 1)])["V"]
    route_done_v = b.now("V")

    hall = b.sb(f"{tag}_hall", [128, 32 * TQ], BF16)
    web_t = b.shared("l3f_webt", [128, TQ], BF16)
    wb2 = b.shared("wb2qk", [128, 2 * ND * 128], BF16)
    rbuf = b.sb(f"{tag}_rb", [128, 512], BF16)
    xold = b.shared("scr8kA", [128, 2 * D], F32)
    xnew = b.shared("scr8kB", [128, 2 * D], F32)
    out_tok = None
    so_tok = {}
    out_v = {}
    guard = {}
    psq = None
    for q in range(T // TQ):
        t0 = q * TQ
        first = True
        for ei in range(8):
            Xfn = lambda dc, t0=t0: xnT[:, dc * T + t0:dc * T + t0 + TQ]
            b.wait("tensor", "V", b.now("V"))
            b.wait("tensor", "S", b.now("S"))
            tkb = b.emit("tensor", lambda e, ei=ei, t0=t0: e.matmul(
                ctx.psB[:, 0:TQ], ctx.ones_row[:, :],
                weR[0:1, ei * T + t0:ei * T + t0 + TQ],
                start=True, stop=True), incs=[("T", 1)])
            b.wait("vector", "T", tkb["T"])
            web_tok = b.emit("vector", lambda e: e.tensor_copy(
                out=web_t[:, :], in_=ctx.psB[:, 0:TQ]), incs=[("V", 1)])["V"]

            def evac_h(ft, ts, pa, nc_, t_tok, ei=ei):
                dst = hall[:, ft * TQ:ft * TQ + nc_]
                b.wait("scalar", "T", t_tok)
                if hasattr(b, "l0f_rbf"):
                    b.wait("scalar", "V", b.l0f_rbf)
                tks = b.emit("scalar", lambda e, pa=pa, nc_=nc_, ei=ei, ft=ft:
                             e.activation(out=rbuf[:, 0:nc_], in_=pa,
                                          func=AF.Gelu,
                                          bias=b1c[:, ei * 32 + ft:
                                                   ei * 32 + ft + 1]),
                             incs=[("S", 1)])
                b.wait("vector", "S", tks["S"])
                tkv = b.emit("vector", lambda e, dst=dst, nc_=nc_:
                             e.tensor_tensor(out=dst, in0=rbuf[:, 0:nc_],
                                             in1=web_t[:, 0:nc_], op=OP.mult),
                             incs=[("V", 1)])
                b.l0f_rbf = tkv["V"]
                return ("V", tkv["V"])

            ew = [("V", b.now("V")), ("S", b.now("S")), ("V", web_tok)]
            if q == 0 and ei == 0:
                ew.append(("V", route_done_v))
            mm_fm(b, ctx, w1bf[ei], 32, ND, Xfn, TQ, evac_h,
                  ctx.psB[:, 0:1024], wb2, extra_wait=ew)
            h_done = b.now("V")
            for dc in range(32):
                ci = ei * 32 + dc
                gci = q * 8 * 32 + ci
                wslot = dc % 2
                if dc < 2:
                    b.wait("gpsimd", "T", b.now("T"))
                elif gci >= 2:
                    b.wait("gpsimd", "T", guard[gci - 2])
                wld = b.dma("gpsimd", wb2[:, wslot * 1024:wslot * 1024 + 1024],
                            w2bf[ei][dc], sem=f"wc{wslot}")
                b.wait("tensor", f"wc{wslot}", wld)
                if dc == 0:
                    b.wait("tensor", "V", h_done)
                if first:
                    b.wait("tensor", "V", b.now("V"))
                    b.wait("tensor", "S", b.now("S"))
                    if q > 0:
                        b.wait("tensor", psq[0], psq[1])
                for tt in range(2):
                    for half in range(2):
                        pa = ctx.psA[:, (tt * 2 + half) * 512:
                                     (tt * 2 + half) * 512 + 512]
                        b.emit("tensor", lambda e, pa=pa, dc=dc, tt=tt,
                               wslot=wslot, half=half,
                               st=(ei == 0 and dc == 0): e.matmul(
                            pa, hall[:, dc * TQ + tt * 128:
                                     dc * TQ + (tt + 1) * 128],
                            wb2[:, wslot * 1024 + half * 512:
                                wslot * 1024 + half * 512 + 512],
                            start=st, stop=False),
                            incs=[("T", 1)] if (tt == 1 and half == 1) else ())
                        first = False
                guard[gci] = b.cnt["T"]
        mm_done = None
        for tt in range(2):
            for half in range(2):
                pa = ctx.psA[:, (tt * 2 + half) * 512:(tt * 2 + half) * 512 + 512]
                mm_done = b.emit("tensor", lambda e, pa=pa, tt=tt, half=half,
                                 t0=t0: e.matmul(
                    pa, weE[:, t0 + tt * 128:t0 + (tt + 1) * 128],
                    b2bf[:, half * 512:(half + 1) * 512],
                    start=False, stop=True),
                    incs=[("T", 1)])["T"]
        for tt in range(2):
            tgl = q * 2 + tt
            buf = tt % 2
            if tgl >= 2:
                b.wait("sync", "V", out_v[tgl - 2])
                b.wait("sync", f"so{tgl % 2}", so_tok[tgl - 2])
            if tt == 0:
                b.wait("sync", "V", b.now("V"))
                b.wait("sync", "S", b.now("S"))
            xo = xold[:, buf * D:(buf + 1) * D]
            xl = b.dma("sync", xo, x_in[(t0 + tt * 128):(t0 + (tt + 1) * 128), :],
                       sem=f"xo{tt % 2}")
            tkv = None
            for half in range(2):
                pa = ctx.psA[:, (tt * 2 + half) * 512:(tt * 2 + half) * 512 + 512]
                b.wait("vector", "T", mm_done)
                b.wait("vector", f"xo{tt % 2}", xl)
                xn_sl = xnew[:, buf * D + half * 512:buf * D + half * 512 + 512]
                tkv = b.emit("vector", lambda e, pa=pa, xn_sl=xn_sl, xo=xo,
                             half=half: e.tensor_tensor(
                    out=xn_sl, in0=pa, in1=xo[:, half * 512:(half + 1) * 512],
                    op=OP.add), incs=[("V", 1)])
            out_v[tgl] = tkv["V"]
            b.wait("sync", "V", tkv["V"])
            out_tok = b.dma("sync", x_out[(t0 + tt * 128):(t0 + (tt + 1) * 128), :],
                            xnew[:, buf * D:(buf + 1) * D], sem=f"so{tgl % 2}")
            so_tok[tgl] = out_tok
        psq = ("V", b.now("V"))
    return out_tok


def prep_l0_ffn(inp, H):
    if "L0f_wgr" in H:
        return
    H["L0f_wgr"] = np.ascontiguousarray(
        inp["L0_ffn_wg"].astype(np.float32).reshape(ND, 128, 8))
    w1 = inp["L0_ffn_w1"].astype(np.float32)
    w2 = inp["L0_ffn_w2"].astype(np.float32)
    H["L0f_w1_st"] = np.ascontiguousarray(np.stack(
        [_st_tile(np.ascontiguousarray(w1[e]), ND, 32) for e in range(8)]))
    H["L0f_w2_ch"] = np.ascontiguousarray(np.stack(
        [np.ascontiguousarray(w2[e]).reshape(32, 128, ND * 128)
         for e in range(8)]))
    H["L0f_b1t"] = np.ascontiguousarray(
        inp["L0_ffn_b1"].astype(np.float32).reshape(8, 32, 128)
        .transpose(2, 0, 1).reshape(128, 8 * 32))
    H["L0f_b2"] = inp["L0_ffn_b2"].astype(np.float32)
    io = np.arange(8, dtype=np.float32)
    H["c_iota8"] = np.tile(np.concatenate([io, io + 999.0])[None, :], (128, 1))
    tri = (np.arange(128)[:, None] <= np.arange(128)[None, :])
    H["c_triu"] = tri.astype(np.float32)


def build_l0f(phase):
    nc = bass.Bass("TRN2", target_bir_lowering=False)
    b = Bld(nc)
    ctx = Ctx(b)
    if phase == "A":
        emit_l0f_A(b, ctx, "s2A", b.inp("x_io", [T, D]))
    else:
        emit_l0f_B(b, ctx, "s2B", b.inp("x_io", [T, D]),
                   b.outp("x_out", [T, D]))
    b.finalize({"so0": b.now("so0"), "so1": b.now("so1"),
                "d_out": b.now("d_out")})
    return b


def run_sub2(x_full, H):
    bA = build_l0f("A")
    in_maps = []
    for c in range(NCORES):
        m = {name: (np.ascontiguousarray(x_full[c]) if name == "x_io"
                    else H[name]) for name in bA.inputs}
        in_maps.append(m)
    resA = run_bass_kernel_spmd(bA.nc, in_maps, core_ids=list(range(NCORES)))
    cnt_all = np.ascontiguousarray(np.stack(
        [np.asarray(resA.results[c]["cnt_part"], np.float32)[0]
         for c in range(NCORES)]))
    bB = build_l0f("B")
    in_maps = []
    for c in range(NCORES):
        m = {}
        for name in bB.inputs:
            if name == "x_io":
                m[name] = np.ascontiguousarray(x_full[c])
            elif name == "cnt_all":
                m[name] = cnt_all
            elif name == "core_mask":
                m[name] = np.ascontiguousarray(
                    (np.arange(8) < c).astype(np.float32).reshape(8, 1))
            else:
                m[name] = H[name]
        in_maps.append(m)
    resB = run_bass_kernel_spmd(bB.nc, in_maps, core_ids=list(range(NCORES)))
    return np.stack([np.asarray(resB.results[c]["x_out"], np.float32)
                     for c in range(NCORES)])


SUBS[2] = (None, prep_l0_ffn)
SUBS[4] = (None, prep_l0_ffn)


def emit_l1_ffn(b, ctx, tag, x_in, x_out):
    TQ = 256
    xnT = b.shared("xnT", [128, ND * T], BF16)
    wgbf, _ = bf16_mirror(b, "L1f_wg", [ND, 128, 8])
    w1bf, _ = bf16_mirror(b, "L1f_w1_st", [8, 16, 128, ND * 128])
    w2bf, _ = bf16_mirror(b, "L1f_w2_st", [8, 8, 128, ND * 128])
    iota8, iotok = ctx.lazy("c_iota8", [128, 16], F32)
    lbr = b.sb(f"{tag}_lbr", [1, 8], BF16)
    b.dma("gpsimd", lbr[:, :], b.inp("L1f_rb", [1, 8])[:, :], sem="d_cg")
    hct = b.handoff("gpsimd", "d_cg", "HCg")

    pro = prologue(b, ctx, tag, x_in, 0, T, xnT, T, 0, ctx.psT,
                   pre_tok=("T", b.now("T")))
    wg = b.sb(f"{tag}_wg", [128, ND * 8], BF16)
    b.wait("gpsimd", "d_m", b.now("d_m"))
    wgld = b.dma("gpsimd", wg[:, :].rearrange("p (c f) -> p c f", c=ND),
                 wgbf.rearrange("c p f -> p c f"), sem="rw")
    lbt = b.shared("gb_g", [128, D], BF16)[:, 0:8]
    bcast_row(b, ctx, lbt, lbr, 8, row_wait=("HCg", hct))
    lb_tok = b.now("V")
    rt = b.shared("l3f_rt", [128, 16], F32)
    eqs = b.shared("l3f_eqs", [128, 16], F32)
    oh = b.shared("l3f_oh", [128, 16], F32)
    lgs = b.shared("l3f_lg", [128, 16], F32)
    weR = b.sb(f"{tag}_weR", [1, 8 * T], BF16)
    prev_v = None
    tr_fr = None
    for tt in range(NT):
        bk = tt % 2
        pa = ctx.psB[:, bk * 512:bk * 512 + 8]
        if tt == 0:
            b.wait("tensor", "rw", wgld)
            b.wait("tensor", "V", pro)
            b.wait("tensor", "V", b.now("V"))
            b.wait("tensor", "S", b.now("S"))
        if prev_v is not None:
            b.wait("tensor", "V", prev_v)
        tk = None
        for dc in range(ND):
            tk = b.emit("tensor", lambda e, pa=pa, dc=dc, tt=tt: e.matmul(
                pa, xnT[:, dc * T + tt * 128:dc * T + (tt + 1) * 128],
                wg[:, dc * 8:(dc + 1) * 8], start=(dc == 0), stop=(dc == ND - 1)),
                incs=[("T", 1)] if dc == ND - 1 else ())
        b.wait("vector", "T", tk["T"])
        if tt == 0:
            b.wait("vector", "V", lb_tok)
            b.wait("vector", iotok[0], iotok[1])
        lbd = lgs[:, 8:16]
        b.emit("vector", lambda e, pa=pa, lbd=lbd: e.tensor_tensor(
            out=lbd, in0=pa, in1=lbt, op=OP.add))
        b.drain("vector")
        sel1 = rt[:, 2:3]
        sel2 = rt[:, 3:4]
        route_top2(b, ctx, lbd, lbd, rt, eqs, oh[:, 0:8], oh[:, 8:16],
                   sel1, sel2, iota8)
        dcol = rt[:, 6:7]
        b.emit("vector", lambda e: e.tensor_tensor(
            out=dcol, in0=sel1, in1=sel2, op=OP.subtract))
        b.drain("vector")
        tvv = b.emit("vector", lambda e: e.nop(), incs=[("V", 1)])
        b.wait("scalar", "V", tvv["V"])
        w1s = rt[:, 4:5]
        w2s = rt[:, 5:6]
        ts1 = b.emit("scalar", lambda e: e.activation(
            out=w1s, in_=dcol, func=AF.Sigmoid), incs=[("S", 1)])
        b.wait("vector", "S", ts1["S"])
        b.emit("vector", lambda e: e.tensor_scalar(
            out=w2s, in0=w1s, scalar1=-1.0, scalar2=1.0, op0=OP.mult,
            op1=OP.add))
        b.drain("vector")
        web = eqs[:, 0:8]
        b.emit("vector", lambda e: e.tensor_scalar(
            out=web, in0=oh[:, 0:8], scalar1=w1s, scalar2=None, op0=OP.mult))
        b.drain("vector")
        webf_ = eqs[:, 8:16]
        b.emit("vector", lambda e: e.scalar_tensor_tensor(
            out=webf_, in0=oh[:, 8:16], scalar=w2s, in1=web, op0=OP.mult,
            op1=OP.add))
        b.drain("vector")
        webb = b.shared("l3f_webb", [128, 8], BF16)
        prev_v = b.emit("vector", lambda e: e.tensor_copy(
            out=webb[:, :], in_=webf_), incs=[("V", 1)])["V"]
        b.wait("tensor", "V", prev_v)
        if tr_fr is not None:
            b.wait("tensor", "V", tr_fr)
        ttk = None
        for e_ in range(8):
            ttk = b.emit("tensor", lambda e, e_=e_, bk=bk: e.transpose(
                out=ctx.psT[0:1, bk * 1024 + e_ * 128:bk * 1024 + (e_ + 1) * 128],
                in_=webb[:, e_:e_ + 1], identity=ctx.identity[:, :]),
                incs=[("T", 1)] if e_ == 7 else ())
        b.wait("vector", "T", ttk["T"])
        dst = weR[0:1, :].rearrange("p (e t) -> p e t", e=8)[
            :, :, tt * 128:(tt + 1) * 128]
        tr_fr = b.emit("vector", lambda e, dst=dst, bk=bk: e.tensor_copy(
            out=dst, in_=ctx.psT[0:1, bk * 1024:bk * 1024 + 1024].rearrange(
                "p (e t) -> p e t", e=8)), incs=[("V", 1)])["V"]
    route_done_v = b.now("V")

    hall = b.shared("l3f_hall", [128, 9 * ND * TQ], BF16)
    web_t = b.shared("l3f_webt", [128, TQ], BF16)
    wb2 = b.shared("wb2qk", [128, 2 * ND * 128], BF16)
    gbuf = b.sb(f"{tag}_gbuf", [128, ND * TQ], BF16)
    xold = b.shared("scr8kA", [128, 2 * D], F32)
    xnew = b.shared("scr8kB", [128, 2 * D], F32)
    out_tok = None
    so_tok = {}
    out_v = {}
    guard = {}
    psq = None
    for q in range(T // TQ):
        t0 = q * TQ
        for ei in range(8):
            Xfn = lambda dc, t0=t0: xnT[:, dc * T + t0:dc * T + t0 + TQ]
            hsl = hall[:, ei * ND * TQ:(ei + 1) * ND * TQ]
            b.wait("tensor", "V", b.now("V"))
            b.wait("tensor", "S", b.now("S"))
            tkb = b.emit("tensor", lambda e, ei=ei, t0=t0: e.matmul(
                ctx.psB[:, 0:TQ], ctx.ones_row[:, :],
                weR[0:1, ei * T + t0:ei * T + t0 + TQ],
                start=True, stop=True), incs=[("T", 1)])
            b.wait("vector", "T", tkb["T"])
            web_tok = b.emit("vector", lambda e: e.tensor_copy(
                out=web_t[:, :], in_=ctx.psB[:, 0:TQ]), incs=[("V", 1)])["V"]

            def evac_g(ft, ts, pa, nc_, t_tok, ei=ei, hsl=hsl):
                if ft < ND:
                    b.wait("scalar", "T", t_tok)
                    if hasattr(b, "l1f_sf"):
                        b.wait("scalar", "V", b.l1f_sf)
                    dstg = gbuf[:, ft * TQ:ft * TQ + nc_]
                    tks = b.emit("scalar", lambda e, pa=pa, dstg=dstg:
                                 e.activation(out=dstg, in_=pa, func=AF.Silu),
                                 incs=[("S", 1)])
                    return ("S", tks["S"])
                dst = hsl[:, (ft - ND) * TQ:(ft - ND) * TQ + nc_]
                gsl = gbuf[:, (ft - ND) * TQ:(ft - ND) * TQ + nc_]
                b.wait("vector", "T", t_tok)
                b.wait("vector", "S", b.now("S"))
                b.emit("vector", lambda e, dst=dst, pa=pa, gsl=gsl:
                       e.tensor_tensor(out=dst, in0=pa, in1=gsl, op=OP.mult))
                b.drain("vector")
                tkv = b.emit("vector", lambda e, dst=dst, nc_=nc_:
                             e.tensor_tensor(out=dst, in0=dst,
                                             in1=web_t[:, 0:nc_], op=OP.mult),
                             incs=[("V", 1)])
                b.l1f_sf = tkv["V"]
                return ("V", tkv["V"])

            ew = [("V", b.now("V")), ("S", b.now("S")), ("V", web_tok)]
            if q == 0 and ei == 0:
                ew.append(("V", route_done_v))
            mm_fm(b, ctx, w1bf[ei], 16, ND, Xfn, TQ, evac_g,
                  ctx.psA[:, 0:1024], wb2, extra_wait=ew)
        h_done = b.now("V")
        b.wait("gpsimd", "T", b.now("T"))
        first = True
        for ei in range(8):
            for dc in range(ND):
                ci = ei * ND + dc
                gci = q * 8 * ND + ci
                wslot = ci % 2
                if ci < 2:
                    b.wait("gpsimd", "T", b.now("T"))
                elif gci >= 2:
                    b.wait("gpsimd", "T", guard[gci - 2])
                wld = b.dma("gpsimd", wb2[:, wslot * 1024:wslot * 1024 + 1024],
                            w2bf[ei][dc], sem=f"wc{wslot}")
                b.wait("tensor", f"wc{wslot}", wld)
                if first:
                    b.wait("tensor", "V", h_done)
                    b.wait("tensor", "V", b.now("V"))
                    b.wait("tensor", "S", b.now("S"))
                    if q > 0:
                        b.wait("tensor", psq[0], psq[1])
                for tt in range(2):
                    for half in range(2):
                        pa = ctx.psA[:, (tt * 2 + half) * 512:
                                     (tt * 2 + half) * 512 + 512]
                        b.emit("tensor", lambda e, pa=pa, ei=ei, dc=dc,
                               tt=tt, wslot=wslot, half=half,
                               st=(ei == 0 and dc == 0),
                               sp=(ei == 7 and dc == ND - 1): e.matmul(
                            pa, hall[:, ei * ND * TQ + dc * TQ + tt * 128:
                                     ei * ND * TQ + dc * TQ + (tt + 1) * 128],
                            wb2[:, wslot * 1024 + half * 512:
                                wslot * 1024 + half * 512 + 512],
                            start=st, stop=sp),
                            incs=[("T", 1)] if (tt == 1 and half == 1) else ())
                        first = False
                guard[gci] = b.cnt["T"]
        mm_done = b.cnt["T"]
        for tt in range(2):
            tgl = q * 2 + tt
            buf = tt % 2
            if tgl >= 2:
                b.wait("sync", "V", out_v[tgl - 2])
                b.wait("sync", f"so{tgl % 2}", so_tok[tgl - 2])
            if tt == 0:
                b.wait("sync", "V", b.now("V"))
                b.wait("sync", "S", b.now("S"))
            xo = xold[:, buf * D:(buf + 1) * D]
            xl = b.dma("sync", xo, x_in[(t0 + tt * 128):(t0 + (tt + 1) * 128), :],
                       sem=f"xo{tt % 2}")
            tkv = None
            for half in range(2):
                pa = ctx.psA[:, (tt * 2 + half) * 512:(tt * 2 + half) * 512 + 512]
                b.wait("vector", "T", mm_done)
                b.wait("vector", f"xo{tt % 2}", xl)
                xn_sl = xnew[:, buf * D + half * 512:buf * D + half * 512 + 512]
                tkv = b.emit("vector", lambda e, pa=pa, xn_sl=xn_sl, xo=xo,
                             half=half: e.tensor_tensor(
                    out=xn_sl, in0=pa, in1=xo[:, half * 512:(half + 1) * 512],
                    op=OP.add), incs=[("V", 1)])
            out_v[tgl] = tkv["V"]
            b.wait("sync", "V", tkv["V"])
            out_tok = b.dma("sync", x_out[(t0 + tt * 128):(t0 + (tt + 1) * 128), :],
                            xnew[:, buf * D:(buf + 1) * D], sem=f"so{tgl % 2}")
            so_tok[tgl] = out_tok
        psq = ("V", b.now("V"))
    return out_tok


def prep_l1_ffn(inp, H):
    if "L1f_wg" in H:
        return
    H["L1f_wg"] = np.ascontiguousarray(
        inp["L1_ffn_router_w"].astype(np.float32).reshape(ND, 128, 8))
    wgte = inp["L1_ffn_we_gate"].astype(np.float32)
    wup = inp["L1_ffn_we_up"].astype(np.float32)
    wdn = inp["L1_ffn_we_down"].astype(np.float32)
    H["L1f_w1_st"] = np.ascontiguousarray(np.stack(
        [_st_tile(np.ascontiguousarray(
            np.concatenate([wgte[e], wup[e]], axis=1)), ND, 16)
         for e in range(8)]))
    H["L1f_w2_st"] = np.ascontiguousarray(np.stack(
        [np.ascontiguousarray(wdn[e]).reshape(ND, 128, ND * 128)
         for e in range(8)]))
    H["L1f_rb"] = inp["L1_ffn_router_b"].astype(np.float32).reshape(1, 8)
    io = np.arange(8, dtype=np.float32)
    H["c_iota8"] = np.tile(np.concatenate([io, io + 999.0])[None, :], (128, 1))


SUBS[6] = (emit_l1_ffn, prep_l1_ffn)
SUBS[8] = (emit_l1_ffn, prep_l1_ffn)


def emit_l3_ffn(b, ctx, tag, x_in, x_out):
    TQ = 256
    xnT = b.shared("xnT", [128, ND * T], BF16)
    wgbf, _ = bf16_mirror(b, "L3f_wg", [ND, 128, 8])
    w1bf, _ = bf16_mirror(b, "L3f_w1_st", [9, 8, 128, ND * 128])
    w2bf, _ = bf16_mirror(b, "L3f_w2_st", [9, 8, 128, ND * 128])
    iota8, iotok = ctx.lazy("c_iota8", [128, 16], F32)
    lbr = b.sb(f"{tag}_lbr", [1, 8], BF16)
    b.dma("gpsimd", lbr[:, :], b.inp("L3f_lb", [1, 8])[:, :], sem="d_cg")
    hct = b.handoff("gpsimd", "d_cg", "HCg")

    pro = prologue(b, ctx, tag, x_in, 0, T, xnT, T, 0, ctx.psT,
                   pre_tok=("T", b.now("T")))
    wg = b.sb(f"{tag}_wg", [128, ND * 8], BF16)
    b.wait("gpsimd", "d_m", b.now("d_m"))
    wgld = b.dma("gpsimd", wg[:, :].rearrange("p (c f) -> p c f", c=ND),
                 wgbf.rearrange("c p f -> p c f"), sem="rw")
    lbt = b.shared("gb_g", [128, D], BF16)[:, 0:8]
    bcast_row(b, ctx, lbt, lbr, 8, row_wait=("HCg", hct))
    lb_tok = b.now("V")
    rt = b.shared("l3f_rt", [128, 16], F32)
    eqs = b.shared("l3f_eqs", [128, 16], F32)
    oh = b.shared("l3f_oh", [128, 16], F32)
    lgs = b.shared("l3f_lg", [128, 16], F32)
    weR = b.sb(f"{tag}_weR", [1, 8 * T], BF16)
    prev_v = None
    tr_fr = None
    for tt in range(NT):
        bk = tt % 2
        pa = ctx.psB[:, bk * 512:bk * 512 + 8]
        if tt == 0:
            b.wait("tensor", "rw", wgld)
            b.wait("tensor", "V", pro)
            b.wait("tensor", "V", b.now("V"))
            b.wait("tensor", "S", b.now("S"))
        if prev_v is not None:
            b.wait("tensor", "V", prev_v)
        tk = None
        for dc in range(ND):
            tk = b.emit("tensor", lambda e, pa=pa, dc=dc, tt=tt: e.matmul(
                pa, xnT[:, dc * T + tt * 128:dc * T + (tt + 1) * 128],
                wg[:, dc * 8:(dc + 1) * 8], start=(dc == 0), stop=(dc == ND - 1)),
                incs=[("T", 1)] if dc == ND - 1 else ())
        b.wait("vector", "T", tk["T"])
        if tt == 0:
            b.wait("vector", "V", lb_tok)
            b.wait("vector", iotok[0], iotok[1])
        lg = lgs[:, 0:8]
        lbd = lgs[:, 8:16]
        b.emit("vector", lambda e, pa=pa, lg=lg: e.tensor_copy(out=lg, in_=pa))
        b.drain("vector")
        b.emit("vector", lambda e, lg=lg, lbd=lbd: e.tensor_tensor(
            out=lbd, in0=lg, in1=lbt, op=OP.add))
        b.drain("vector")
        sel1 = rt[:, 2:3]
        sel2 = rt[:, 3:4]
        route_top2(b, ctx, lbd, lg, rt, eqs, oh[:, 0:8], oh[:, 8:16],
                   sel1, sel2, iota8)
        w1s = rt[:, 4:5]
        w2s = rt[:, 5:6]
        tvv = b.emit("vector", lambda e: e.nop(), incs=[("V", 1)])
        b.wait("scalar", "V", tvv["V"])
        ts1 = b.emit("scalar", lambda e, sel1=sel1, w1s=w1s: e.activation(
            out=w1s, in_=sel1, func=AF.Sigmoid), incs=[("S", 1)])
        ts1 = b.emit("scalar", lambda e, sel2=sel2, w2s=w2s: e.activation(
            out=w2s, in_=sel2, func=AF.Sigmoid), incs=[("S", 1)])
        b.wait("vector", "S", ts1["S"])
        den = rt[:, 6:7]
        b.emit("vector", lambda e, w1s=w1s, w2s=w2s, den=den: e.tensor_tensor(
            out=den, in0=w1s, in1=w2s, op=OP.add))
        b.drain("vector")
        b.emit("vector", lambda e, den=den: e.tensor_scalar(
            out=den, in0=den, scalar1=1e-6, scalar2=None, op0=OP.add))
        b.drain("vector")
        b.emit("vector", lambda e, den=den: e.reciprocal(den, den))
        b.drain("vector")
        b.emit("vector", lambda e, w1s=w1s, den=den: e.tensor_tensor(
            out=w1s, in0=w1s, in1=den, op=OP.mult))
        b.drain("vector")
        b.emit("vector", lambda e, w2s=w2s, den=den: e.tensor_tensor(
            out=w2s, in0=w2s, in1=den, op=OP.mult))
        b.drain("vector")
        web = eqs[:, 0:8]
        b.emit("vector", lambda e, web=web, w1s=w1s: e.tensor_scalar(
            out=web, in0=oh[:, 0:8], scalar1=w1s, scalar2=None, op0=OP.mult))
        b.drain("vector")
        webf_ = eqs[:, 8:16]
        b.emit("vector", lambda e, webf_=webf_, w2s=w2s, web=web:
               e.scalar_tensor_tensor(out=webf_, in0=oh[:, 8:16], scalar=w2s,
                                      in1=web, op0=OP.mult, op1=OP.add))
        b.drain("vector")
        wecol = oh[:, 0:8]
        webb = b.shared("l3f_webb", [128, 8], BF16)
        prev_v = b.emit("vector", lambda e, webf_=webf_, webb=webb: e.tensor_copy(
            out=webb[:, :], in_=webf_), incs=[("V", 1)])["V"]
        b.wait("tensor", "V", prev_v)
        if tr_fr is not None:
            b.wait("tensor", "V", tr_fr)
        ttk = None
        for e_ in range(8):
            ttk = b.emit("tensor", lambda e, e_=e_, tt=tt, bk=bk: e.transpose(
                out=ctx.psT[0:1, bk * 1024 + e_ * 128:bk * 1024 + (e_ + 1) * 128],
                in_=webb[:, e_:e_ + 1], identity=ctx.identity[:, :]),
                incs=[("T", 1)] if e_ == 7 else ())
        b.wait("vector", "T", ttk["T"])
        dst = weR[0:1, :].rearrange("p (e t) -> p e t", e=8)[:, :, tt * 128:(tt + 1) * 128]
        tr_fr = b.emit("vector", lambda e, dst=dst, bk=bk: e.tensor_copy(
            out=dst, in_=ctx.psT[0:1, bk * 1024:bk * 1024 + 1024].rearrange(
                "p (e t) -> p e t", e=8)), incs=[("V", 1)])["V"]
    route_done_v = b.now("V")
    route_done_t = b.now("T")

    hall = b.shared("l3f_hall", [128, 9 * ND * TQ], BF16)
    web_t = b.shared("l3f_webt", [128, TQ], BF16)
    wb2 = b.shared("wb2qk", [128, 2 * ND * 128], BF16)
    rbuf = b.shared("l3f_rb", [128, 512], BF16)
    xold = b.shared("scr8kA", [128, 2 * D], F32)
    xnew = b.shared("scr8kB", [128, 2 * D], F32)
    out_tok = None
    so_tok = {}
    out_v = {}
    for q in range(T // TQ):
        t0 = q * TQ
        for ei in range(9):
            Xfn = lambda dc, t0=t0: xnT[:, dc * T + t0:dc * T + t0 + TQ]
            hsl = hall[:, ei * ND * TQ:(ei + 1) * ND * TQ]
            if ei < 8:
                b.wait("tensor", "V", b.now("V"))
                b.wait("tensor", "S", b.now("S"))
                tkb = b.emit("tensor", lambda e, ei=ei, t0=t0: e.matmul(
                    ctx.psB[:, 0:TQ], ctx.ones_row[:, :],
                    weR[0:1, ei * T + t0:ei * T + t0 + TQ],
                    start=True, stop=True), incs=[("T", 1)])
                b.wait("vector", "T", tkb["T"])
                web_tok = b.emit("vector", lambda e: e.tensor_copy(
                    out=web_t[:, :], in_=ctx.psB[:, 0:TQ]), incs=[("V", 1)])["V"]

            def evac_h(ft, ts, pa, nc_, t_tok, ei=ei):
                dst = hsl[:, ft * TQ + ts * 512:ft * TQ + ts * 512 + nc_]
                b.wait("scalar", "T", t_tok)
                if hasattr(b, "l3f_rbf"):
                    b.wait("scalar", "V", b.l3f_rbf)
                tks = b.emit("scalar", lambda e, pa=pa, nc_=nc_: e.activation(
                    out=rbuf[:, 0:nc_], in_=pa, func=AF.Relu), incs=[("S", 1)])
                b.wait("vector", "S", tks["S"])
                if ei < 8:
                    b.emit("vector", lambda e, dst=dst, nc_=nc_: e.tensor_tensor(
                        out=dst, in0=rbuf[:, 0:nc_], in1=rbuf[:, 0:nc_],
                        op=OP.mult))
                    b.drain("vector")
                    tkv = b.emit("vector", lambda e, dst=dst, ts=ts, nc_=nc_:
                                 e.tensor_tensor(out=dst, in0=dst,
                                                 in1=web_t[:, ts * 512:ts * 512 + nc_],
                                                 op=OP.mult), incs=[("V", 1)])
                else:
                    tkv = b.emit("vector", lambda e, dst=dst, nc_=nc_:
                                 e.tensor_tensor(out=dst, in0=rbuf[:, 0:nc_],
                                                 in1=rbuf[:, 0:nc_], op=OP.mult),
                                 incs=[("V", 1)])
                b.l3f_rbf = tkv["V"]
                return ("V", tkv["V"])

            ew = [("V", b.now("V")), ("S", b.now("S"))]
            if q == 0 and ei == 0:
                ew.append(("V", route_done_v))
            if ei < 8:
                ew.append(("V", web_tok))
            mm_fm(b, ctx, w1bf[ei], ND, ND, Xfn, TQ, evac_h,
                  ctx.psA[:, 0:1024], wb2, extra_wait=ew)
        h_done = b.now("V")
        if q == 0 and "hallq0" in DBG:
            dtok = dbg_dump(b, "hallq0", hall[:, :], [128, 9 * ND * TQ], BF16)
            b.wait("vector", "d_out", dtok)
        b.wait("gpsimd", "T", b.now("T"))
        first = True
        tkmm = None
        for ei in range(9):
            for dc in range(ND):
                ci = ei * ND + dc
                gci = q * 9 * ND + ci
                wslot = ci % 2
                b.w2guard = getattr(b, "w2guard", {})
                if ci < 2:
                    b.wait("gpsimd", "T", b.now("T"))
                elif gci >= 2:
                    b.wait("gpsimd", "T", b.w2guard[gci - 2])
                wld = b.dma("gpsimd", wb2[:, wslot * 1024:wslot * 1024 + 1024],
                            w2bf[ei][dc], sem=f"wc{wslot}")
                b.wait("tensor", f"wc{wslot}", wld)
                if first:
                    b.wait("tensor", "V", h_done)
                    b.wait("tensor", "V", b.now("V"))
                    b.wait("tensor", "S", b.now("S"))
                    if q > 0:
                        b.wait("tensor", psq[0], psq[1])
                for tt in range(2):
                    for half in range(2):
                        pa = ctx.psA[:, (tt * 2 + half) * 512:
                                     (tt * 2 + half) * 512 + 512]
                        tkmm = b.emit("tensor", lambda e, pa=pa, ei=ei, dc=dc,
                                      tt=tt, half=half, wslot=wslot,
                                      st=(ei == 0 and dc == 0),
                                      sp=(ei == 8 and dc == ND - 1):
                                      e.matmul(
                            pa, hall[:, ei * ND * TQ + dc * TQ + tt * 128:
                                     ei * ND * TQ + dc * TQ + (tt + 1) * 128],
                            wb2[:, wslot * 1024 + half * 512:
                                wslot * 1024 + half * 512 + 512],
                            start=st, stop=sp),
                            incs=[("T", 1)] if (tt == 1 and half == 1) else ())
                        first = False
                b.w2guard[gci] = b.cnt["T"]
        mm_done = b.cnt["T"]
        for tt in range(2):
            tgl = q * 2 + tt
            buf = tt % 2
            if tgl >= 2:
                b.wait("sync", "V", out_v[tgl - 2])
                b.wait("sync", f"so{tgl % 2}", so_tok[tgl - 2])
            if tt == 0:
                b.wait("sync", "V", b.now("V"))
                b.wait("sync", "S", b.now("S"))
            xo = xold[:, buf * D:(buf + 1) * D]
            xl = b.dma("sync", xo, x_in[(t0 + tt * 128):(t0 + (tt + 1) * 128), :],
                       sem=f"xo{tt % 2}")
            tkv = None
            for half in range(2):
                pa = ctx.psA[:, (tt * 2 + half) * 512:(tt * 2 + half) * 512 + 512]
                b.wait("vector", "T", mm_done)
                b.wait("vector", f"xo{tt % 2}", xl)
                xn_sl = xnew[:, buf * D + half * 512:buf * D + half * 512 + 512]
                tkv = b.emit("vector", lambda e, pa=pa, xn_sl=xn_sl, xo=xo,
                             half=half: e.tensor_tensor(
                    out=xn_sl, in0=pa, in1=xo[:, half * 512:(half + 1) * 512],
                    op=OP.add), incs=[("V", 1)])
            out_v[tgl] = tkv["V"]
            b.wait("sync", "V", tkv["V"])
            out_tok = b.dma("sync", x_out[(t0 + tt * 128):(t0 + (tt + 1) * 128), :],
                            xnew[:, buf * D:(buf + 1) * D], sem=f"so{tgl % 2}")
            so_tok[tgl] = out_tok
        if q == 0 and "xnewq0" in DBG:
            dtok = dbg_dump(b, "xnewq0", xnew[:, :], [128, 2 * D], F32)
            b.wait("vector", "d_out", dtok)
        psq = ("V", b.now("V"))
    return out_tok


def emit_l0_attn_a(b, ctx, tag, x_in, x_out):
    TQ = 512
    NQ = T // TQ
    xnT = b.shared("xnT", [128, ND * T], BF16)
    qkvT = xnT
    hbuf = b.sb(f"{tag}_h", [128, 2 * T], BF16)
    wb2 = b.shared("wb2qk", [128, 2 * ND * 128], BF16)
    wbu = b.sb(f"{tag}_wb", [128, 2 * 16 * 128], BF16)
    u1 = b.sb(f"{tag}_u1", [128, 2 * TQ], BF16)
    rb = b.sb(f"{tag}_rb", [128, 512], BF16)
    sg = b.sb(f"{tag}_sg", [128, 1024], BF16)
    rs = b.sb(f"{tag}_rs", [1, 1024], F32)
    carry = b.sb(f"{tag}_cy", [128, 16], F32)
    xob = b.sb(f"{tag}_xo", [128, 2 * D], F32)
    xnb = b.sb(f"{tag}_xn2", [128, 2 * D], F32)
    rr_all = b.shared("st_rr", [128, NT], F32)
    nmr_all = b.shared("st_nmr", [128, NT], F32)
    scrA = b.shared("scr8kA", [128, 2 * D], F32)
    scrB = b.shared("scr8kB", [128, 2 * D], F32)
    Ct, ctok = ctx.lazy("c_l0a_C", [128, T])
    Sgt, _ = ctx.lazy("c_l0a_Sg", [128, T])
    w1bf, _ = bf16_mirror(b, "L0a_w1_st", [2, 128, ND * 128])
    w2qbf, _ = bf16_mirror(b, "L0a_w2qkv_st", [48, 128, 2 * 128])
    w1cbf, _ = bf16_mirror(b, "L0a_w1c_st", [2, 128, 16 * 128])
    w2rbf, _ = bf16_mirror(b, "L0a_w2r", [2, 128, 2048])
    wshare = b.shared("wshare", [128, ND * 1024], BF16)
    w2r = wshare[:, 0:4096]
    b.wait("gpsimd", "T", b.now("T"))
    b.wait("gpsimd", "d_m", b.now("d_m"))
    w2ld = b.dma("gpsimd", w2r.rearrange("p (c f) -> p c f", c=2),
                 w2rbf.rearrange("c p f -> p c f"), sem="rw")

    pro = prologue(b, ctx, tag, x_in, 0, T, xnT, T, 0, ctx.psT,
                   pre_tok=("T", b.now("T")), stats_sink=(rr_all, nmr_all),
                   post_rms=EPS32)

    def evac_h(ft, ts, pa, nc_, t_tok):
        b.wait("scalar", "T", t_tok)
        if hasattr(b, "l0a_rbf"):
            b.wait("scalar", "V", b.l0a_rbf)
        tks = b.emit("scalar", lambda e, pa=pa, nc_=nc_: e.activation(
            out=rb[:, 0:nc_], in_=pa, func=AF.Relu), incs=[("S", 1)])
        b.wait("vector", "S", tks["S"])
        dst = hbuf[:, ft * T + ts * 512:ft * T + ts * 512 + nc_]
        tkv = b.emit("vector", lambda e, dst=dst, nc_=nc_: e.tensor_tensor(
            out=dst, in0=rb[:, 0:nc_], in1=rb[:, 0:nc_], op=OP.mult),
            incs=[("V", 1)])
        b.l0a_rbf = tkv["V"]
        return ("V", tkv["V"])
    mm_fm(b, ctx, w1bf, 2, ND, lambda dc: xnT[:, dc * T:(dc + 1) * T], T,
          evac_h, ctx.psB, wb2,
          extra_wait=[("V", pro), ("S", b.now("S"))])

    b.wait("vector", ctok[0], ctok[1])
    b.wait("vector", "V", b.now("V"))
    b.emit("vector", lambda e: e.memset(carry[:, :], 0.0), incs=[("V", 1)])
    b.drain("vector")

    ps1 = ctx.psB[0:1, 0:512]
    ps2 = ctx.psB[:, 512:1024]
    sqf = scrA[:, 0:512]
    nb = scrA[:, 512:1024]
    sw = scrA[:, 1024:1536]
    rp = scrA[:, 1536:2048]
    kn = scrB[:, 0:512]
    ksw = scrB[:, 512:1024]
    krp = scrB[:, 1024:1536]
    cums = scrB[:, 1536:2048]
    so_tok = {}
    out_v = {}
    out_tok = None
    for q in range(NQ):
        t0 = q * TQ
        def evac_q(ft, ts, pa, nc_, t_tok):
            b.wait("vector", "T", t_tok)
            dst = qkvT[:, ft * TQ:ft * TQ + nc_]
            tkv = b.emit("vector", lambda e, dst=dst, pa=pa: e.tensor_copy(
                out=dst, in_=pa), incs=[("V", 1)])
            return ("V", tkv["V"])
        b.wait("vector", "S", b.now("S"))
        mm_fm(b, ctx, w2qbf, 48, 2,
              lambda dc, t0=t0: hbuf[:, dc * T + t0:dc * T + t0 + TQ],
              TQ, evac_q, ctx.psB, wb2,
              extra_wait=[("V", b.now("V")), ("S", b.now("S"))])
        qkv_rdy = b.now("V")
        Cs = Ct[:, t0:t0 + TQ]
        Ss = Sgt[:, t0:t0 + TQ]
        for j in range(16):
            qc = qkvT[:, j * TQ:(j + 1) * TQ]
            kc = qkvT[:, (16 + j) * TQ:(17 + j) * TQ]
            vc = qkvT[:, (32 + j) * TQ:(33 + j) * TQ]
            yc = qkvT[:, (48 + j) * TQ:(49 + j) * TQ]
            tq = tk2 = None
            for side in range(2):
                ch = qc if side == 0 else kc
                nbuf = nb if side == 0 else kn
                swb = sw if side == 0 else ksw
                rpb = rp if side == 0 else krp
                if side == 0 and j == 0:
                    b.wait("scalar", "V", qkv_rdy)
                if hasattr(b, "l0a_mm1"):
                    b.wait("scalar", "T", b.l0a_mm1)
                tsq = b.emit("scalar", lambda e, ch=ch: e.activation(
                    out=sqf, in_=ch, func=AF.Square), incs=[("S", 1)])["S"]
                b.wait("tensor", "S", tsq)
                if hasattr(b, "l0a_rs1"):
                    b.wait("tensor", "S", b.l0a_rs1)
                mm1 = b.emit("tensor", lambda e: e.matmul(
                    ps1, ctx.ones_t[:, 0:1], sqf, start=True, stop=True),
                    incs=[("T", 1)])["T"]
                b.l0a_mm1 = mm1
                b.wait("scalar", "T", mm1)
                rs1 = b.emit("scalar", lambda e: e.activation(
                    out=rs[0:1, 0:512], in_=ps1, func=AF.Sqrt,
                    bias=ctx.cf[0:1, 2:3], scale=ctx.cf[0:1, 15:16]),
                    incs=[("S", 1)])["S"]
                b.l0a_rs1 = rs1
                b.wait("vector", "S", rs1)
                rc = b.emit("vector", lambda e: e.reciprocal(
                    rs[0:1, 512:1024], rs[0:1, 0:512]), incs=[("V", 1)])["V"]
                b.wait("tensor", "V", rc)
                if hasattr(b, "l0a_nm"):
                    b.wait("tensor", "V", b.l0a_nm)
                mm2 = b.emit("tensor", lambda e: e.matmul(
                    ps2, ctx.ones_t[0:1, 0:128], rs[0:1, 512:1024],
                    start=True, stop=True), incs=[("T", 1)])["T"]
                b.wait("vector", "T", mm2)
                nm = b.emit("vector", lambda e, ch=ch, nbuf=nbuf: e.tensor_tensor(
                    out=nbuf, in0=ch, in1=ps2, op=OP.mult), incs=[("V", 1)])["V"]
                b.l0a_nm = nm
                b.wait("sync", "V", nm)
                b.dma("sync", swb[64:128, :], nbuf[0:64, :], sem="d_s")
                d2 = b.dma("sync", swb[0:64, :], nbuf[64:128, :], sem="d_s")
                b.wait("vector", "d_s", d2)
                b.emit("vector", lambda e, nbuf=nbuf, rpb=rpb, Cs=Cs:
                       e.tensor_tensor(out=rpb, in0=nbuf, in1=Cs, op=OP.mult))
                b.drain("vector")
                b.emit("vector", lambda e, swb=swb, Ss=Ss: e.tensor_tensor(
                    out=swb, in0=swb, in1=Ss, op=OP.mult))
                b.drain("vector")
                rt = b.emit("vector", lambda e, rpb=rpb, swb=swb: e.tensor_tensor(
                    out=rpb, in0=rpb, in1=swb, op=OP.add), incs=[("V", 1)])["V"]
                b.wait("scalar", "V", rt)
                if side == 0:
                    tq = b.emit("scalar", lambda e, rpb=rpb, qc=qc: e.activation(
                        out=qc, in_=rpb, func=AF.Relu), incs=[("S", 1)])["S"]
                else:
                    tk2 = b.emit("scalar", lambda e, rpb=rpb, kn=kn: e.activation(
                        out=kn, in_=rpb, func=AF.Sigmoid), incs=[("S", 1)])["S"]
            b.wait("vector", "S", tk2)
            b.emit("vector", lambda e, kn=kn, vc=vc: e.tensor_tensor(
                out=ksw, in0=kn, in1=vc, op=OP.mult))
            b.drain("vector")
            b.emit("vector", lambda e, j=j: e.tensor_tensor_scan(
                out=cums, data0=ksw, data1=ksw, initial=carry[:, j:j + 1],
                op0=OP.add, op1=OP.bypass))
            b.drain("vector")
            b.emit("vector", lambda e, j=j: e.tensor_copy(
                out=carry[:, j:j + 1], in_=cums[:, TQ - 1:TQ]))
            b.drain("vector")
            b.wait("vector", "S", tq)
            b.emit("vector", lambda e, qc=qc, yc=yc: e.tensor_tensor(
                out=yc, in0=cums, in1=qc, op=OP.mult), incs=[("V", 1)])
        def evac_u(ft, ts, pa, nc_, t_tok):
            b.wait("vector", "T", t_tok)
            dst = u1[:, ft * TQ:ft * TQ + nc_]
            tkv = b.emit("vector", lambda e, dst=dst, pa=pa: e.tensor_copy(
                out=dst, in_=pa), incs=[("V", 1)])
            return ("V", tkv["V"])
        mm_fm(b, ctx, w1cbf, 2, 16,
              lambda dc: qkvT[:, (48 + dc) * TQ:(49 + dc) * TQ],
              TQ, evac_u, ctx.psB, wbu,
              extra_wait=[("V", b.now("V")), ("S", b.now("S"))])
        u1_rdy = b.now("V")
        for tt in range(4):
            tgl = q * 4 + tt
            buf = tt % 2
            xo = xob[:, buf * D:(buf + 1) * D]
            xn2 = xnb[:, buf * D:(buf + 1) * D]
            if tgl >= 2:
                b.wait("sync", "V", out_v[tgl - 2])
                b.wait("sync", f"so{tgl % 2}", so_tok[tgl - 2])
            xl = b.dma("sync", xo, x_in[t0 + tt * 128:t0 + (tt + 1) * 128, :],
                       sem=f"xo{tt % 2}")
            if tt == 0:
                b.wait("tensor", "V", u1_rdy)
                b.wait("tensor", "rw", w2ld)
            b.wait("tensor", "V", b.now("V"))
            b.wait("tensor", "S", b.now("S"))
            tkf = None
            for dc in range(2):
                for seg in range(4):
                    tkf = b.emit("tensor", lambda e, dc=dc, seg=seg, tt=tt:
                                 e.matmul(
                        ctx.psA[:, seg * 512:(seg + 1) * 512],
                        u1[:, dc * TQ + tt * 128:dc * TQ + (tt + 1) * 128],
                        w2r[:, dc * 2048 + seg * 512:dc * 2048 + (seg + 1) * 512],
                        start=(dc == 0), stop=(dc == 1)),
                        incs=[("T", 1)] if (dc == 1 and seg == 3) else ())
            b.wait("scalar", "T", tkf["T"])
            b.wait("scalar", "V", b.now("V"))
            sil = b.emit("scalar", lambda e: e.activation(
                out=sg[:, :], in_=ctx.psA[:, 1024:2048], func=AF.Silu),
                incs=[("S", 1)])["S"]
            b.wait("vector", "S", sil)
            b.wait("vector", f"xo{tt % 2}", xl)
            tv = None
            for half in range(2):
                xnh = xn2[:, half * 512:(half + 1) * 512]
                xoh = xo[:, half * 512:(half + 1) * 512]
                b.emit("vector", lambda e, half=half, xnh=xnh: e.tensor_tensor(
                    out=xnh, in0=ctx.psA[:, half * 512:(half + 1) * 512],
                    in1=sg[:, half * 512:(half + 1) * 512], op=OP.mult))
                b.drain("vector")
                b.emit("vector", lambda e, xnh=xnh, xoh=xoh: e.tensor_tensor(
                    out=xnh, in0=xnh, in1=xoh, op=OP.add))
                b.drain("vector")
                b.emit("vector", lambda e, xnh=xnh, xoh=xoh, tgl=tgl:
                       e.scalar_tensor_tensor(
                    out=xnh, in0=xoh, scalar=rr_all[:, tgl:tgl + 1], in1=xnh,
                    op0=OP.mult, op1=OP.add))
                b.drain("vector")
                tv = b.emit("vector", lambda e, xnh=xnh, tgl=tgl: e.tensor_scalar(
                    out=xnh, in0=xnh, scalar1=nmr_all[:, tgl:tgl + 1],
                    scalar2=None, op0=OP.add), incs=[("V", 1)])
            out_v[tgl] = tv["V"]
            b.wait("sync", "V", out_v[tgl])
            so_tok[tgl] = b.dma("sync",
                                x_out[t0 + tt * 128:t0 + (tt + 1) * 128, :],
                                xn2, sem=f"so{tgl % 2}")
            out_tok = so_tok[tgl]
    return out_tok


def prep_l0_attn_a(inp, H):
    if "L0a_w1_st" in H:
        return
    import ml_dtypes
    bf = ml_dtypes.bfloat16
    w1 = inp["L0_attn_a_qkv_w1"].astype(np.float32)
    w2 = inp["L0_attn_a_qkv_w2"].astype(np.float32)
    c1 = inp["L0_attn_a_cproj_w1"].astype(np.float32)
    c2 = inp["L0_attn_a_cproj_w2"].astype(np.float32)
    H["L0a_w1_st"] = _st_tile(np.ascontiguousarray(w1.T), ND, 2)
    H["L0a_w2qkv_st"] = _st_tile(np.ascontiguousarray(w2.T), 2, 48)
    H["L0a_w1c_st"] = _st_tile(np.ascontiguousarray(c1.T), 16, 2)
    H["L0a_w2r"] = np.ascontiguousarray(c2.T.reshape(2, 128, 2048))
    quarter = 32
    freq = (1.0 / 1024.0) ** np.linspace(0.0, 1.0, quarter).astype(np.float32)
    freq = np.concatenate([freq.astype(np.float32), np.zeros(quarter, np.float32)])
    theta = np.arange(T, dtype=np.float32)[:, None] * freq[None, :]
    cos = np.cos(theta).T.astype(np.float32)
    sin = np.sin(theta).T.astype(np.float32)
    H["c_l0a_C"] = np.concatenate([cos, cos], 0).astype(bf)
    H["c_l0a_Sg"] = np.concatenate([sin, -sin], 0).astype(bf)


SUBS[1] = (emit_l0_attn_a, prep_l0_attn_a)


def emit_l3aa_A(b, ctx, tag, x_in):
    xnT = b.shared("xnT", [128, ND * T], BF16)
    ybuf = xnT
    wb2 = b.shared("wb2qk", [128, 2 * ND * 128], BF16)
    qd = b.dram(f"{tag}_qd", [8, 128, T], BF16)
    kd = b.dram(f"{tag}_kd", [8, 128, T], BF16)
    vd = b.dram(f"{tag}_vd", [T, 16 * 65], BF16)
    qbc = b.sb(f"{tag}_qbc", [128, 16], F32)
    b.dma("sync", qbc[:, :], b.inp("L3aa_bqk", [128, 16])[:, :], sem="d_cs")
    qbt = b.handoff("sync", "d_cs", "HCs")
    bv_row = b.sb(f"{tag}_bvr", [1, 1024], BF16)
    b.dma("gpsimd", bv_row[:, :], b.inp("L3aa_bv", [1, 1024], BF16)[:, :],
          sem="d_cg")
    bvt = b.handoff("gpsimd", "d_cg", "HCg")
    vbt = b.shared("gb_o", [128, D], BF16)
    bcast_row(b, ctx, vbt, bv_row, D, row_wait=("HCg", bvt))

    pro = prologue(b, ctx, tag, x_in, 0, T, xnT, T, 0, ctx.psT,
                   pre_tok=("T", b.now("T")))
    b.wait("vector", "HCs", qbt)
    attn_stage_qk(b, ctx, tag, xnT, "L3aa_wqk_st", 8, qd, kd,
                  qb_col=qbc[:, 0:8], kb_col=qbc[:, 8:16],
                  extra_wait=[("V", pro)], wb2=wb2)
    attn_stage_v(b, ctx, tag, xnT, "L3aa_wv", vd, 16, vb_tile=vbt, xn_rdy=pro)
    ytok = attn_core(b, ctx, tag, 16, qd, kd, vd, ybuf, ctx.const(0.125),
                     stage_tok=b.now("d_x"))
    b.wait("sync", "V", ytok)
    b.wait("sync", "d_s", b.now("d_s"))
    ya = b.outp("ya_d", [128, 8 * T], BF16)
    ya_tok = b.dma("sync", ya[:, :], ybuf[:, :], sem="d_out")
    acc = b.sb(f"{tag}_acc", [1, 16], F32)
    nr = b.sb(f"{tag}_nr", [1, 2 * 512], F32)
    scr = b.shared("scr8kA", [128, 2 * D], F32)
    b.wait("vector", "V", b.now("V"))
    b.emit("vector", lambda e: e.memset(acc[:, :], 0.0), incs=[("V", 1)])
    b.drain("vector")
    b.wait("scalar", "S", b.now("S"))
    b.wait("scalar", "V", ytok)
    first_t = True
    sk = tk = None
    for c in range(8):
        for tb in range(T // 2048):
            if tk is not None:
                b.wait("scalar", "T", tk["T"])
            stk = b.emit("scalar", lambda e, c=c, tb=tb: e.activation(
                out=scr[:, :],
                in_=ybuf[:, c * T + tb * 2048:c * T + (tb + 1) * 2048],
                func=AF.Square), incs=[("S", 1)])
            b.wait("tensor", "S", stk["S"])
            if first_t:
                b.wait("tensor", "V", b.now("V"))
                first_t = False
            for hh in range(2):
                for st in range(4):
                    pa = ctx.psB[0:1, 0:512]
                    if sk is not None:
                        b.wait("tensor", "S", sk["S"])
                    tk = b.emit("tensor", lambda e, pa=pa, hh=hh, st=st:
                                e.matmul(pa, ctx.ones_t[64 * hh:64 * hh + 64, 0:1],
                                         scr[64 * hh:64 * hh + 64,
                                             st * 512:(st + 1) * 512],
                                         start=True, stop=True), incs=[("T", 1)])
                    b.wait("scalar", "T", tk["T"])
                    sk = b.emit("scalar", lambda e, pa=pa: e.activation(
                        out=nr[:, 0:512], in_=pa, func=AF.Sqrt), incs=[("S", 1)])
                    b.wait("vector", "S", sk["S"])
                    rk = b.emit("vector", lambda e: e.tensor_reduce(
                        nr[:, 512:513], nr[:, 0:512], AX.X, OP.add),
                        incs=[("V", 1)])
                    b.drain("vector")
                    h = 2 * c + hh
                    b.emit("vector", lambda e, h=h: e.tensor_tensor(
                        out=acc[:, h:h + 1], in0=acc[:, h:h + 1],
                        in1=nr[:, 512:513], op=OP.add), incs=[("V", 1)])
                    b.drain("vector")
                    b.wait("scalar", "V", b.now("V"))
    nrm_out = b.outp("nrm_part", [1, 16], F32)
    b.wait("sync", "V", b.now("V"))
    b.dma("sync", nrm_out[:, :], acc[:, :], sem="d_out")
    return ya_tok


def emit_l3aa_B(b, ctx, tag, x_in, x_out):
    ybuf = b.shared("xnT", [128, ND * T], BF16)
    ya = b.inp("ya_d", [128, 8 * T], BF16)
    b.dma("sync", ybuf[:, :], ya[:, :], sem="d_x")
    yh = b.handoff("sync", "d_x", "HCs")
    nrm = b.sb(f"{tag}_nrm", [8, 16], F32)
    b.dma("sync", nrm[:, :], b.inp("nrm_all", [8, 16])[:, :], sem="d_cs")
    gatec = b.sb(f"{tag}_gc", [16, 1], F32)
    b.dma("sync", gatec[:, :], b.inp("L3aa_gate_c", [16, 1])[:, :], sem="d_cs")
    gater = b.sb(f"{tag}_gr", [1, 16], F32)
    b.dma("sync", gater[:, :], b.inp("L3aa_gate_r", [1, 16])[:, :], sem="d_cs")
    bo = b.sb(f"{tag}_bo", [16, 1024], F32)
    b.dma("sync", bo[:, :], b.inp("L3aa_bo", [16, 1024])[:, :], sem="d_cs")
    ct = b.handoff("sync", "d_cs", "HCs")
    srow = b.sb(f"{tag}_sr", [1, 16], F32)
    bo16 = b.sb(f"{tag}_bo16", [16, 1024], BF16)
    g2cb = b.sb(f"{tag}_g2cb", [16, 1], BF16)
    brow = b.sb(f"{tag}_brow", [1, D], BF16)
    pscale = b.sb(f"{tag}_psc", [128, 8], F32)
    ot = b.shared("gb_o", [128, D], BF16)
    b.wait("tensor", "HCs", ct)
    b.wait("tensor", "V", b.now("V"))
    b.wait("tensor", "S", b.now("S"))
    tk = b.emit("tensor", lambda e: e.matmul(
        ctx.psB[0:1, 0:16], ctx.ones_t[0:8, 0:1], nrm[:, :],
        start=True, stop=True), incs=[("T", 1)])
    b.wait("vector", "T", tk["T"])
    b.emit("vector", lambda e: e.tensor_scalar(
        out=srow[:, :], in0=ctx.psB[0:1, 0:16], scalar1=1.0 / 32768.0,
        scalar2=1e-5, op0=OP.mult, op1=OP.max), incs=[("V", 1)])
    b.drain("vector")
    b.emit("vector", lambda e: e.reciprocal(srow[:, :], srow[:, :]),
           incs=[("V", 1)])
    b.drain("vector")
    b.emit("vector", lambda e: e.tensor_scalar(
        out=gatec[:, :], in0=gatec[:, :], scalar1=1.0, scalar2=0.0,
        op0=OP.min, op1=OP.max), incs=[("V", 1)])
    b.drain("vector")
    b.emit("vector", lambda e: e.tensor_scalar(
        out=gatec[:, :], in0=gatec[:, :], scalar1=1.0 / 16.0, scalar2=None,
        op0=OP.mult), incs=[("V", 1)])
    b.drain("vector")
    b.emit("vector", lambda e: e.tensor_copy(out=g2cb[:, :], in_=gatec[:, :]),
           incs=[("V", 1)])
    b.emit("vector", lambda e: e.tensor_copy(out=bo16[:, :], in_=bo[:, :]),
           incs=[("V", 1)])
    b.drain("vector")
    bv = b.now("V")
    b.wait("tensor", "V", bv)
    tkb = None
    for half in range(2):
        tkb = b.emit("tensor", lambda e, half=half: e.matmul(
            ctx.psB[0:1, half * 512:half * 512 + 512], g2cb[:, :],
            bo16[:, half * 512:(half + 1) * 512], start=True, stop=True),
            incs=[("T", 1)] if half == 1 else ())
    b.wait("vector", "T", tkb["T"])
    b.emit("vector", lambda e: e.tensor_copy(
        out=brow[:, 0:512], in_=ctx.psB[0:1, 0:512]), incs=[("V", 1)])
    b.emit("vector", lambda e: e.tensor_copy(
        out=brow[:, 512:1024], in_=ctx.psB[0:1, 512:1024]), incs=[("V", 1)])
    b.drain("vector")
    b.emit("vector", lambda e: e.tensor_scalar(
        out=gater[:, :], in0=gater[:, :], scalar1=1.0, scalar2=0.0,
        op0=OP.min, op1=OP.max), incs=[("V", 1)])
    b.drain("vector")
    b.emit("vector", lambda e: e.tensor_scalar(
        out=gater[:, :], in0=gater[:, :], scalar1=1.0 / 16.0, scalar2=None,
        op0=OP.mult), incs=[("V", 1)])
    b.drain("vector")
    b.emit("vector", lambda e: e.tensor_tensor(
        out=srow[:, :], in0=srow[:, :], in1=gater[:, :], op=OP.mult),
        incs=[("V", 1)])
    b.drain("vector")
    bcast_row(b, ctx, ot, brow, D)
    return out_13b(b, ctx, tag, ybuf, srow, pscale, ot, x_in, x_out, yh)


def out_13b(b, ctx, tag, ybuf, srow, pscale, ot, x_in, x_out, yh):
    b.wait("tensor", "V", b.now("V"))
    tk = None
    for h in range(16):
        tk = b.emit("tensor", lambda e, h=h: e.matmul(
            ctx.psB[:, h:h + 1], ctx.ones_t[0:1, 0:128], srow[0:1, h:h + 1],
            start=(h == 0), stop=(h == 15)), incs=[("T", 1)] if h == 15 else ())
    b.wait("vector", "T", tk["T"])
    for c in range(8):
        b.emit("vector", lambda e, c=c: e.tensor_copy(
            out=pscale[0:64, c:c + 1], in_=ctx.psB[0:64, 2 * c:2 * c + 1]),
            incs=[("V", 1)])
        b.emit("vector", lambda e, c=c: e.tensor_copy(
            out=pscale[64:128, c:c + 1],
            in_=ctx.psB[64:128, 2 * c + 1:2 * c + 2]), incs=[("V", 1)])
    b.drain("vector")
    b.wait("vector", "HCs", yh)
    sc_done = None
    for c in range(8):
        sc_done = b.emit("vector", lambda e, c=c: e.tensor_scalar(
            out=ybuf[:, c * T:(c + 1) * T], in0=ybuf[:, c * T:(c + 1) * T],
            scalar1=pscale[:, c:c + 1], scalar2=None, op0=OP.mult),
            incs=[("V", 1)])["V"]
    return resid_proj(b, ctx, tag, ybuf, 8, "L3aa_wo_t", x_in, x_out,
                      bias_tile=ot, y_rdy=[("V", sc_done)])


def prep_l3_attn_a(inp, H):
    if "L3aa_wqk_st" in H:
        return
    import ml_dtypes
    wq = inp["L3_attn_a_Wq"].astype(np.float32).transpose(1, 0, 2).reshape(D, 1024)
    wk = inp["L3_attn_a_Wk"].astype(np.float32).transpose(1, 0, 2).reshape(D, 1024)
    wv = inp["L3_attn_a_Wv"].astype(np.float32).transpose(1, 0, 2).reshape(D, 1024)
    H["L3aa_wqk_st"] = _st_tile(np.ascontiguousarray(
        np.concatenate([wq, wk], axis=1)), ND, 16)
    H["L3aa_wv"] = np.ascontiguousarray(wv.reshape(ND, 128, 1024))
    H["L3aa_wo_t"] = np.ascontiguousarray(
        inp["L3_attn_a_Wo"].astype(np.float32).reshape(1024, D).reshape(8, 128, D))
    bqk = np.concatenate(
        [inp["L3_attn_a_bq"].astype(np.float32).reshape(1024).reshape(8, 128).T,
         inp["L3_attn_a_bk"].astype(np.float32).reshape(1024).reshape(8, 128).T],
        axis=1)
    H["L3aa_bqk"] = np.ascontiguousarray(bqk)
    H["L3aa_bv"] = inp["L3_attn_a_bv"].astype(np.float32).reshape(1, 1024).astype(
        ml_dtypes.bfloat16)
    H["L3aa_bo"] = inp["L3_attn_a_bo"].astype(np.float32)
    H["L3aa_gate_c"] = inp["L3_attn_a_gate"].astype(np.float32).reshape(16, 1)
    H["L3aa_gate_r"] = inp["L3_attn_a_gate"].astype(np.float32).reshape(1, 16)


def build_13A():
    nc = bass.Bass("TRN2", target_bir_lowering=False)
    b = Bld(nc)
    ctx = Ctx(b)
    emit_l3aa_A(b, ctx, "s13A", b.inp("x_io", [T, D]))
    b.finalize({"so0": b.now("so0"), "so1": b.now("so1"),
                "d_out": b.now("d_out")})
    return b


def build_13B():
    nc = bass.Bass("TRN2", target_bir_lowering=False)
    b = Bld(nc)
    ctx = Ctx(b)
    emit_l3aa_B(b, ctx, "s13B", b.inp("x_io", [T, D]), b.outp("x_out", [T, D]))
    b.finalize({"so0": b.now("so0"), "so1": b.now("so1"),
                "d_out": b.now("d_out")})
    return b


def run_sub13(x_full, H):
    bA = build_13A()
    in_maps = []
    for c in range(NCORES):
        m = {}
        for name, (shape, dtype) in bA.inputs.items():
            m[name] = (np.ascontiguousarray(x_full[c]) if name == "x_io"
                       else H[name])
        in_maps.append(m)
    resA = run_bass_kernel_spmd(bA.nc, in_maps, core_ids=list(range(NCORES)))
    nrm_all = np.ascontiguousarray(np.stack(
        [np.asarray(resA.results[c]["nrm_part"], np.float32)[0]
         for c in range(NCORES)]))
    ya = [np.asarray(resA.results[c]["ya_d"]) for c in range(NCORES)]
    bB = build_13B()
    in_maps = []
    for c in range(NCORES):
        m = {}
        for name, (shape, dtype) in bB.inputs.items():
            if name == "x_io":
                m[name] = np.ascontiguousarray(x_full[c])
            elif name == "ya_d":
                m[name] = ya[c]
            elif name == "nrm_all":
                m[name] = nrm_all
            else:
                m[name] = H[name]
        in_maps.append(m)
    resB = run_bass_kernel_spmd(bB.nc, in_maps, core_ids=list(range(NCORES)))
    return np.stack([np.asarray(resB.results[c]["x_out"], np.float32)
                     for c in range(NCORES)])


SUBS[13] = (None, prep_l3_attn_a)
PHASED = {13: run_sub13, 2: run_sub2, 4: run_sub2}


def prep_l3_ffn(inp, H):
    if "L3f_wg" in H:
        return
    H["L3f_wg"] = np.ascontiguousarray(
        inp["L3_ffn_gate_w"].astype(np.float32).T.reshape(ND, 128, 8))
    w1 = inp["L3_ffn_w1"].astype(np.float32)
    w2 = inp["L3_ffn_w2"].astype(np.float32)
    cfc = inp["L3_ffn_cfc"].astype(np.float32)
    cproj = inp["L3_ffn_cproj"].astype(np.float32)
    w1st = np.stack([_st_tile(np.ascontiguousarray(w1[e].T), ND, ND)
                     for e in range(8)] +
                    [_st_tile(np.ascontiguousarray(cfc.T), ND, ND)])
    w2st = np.stack([np.ascontiguousarray(w2[e].T).reshape(ND, 128, ND * 128)
                     for e in range(8)] +
                    [np.ascontiguousarray(cproj.T).reshape(ND, 128, ND * 128)])
    H["L3f_w1_st"] = np.ascontiguousarray(w1st)
    H["L3f_w2_st"] = np.ascontiguousarray(w2st)
    H["L3f_lb"] = inp["L3_ffn_lb_bias"].astype(np.float32).reshape(1, 8)
    io = np.arange(8, dtype=np.float32)
    H["c_iota8"] = np.tile(np.concatenate([io, io + 999.0])[None, :], (128, 1))


SUBS[14] = (emit_l3_ffn, prep_l3_ffn)
SUBS[16] = (emit_l3_ffn, prep_l3_ffn)
